# version 01
# baseline (speedup 1.0000x reference)
import math
import numpy as np
from contextlib import ExitStack

import concourse.bass as bass
import concourse.mybir as mybir
from concourse.bass_utils import run_bass_kernel_spmd

F32, F16, BF16 = mybir.dt.float32, mybir.dt.float16, mybir.dt.bfloat16
AF = mybir.ActivationFunctionType
ALU = mybir.AluOpType
AX = mybir.AxisListType

T = 2048
D = 1024
NT = 16
ND = 8
TC = 4
NH = 16
HD = 64
PAGE = 256
NP = 8
EPS = 1e-5
EXPC = 30.0
NE = 8
RA = 272
MC = [(0, 128), (128, 128), (256, 16)]


class Sem:
    def __init__(self, es, nc, name):
        self.h = es.enter_context(nc.semaphore(name))
        self.n = 0
    def inc(self, instr, k=1):
        instr.then_inc(self.h, k)
        self.n += k
        return self.n


EW_OPS = {"tensor_tensor", "tensor_scalar", "tensor_reduce", "tensor_copy",
          "reciprocal", "memset", "activation"}


class EW:
    def __init__(self, eng, sem):
        self._e, self._s = eng, sem
    def __getattr__(self, nm):
        f = getattr(self._e, nm)
        if nm in EW_OPS:
            def w(*a, **k):
                if self._s.n > 0:
                    self._e.wait_ge(self._s.h, self._s.n)
                ins = f(*a, **k)
                self._s.inc(ins)
                return ins
            return w
        return f


class L1B:
    def __init__(self, dbg=(), upto="all"):
        self.dbg = set(dbg)
        self.upto = upto
        self.nc = nc = bass.Bass("TRN2", target_bir_lowering=False)
        self.es = es = ExitStack()
        self.dbg_h = {}

        dram = lambda n, s, dt, k: nc.dram_tensor(n, s, dt, kind=k).ap()
        self.x_h = dram("x", [T, D], F32, "ExternalInput")
        self.wq_h = dram("wq", [D, D], F32, "ExternalInput")
        self.wd_h = dram("w_down", [D, 512], F32, "ExternalInput")
        self.wku_h = dram("wk_up", [512, D], F32, "ExternalInput")
        self.wvu_h = dram("wv_up", [512, D], F32, "ExternalInput")
        self.wo_h = dram("wo", [D, D], F32, "ExternalInput")
        self.idt_h = dram("ident16", [128, 128], F16, "ExternalInput")
        self.swp_h = dram("swap16", [128, 128], F16, "ExternalInput")
        self.ra_h = dram("ropeA", [128, T], F16, "ExternalInput")
        self.rb_h = dram("ropeB", [128, T], F16, "ExternalInput")
        self.wbase_h = dram("W_base", [D, D], F32, "ExternalInput")
        self.bbase_h = dram("b_basec", [128, ND], F32, "ExternalInput")
        self.bbaser_h = dram("b_baser", [1, D], F32, "ExternalInput")
        self.wd1_h = dram("Wd1", [NE, D, 256], F32, "ExternalInput")
        self.bd1_h = dram("bd1f", [1, NE * 256], F32, "ExternalInput")
        self.wd2t_h = dram("Wd2T", [NE, D, 256], F32, "ExternalInput")
        self.bd2c_h = dram("bd2c", [128, NE, ND], F32, "ExternalInput")
        self.wadn_h = dram("Wad_stack", [D, 512], F32, "ExternalInput")
        self.badn_h = dram("bad_stack", [128, 4], F32, "ExternalInput")
        self.waup_h = dram("Wau_stack", [512, D], F32, "ExternalInput")
        self.baup_h = dram("ba_up", [NE, D], F32, "ExternalInput")
        self.iota8_h = dram("iota8", [1, 8], F32, "ExternalInput")
        self.eexp_h = dram("Eexp", [8, 512], F16, "ExternalInput")
        self.out_h = dram("x2", [T, D], F32, "ExternalOutput")

        sb = lambda *a: es.enter_context(nc.sbuf_tensor(*a))
        pst = lambda *a: es.enter_context(nc.psum_tensor(*a))

        self.idt16 = sb("idt16", [128, 128], F16)
        self.swp16 = sb("swp16", [128, 128], F16)
        self.ones16 = sb("ones16", [128, 64], F16)
        self.onesbf = sb("onesbf", [128, 64], BF16)
        self.onesf32 = sb("onesf32", [128, 1], F32)
        self.ones512 = sb("ones512", [1, 512], F16)
        self.eps_t = sb("eps_t", [128, 1], F32)
        self.c30 = sb("c30", [128, 1], F32)
        self.zero1 = sb("zero1", [128, 1], F32)
        self.xst = sb("xst", [128, 2, D], F32)
        self.xnst = sb("xnst", [128, 2, D], F16)
        self.sums = sb("sums", [128, NT], F32)
        self.sumsq = sb("sumsq", [128, NT], F32)
        self.means = sb("means", [128, NT], F32)
        self.rstds = sb("rstds", [128, NT], F32)
        self.vart = sb("vart", [128, NT], F32)
        self.lnbias = sb("lnbias", [128, NT], F32)
        self.scr = sb("scr", [128, 1536], F32)
        scrf16 = self.scr.bitcast(F16)
        scrbf = self.scr.bitcast(BF16)
        self.sqscr = scrf16[:, 0:1024]
        self.rsc1 = self.scr[:, 512:1024]
        self.rsc2 = self.scr[:, 1024:1536]
        self.osc = self.scr[:, 512:768]
        self.rrec = self.scr[:, 768:1024]
        self.rhl = scrbf[:, 2048:2560].rearrange("p (a w) -> p a w", w=PAGE)
        self.rlo32 = self.scr[:, 1280:1536]

        self.wqa = sb("wqa", [128, 2, ND, 128], F16)
        self.wd16 = sb("wd16", [128, ND, 512], F16)
        self.wkv = sb("wkv", [128, 8192], F16)
        self.wku16 = self.wkv[:, 0:4096].rearrange("p (c m) -> p c m", m=1024)
        self.wvu16 = self.wkv[:, 4096:8192].rearrange("p (c m) -> p c m", m=1024)

        self.ar1 = sb("ar1", [128, 16384], F16)
        self.arq = sb("arq", [128, 16384], F16)
        self.ark = sb("ark", [128, 16384], F16)
        self.arv = sb("arv", [128, 16384], F16)
        self.arl = sb("arl", [128, 8704], F16)
        self.arr = sb("arr", [128, 2048], F32)

        self.xnT = self.ar1.rearrange("p (c t) -> p c t", t=T)
        self.att_tok = self.ar1.rearrange("p (i d) -> p i d", d=D)
        self.xn2_tok = self.att_tok
        self.ffn_fm = self.xnT
        self.q0 = self.arq.rearrange("p (c t) -> p c t", t=T)
        self.xn2T = self.q0
        self.ffn_tok = self.arq.rearrange("p (i d) -> p i d", d=D)
        self.k0 = self.ark.rearrange("p (c t) -> p c t", t=T)
        self.vstore = self.arv.bitcast(BF16).rearrange("p (i d) -> p i d", d=D)
        self.x1a = self.ark.bitcast(F32).rearrange("p (i d) -> p i d", d=D)
        self.x1b = self.arv.bitcast(F32).rearrange("p (i d) -> p i d", d=D)
        self.latT = self.arl[:, 0:8192].rearrange("p (c t) -> p c t", t=T)
        self.woa = self.arl[:, 0:8192].rearrange("p (c m) -> p c m", m=1024)
        self._wd1h = self.arl[:, 0:2048].rearrange("p (c r) -> p c r", r=256)
        self._wd1l = self.arl[:, 2048:4096].rearrange("p (c r) -> p c r", r=256)
        self._wd2th = self.arl[:, 4096:6272].rearrange("p (c r) -> p c r", r=RA)
        self._wd2tl = self.arl[:, 6272:8448].rearrange("p (c r) -> p c r", r=RA)
        self._wad = self.arl[:, 0:4096].rearrange("p (c r) -> p c r", r=512)
        self._wau = self.arl[:, 4096:8192].rearrange("p (c r) -> p c r", r=1024)
        self.htok = self.wkv[:, 0:4352].rearrange("p (i r) -> p i r", r=RA)
        self.ptile = self.wkv.bitcast(BF16)[:, 0:4096].rearrange("p (j w) -> p j w", w=PAGE)
        self.bd1f = self.wkv[0:1, 4352:6400]
        self.eexp16 = self.wkv[0:8, 6400:6912]
        self.baup16 = self.wkv[0:8, 6912:7936]
        self.opage = self.wd16.rearrange("p c m -> p (c m)").rearrange("p (b c w) -> p b c w", b=2, w=PAGE)
        self.wstage = self.arr.rearrange("p (c r) -> p c r", r=256)
        self.bd1f32 = self.arr[0:1, 0:2048]
        self.bd1l = self.wd16.rearrange("p c m -> p (c m)").bitcast(F16)[0:1, 0:2048]
        arrf16 = self.arr.bitcast(F16)
        self.ropeA = arrf16[:, 0:2048]
        self.ropeB = arrf16[:, 2048:4096]
        self.att2p = arrf16.rearrange("p (b c w) -> p b c w", b=2, w=PAGE)
        self.achunk = arrf16.rearrange("p (b c w) -> p b c w", b=2, w=512)
        self.basel = sb("basel", [1, ND, 128], F16)
        self.bbaser = sb("bbaser", [1, D], F16)

        self.bd2c32 = sb("bd2c32", [128, NE, ND], F32)
        self.bd2ch = sb("bd2ch", [128, NE, ND], F16)
        self.bd2cl = sb("bd2cl", [128, NE, ND], F16)

        self.bbasec = sb("bbasec", [128, ND], F32)
        self.badm = sb("badm", [128, 4], F32)
        self.bads = sb("bads", [128, 4], F32)
        self.ohexp = sb("ohexp", [128, 4], F32)
        self.iota8 = sb("iota8s", [1, 8], F32)
        self.iom = sb("iom", [1, 8], F32)
        self.lacc = sb("lacc", [128, 8], F32)
        self.lfin = sb("lfin", [1, 8], F32)
        self.rmin = sb("rmin", [1, 1], F32)
        self.idxs = sb("idxs", [1, 1], F32)
        self.eqrow = sb("eqrow", [1, 8], F32)
        self.zrow = sb("zrow", [1, 8], F32)
        self.ohrow = sb("ohrow", [1, 8], F32)
        self.gsc = self.scr[:, 512:784]
        self.gsc2 = self.scr[:, 784:1056]
        self.xhpart = sb("xhpart", [128, ND], F32)
        self.onec = sb("onec", [8, 1], F16)

        self.psA = pst("psA", [128, 512], F32)
        self.psB = pst("psB", [128, 512], F32)
        self.psT1 = pst("psT1", [128, 128], F16)
        self.psT2 = pst("psT2", [128, 128], F16)
        self.psS1 = pst("psS1", [128, PAGE], F32)
        self.psS2 = pst("psS2", [128, PAGE], F32)
        self.psO = pst("psO", [128, PAGE], F32)
        self.psBC = pst("psBC", [128, PAGE], F32)
        self.psF1 = None
        self.psF2 = None

        self.psF1 = self.psA[:, 0:RA]
        self.psF2 = self.psB[:, 0:RA]
        self.s_ld = Sem(es, nc, "s_ld")
        self.s_ld0 = Sem(es, nc, "s_ld0")
        self.s_ld1 = Sem(es, nc, "s_ld1")
        self.s_wt = Sem(es, nc, "s_wt")
        self.s_wq0 = Sem(es, nc, "s_wq0")
        self.s_wq1 = Sem(es, nc, "s_wq1")
        self.s_wo = Sem(es, nc, "s_wo")
        self.s_wf = Sem(es, nc, "s_wf")
        self.s_we0 = Sem(es, nc, "s_we0")
        self.s_we1 = Sem(es, nc, "s_we1")
        self.s_wb0 = Sem(es, nc, "s_wb0")
        self.s_wb1 = Sem(es, nc, "s_wb1")
        self.s_wa = Sem(es, nc, "s_wa")
        self.s_pe = Sem(es, nc, "s_pe")
        self.s_dv = Sem(es, nc, "s_dv")
        self.s_ac = Sem(es, nc, "s_ac")
        self.s_st = Sem(es, nc, "s_st")
        self.sem = {"pe": self.s_pe, "dv": self.s_dv, "ac": self.s_ac,
                    "ld": self.s_ld, "wt": self.s_wt, "st": self.s_st,
                    "ld0": self.s_ld0, "ld1": self.s_ld1,
                    "wq0": self.s_wq0, "wq1": self.s_wq1, "wo": self.s_wo,
                    "wf": self.s_wf, "we0": self.s_we0, "we1": self.s_we1,
                    "wb0": self.s_wb0, "wb1": self.s_wb1, "wa": self.s_wa}
        self.bank_free = {}

        nc.const_aps.register(F32, 0.0, self.zero1[:])
        self.blk = es.enter_context(nc.Block())
        self.build()

    def wait(self, eng, edge):
        if edge is None:
            return
        sname, cnt = edge
        if cnt > 0:
            eng.wait_ge(self.sem[sname].h, cnt)

    def bank_wait(self, eng, bank):
        self.wait(eng, self.bank_free.get(bank))

    def build(self):
        self.ph_loads()
        if self.upto == "loads":
            self.finalize_dbg()
            return
        self.lnT_done = self.ph_ln_hbm(self.x_h, dst_T=self.xnT)
        if "xnt" in self.dbg:
            self.dbg_h["xnt"] = self.nc.dram_tensor("dbg_xnt", [D, T], F16, kind="ExternalOutput").ap()
        if self.upto == "ln":
            self.finalize_dbg()
            return
        self.q_done = self.ph_proj_q()
        if self.upto == "qonly":
            self.finalize_dbg()
            return
        self.lat_done = self.ph_proj(self.latT, self.wd16, ND, 4, src=self.xnT,
                                     src_edge=self.lnT_done, evac="dv")
        if self.upto == "lat":
            self.finalize_dbg()
            return
        self.k_done = self.ph_proj(self.k0, self.wku16, 4, ND, src=self.latT,
                                   src_edge=self.lat_done, evac="ac")
        self.kproj_pe = ("pe", self.s_pe.n)
        if self.upto == "kp":
            self.finalize_dbg()
            return
        self.ph_vproj()
        if self.upto == "vp":
            self.finalize_dbg()
            return
        self.ph_wo_load()
        self.ropeq_done = self.ph_rope(self.q0, self.q_done)
        if self.upto == "ropeq":
            self.finalize_dbg()
            return
        self.ropek_done = self.ph_rope(self.k0, self.k_done)
        for nm, shape, dt in [("q", [D, T], F16), ("k", [D, T], F16),
                              ("lat", [512, T], F16), ("v", [T, D], BF16)]:
            if nm in self.dbg:
                self.dbg_h[nm] = self.nc.dram_tensor("dbg_" + nm, shape, dt, kind="ExternalOutput").ap()
        if self.upto == "qkv":
            self.finalize_dbg()
            return
        self.ph_attention_wo()
        if "x1" in self.dbg:
            self.dbg_h["x1"] = self.nc.dram_tensor("dbg_x1", [128, NT * D], F32, kind="ExternalOutput").ap()
        if "atok" in self.dbg:
            self.dbg_h["atok"] = self.nc.dram_tensor("dbg_atok", [128, NT * D], F16, kind="ExternalOutput").ap()
        if self.upto == "x1":
            self.finalize_dbg()
            return
        self.ph_ffn_loads()
        self.ph_ln_sbuf()
        self.ph_gram()
        self.ph_argmin()
        for nm, shape, dt in [("losses", [1, 8], F32), ("onehot", [8, 1], F16)]:
            if nm in self.dbg:
                self.dbg_h[nm] = self.nc.dram_tensor("dbg_" + nm, shape, dt, kind="ExternalOutput").ap()
        if self.upto == "route":
            self.finalize_dbg()
            return
        self.ph_adapter_base()
        self.ph_ln4_apply_store()
        if self.upto != "all":
            self.finalize_dbg()

    def ph_loads(self):
        blk, s_wt, s_ld = self.blk, self.s_wt, self.s_ld
        @blk.gpsimd
        def _(g):
            for kc in range(ND):
                s_wt.inc(g.dma_start(out=self.wd16[:, kc, :], in_=self.wd_h[128*kc:128*(kc+1), :]), 16)
            for kc in range(4):
                s_wt.inc(g.dma_start(out=self.wku16[:, kc, :], in_=self.wku_h[128*kc:128*(kc+1), :]), 16)
            for kc in range(4):
                s_wt.inc(g.dma_start(out=self.wvu16[:, kc, :], in_=self.wvu_h[128*kc:128*(kc+1), :]), 16)
        self.wt_attn = ("wt", s_wt.n)
        @blk.sync
        def _(s):
            s_ld.inc(s.dma_start(out=self.idt16[:], in_=self.idt_h), 16)
            s_ld.inc(s.dma_start(out=self.swp16[:], in_=self.swp_h), 16)
            s_ld.inc(s.dma_start(out=self.ropeA, in_=self.ra_h), 16)
            s_ld.inc(s.dma_start(out=self.ropeB, in_=self.rb_h), 16)
            s_ld.inc(s.dma_start(out=self.iota8[:], in_=self.iota8_h), 16)
        self.ld_const = ("ld", self.s_ld.n)
        @blk.vector
        def _(v):
            v = EW(v, self.s_dv)
            v.memset(self.ones16[:], 1.0)
            v.memset(self.onesbf[:], 1.0)
            v.memset(self.onesf32[:], 1.0)
            v.memset(self.ones512[:], 1.0)
            v.memset(self.eps_t[:], EPS)
            v.memset(self.zero1[:], 0.0)
            v.memset(self.c30[:], -EXPC)
        self.init_dv = ("dv", self.s_dv.n)

    def ln_combine(self, v_edge, a_edge):
        blk = self.blk
        s_dv, s_ac = self.s_dv, self.s_ac
        @blk.vector
        def _(v):
            v = EW(v, self.s_dv)
            if v_edge: v.wait_ge(s_dv.h, v_edge)
            if a_edge: v.wait_ge(s_ac.h, a_edge)
            v.tensor_scalar(self.means[:], self.sums[:], 1.0 / D, None, ALU.mult)
            v.tensor_scalar(self.vart[:], self.sumsq[:], 1.0 / D, None, ALU.mult)
            v.tensor_tensor(self.rstds[:], self.means[:], self.means[:], ALU.mult)
            v.tensor_tensor(self.vart[:], self.vart[:], self.rstds[:], ALU.subtract)
        nvar = s_dv.n
        @blk.scalar
        def _(a):
            a = EW(a, self.s_ac)
            a.wait_ge(s_dv.h, nvar)
            a.activation(self.vart[:], self.vart[:], AF.Sqrt, bias=self.eps_t[:], scale=1.0)
        nstd = s_ac.n
        @blk.vector
        def _(v):
            v = EW(v, self.s_dv)
            v.wait_ge(s_ac.h, nstd)
            v.reciprocal(self.rstds[:], self.vart[:])
            v.tensor_tensor(self.lnbias[:], self.means[:], self.rstds[:], ALU.mult)
            v.tensor_scalar(self.lnbias[:], self.lnbias[:], -1.0, None, ALU.mult)
        return s_dv.n

    def ph_ln_hbm(self, src_h, dst_T):
        blk = self.blk
        s_ld, s_dv, s_ac, s_pe = self.s_ld, self.s_dv, self.s_ac, self.s_pe
        src_t = src_h.rearrange("(n p) d -> n p d", p=128)
        lds = [self.s_ld0, self.s_ld1]
        B0, B1 = lds[0].n, lds[1].n
        V0, A0 = s_dv.n, s_ac.n
        ldn = [[B0, B1][i % 2] + 16 * (i // 2 + 1) for i in range(NT)]
        vn = [V0 + 1 + (i + 1) for i in range(NT)]
        an = [A0 + (i + 1) for i in range(NT)]
        @blk.sync
        def _(s):
            for i in range(NT):
                if i >= 2:
                    s.wait_ge(s_dv.h, vn[i-2])
                    s.wait_ge(s_ac.h, an[i-2])
                lds[i % 2].inc(s.dma_start(out=self.xst[:, i % 2, :], in_=src_t[i]), 16)
        z0 = s_dv.n + 1
        @blk.vector
        def _(v):
            v = EW(v, self.s_dv)
            self.wait(v, self.init_dv)
            v.memset(self.sumsq[:], 0.0)
            for i in range(NT):
                v.wait_ge(lds[i % 2].h, ldn[i])
                v.tensor_reduce(self.sums[:, i:i+1], self.xst[:, i % 2, :], AX.X, ALU.add)
        @blk.scalar
        def _(a):
            a = EW(a, self.s_ac)
            self.wait(a, self.init_dv)
            a.wait_ge(s_dv.h, z0)
            for i in range(NT):
                a.wait_ge(lds[i % 2].h, ldn[i])
                if i > 0:
                    a.wait_ge(s_ac.h, an[i-1])
                a.activation(self.sqscr[:], self.xst[:, i % 2, :], AF.Square,
                                      bias=self.zero1[:], scale=1.0, accum_out=self.sumsq[:, i:i+1])
        rdy = self.ln_combine(vn[-1], an[-1])

        C0, C1 = lds[0].n, lds[1].n
        V1, P0 = s_dv.n, s_pe.n
        ldn2 = [[C0, C1][i % 2] + 16 * (i // 2 + 1) for i in range(NT)]
        per = 1 + ND
        vxn = [V1 + i * per + 1 for i in range(NT)]
        vev = [[V1 + i * per + 1 + (c + 1) for c in range(ND)] for i in range(NT)]
        trn = [[P0 + i * ND + c + 1 for c in range(ND)] for i in range(NT)]
        @blk.sync
        def _(s):
            s.wait_ge(s_dv.h, vn[NT-1])
            s.wait_ge(s_ac.h, an[NT-1])
            for i in range(NT):
                if i >= 2:
                    s.wait_ge(s_dv.h, vxn[i-2])
                lds[i % 2].inc(s.dma_start(out=self.xst[:, i % 2, :], in_=src_t[i]), 16)
        @blk.tensor
        def _(t):
            self.wait(t, self.ld_const)
            self.bank_wait(t, "T1")
            self.bank_wait(t, "T2")
            for i in range(NT):
                t.wait_ge(s_dv.h, vxn[i])
                for c in range(ND):
                    if i * ND + c >= 2:
                        pi, pc = divmod(i * ND + c - 2, ND)
                        t.wait_ge(s_dv.h, vev[pi][pc])
                    pb = self.psT1 if c % 2 == 0 else self.psT2
                    s_pe.inc(t.transpose(pb[:], self.xnst[:, i % 2, 128*c:128*(c+1)], self.idt16[:]))
        @blk.vector
        def _(v):
            v = EW(v, self.s_dv)
            v.wait_ge(s_dv.h, rdy)
            for i in range(NT):
                v.wait_ge(lds[i % 2].h, ldn2[i])
                if i >= 2:
                    v.wait_ge(s_pe.h, trn[i-2][-1])
                v.tensor_scalar(self.xnst[:, i % 2, :], self.xst[:, i % 2, :],
                                         self.means[:, i:i+1], self.rstds[:, i:i+1],
                                         ALU.subtract, ALU.mult)
                for c in range(ND):
                    v.wait_ge(s_pe.h, trn[i][c])
                    pb = self.psT1 if c % 2 == 0 else self.psT2
                    v.tensor_copy(dst_T[:, c, 128*i:128*(i+1)], pb[:])
        assert s_dv.n == vev[-1][-1]
        self.bank_free["T1"] = ("dv", vev[-1][-2])
        self.bank_free["T2"] = ("dv", vev[-1][-1])
        self.xst_free = ("dv", vxn[-1])
        return ("dv", s_dv.n)

    def ph_proj_q(self):
        blk = self.blk
        s_pe, s_ac = self.s_pe, self.s_ac
        P0, A0 = s_pe.n, s_ac.n
        wqs = [self.s_wq0, self.s_wq1]
        Q0, Q1 = wqs[0].n, wqs[1].n
        wld = [[Q0, Q1][m % 2] + 16 * ND * (m // 2 + 1) for m in range(ND)]
        mm_n = [[P0 + m * TC + tc + 1 for tc in range(TC)] for m in range(ND)]
        ev_n = [[A0 + m * TC + tc + 1 for tc in range(TC)] for m in range(ND)]
        @blk.gpsimd
        def _(g):
            for m in range(ND):
                if m >= 2:
                    g.wait_ge(s_pe.h, mm_n[m-2][-1])
                for kc in range(ND):
                    wqs[m % 2].inc(g.dma_start(out=self.wqa[:, m % 2, kc, :],
                                               in_=self.wq_h[128*kc:128*(kc+1), 128*m:128*(m+1)]), 16)
        @blk.tensor
        def _(t):
            self.wait(t, self.lnT_done)
            gi = 0
            for m in range(ND):
                t.wait_ge(wqs[m % 2].h, wld[m])
                for tc in range(TC):
                    pb = self.psA if gi % 2 == 0 else self.psB
                    if gi >= 2:
                        pm, ptc = divmod(gi - 2, TC)
                        t.wait_ge(s_ac.h, ev_n[pm][ptc])
                    else:
                        self.bank_wait(t, "A" if gi % 2 == 0 else "B")
                    ins = None
                    for k in range(ND):
                        ins = t.matmul(pb[:], self.wqa[:, m % 2, k, :],
                                       self.xnT[:, k, 512*tc:512*(tc+1)],
                                       start=(k == 0), stop=(k == ND - 1))
                    s_pe.inc(ins)
                    gi += 1
        @blk.scalar
        def _(a):
            a = EW(a, self.s_ac)
            gi = 0
            for m in range(ND):
                for tc in range(TC):
                    pb = self.psA if gi % 2 == 0 else self.psB
                    a.wait_ge(s_pe.h, mm_n[m][tc])
                    a.activation(self.q0[:, m, 512*tc:512*(tc+1)], pb[:], AF.Copy,
                                               bias=0.0, scale=1.0)
                    gi += 1
        self.bank_free["A"] = ("ac", ev_n[-1][-2])
        self.bank_free["B"] = ("ac", ev_n[-1][-1])
        self.qproj_pe = ("pe", s_pe.n)
        return ("ac", self.s_ac.n)

    def ph_wo_load(self):
        blk = self.blk
        @blk.gpsimd
        def _(g):
            self.wait(g, self.kproj_pe)
            self.wait(g, self.vproj_pe)
            for kc in range(ND):
                self.s_wo.inc(g.dma_start(out=self.woa[:, kc, :], in_=self.wo_h[128*kc:128*(kc+1), :]), 16)
        self.wo_ld = ("wo", self.s_wo.n)

    def ph_proj(self, dst, wsb, nK, nM, src, src_edge, evac="ac", wt_edge=None):
        blk = self.blk
        s_pe = self.s_pe
        es_ = self.sem[evac]
        P0, E0 = s_pe.n, es_.n
        ngrp = nM * TC
        mm_n = [P0 + g + 1 for g in range(ngrp)]
        ev_n = [E0 + g + 1 for g in range(ngrp)]
        @blk.tensor
        def _(t):
            self.wait(t, src_edge)
            self.wait(t, wt_edge if wt_edge is not None else self.wt_attn)
            gi = 0
            for m in range(nM):
                for tc in range(TC):
                    pb = self.psA if gi % 2 == 0 else self.psB
                    if gi >= 2:
                        t.wait_ge(es_.h, ev_n[gi-2])
                    else:
                        self.bank_wait(t, "A" if gi % 2 == 0 else "B")
                    ins = None
                    for k in range(nK):
                        ins = t.matmul(pb[:], wsb[:, k, 128*m:128*(m+1)],
                                       src[:, k, 512*tc:512*(tc+1)],
                                       start=(k == 0), stop=(k == nK - 1))
                    s_pe.inc(ins)
                    gi += 1
        eng = blk.scalar if evac == "ac" else blk.vector
        @eng
        def _(a):
            gi = 0
            for m in range(nM):
                for tc in range(TC):
                    pb = self.psA if gi % 2 == 0 else self.psB
                    a.wait_ge(s_pe.h, mm_n[gi])
                    if evac == "ac":
                        es_.inc(a.activation(dst[:, m, 512*tc:512*(tc+1)], pb[:], AF.Copy,
                                             bias=0.0, scale=1.0))
                    else:
                        es_.inc(a.tensor_copy(dst[:, m, 512*tc:512*(tc+1)], pb[:]))
                    gi += 1
        self.bank_free["A"] = (evac, ev_n[-2] if ngrp >= 2 else ev_n[-1])
        self.bank_free["B"] = (evac, ev_n[-1])
        return (evac, es_.n)

    def ph_vproj(self):
        blk, s_pe, s_dv = self.blk, self.s_pe, self.s_dv
        P0, V0 = s_pe.n, s_dv.n
        ngrp = NT * 2
        mm_n = [P0 + g + 1 for g in range(ngrp)]
        ev_n = [V0 + g + 1 for g in range(ngrp)]
        @blk.tensor
        def _(t):
            self.wait(t, self.lat_done)
            gi = 0
            for i in range(NT):
                for vc in range(2):
                    pb = self.psA if gi % 2 == 0 else self.psB
                    if gi >= 2:
                        t.wait_ge(s_dv.h, ev_n[gi-2])
                    else:
                        self.bank_wait(t, "A" if gi % 2 == 0 else "B")
                    ins = None
                    for kc in range(4):
                        ins = t.matmul(pb[:], self.latT[:, kc, 128*i:128*(i+1)],
                                       self.wvu16[:, kc, 512*vc:512*(vc+1)],
                                       start=(kc == 0), stop=(kc == 3))
                    s_pe.inc(ins)
                    gi += 1
        @blk.vector
        def _(v):
            v = EW(v, self.s_dv)
            gi = 0
            for i in range(NT):
                for vc in range(2):
                    pb = self.psA if gi % 2 == 0 else self.psB
                    v.wait_ge(s_pe.h, mm_n[gi])
                    v.tensor_copy(self.vstore[:, i, 512*vc:512*(vc+1)], pb[:])
                    gi += 1
        self.bank_free["A"] = ("dv", ev_n[-2])
        self.bank_free["B"] = ("dv", ev_n[-1])
        self.vproj_pe = ("pe", s_pe.n)
        self.v_done = ("dv", s_dv.n)

    def ph_rope(self, buf, edge):
        blk, s_pe, s_dv = self.blk, self.s_pe, self.s_dv
        P0, V0 = s_pe.n, s_dv.n
        ngrp = ND * TC
        mm_n = [P0 + g + 1 for g in range(ngrp)]
        dv_n = [V0 + 3 * (g + 1) for g in range(ngrp)]
        @blk.tensor
        def _(t):
            self.wait(t, edge)
            self.wait(t, self.ld_const)
            gi = 0
            for c in range(ND):
                for tc in range(TC):
                    pb = self.psA if gi % 2 == 0 else self.psB
                    if gi >= 2:
                        t.wait_ge(s_dv.h, dv_n[gi-2])
                    else:
                        self.bank_wait(t, "A" if gi % 2 == 0 else "B")
                    s_pe.inc(t.matmul(pb[:], self.swp16[:], buf[:, c, 512*tc:512*(tc+1)],
                                      start=True, stop=True))
                    gi += 1
        @blk.vector
        def _(v):
            v = EW(v, self.s_dv)
            gi = 0
            for c in range(ND):
                for tc in range(TC):
                    pb = self.psA if gi % 2 == 0 else self.psB
                    v.wait_ge(s_pe.h, mm_n[gi])
                    sl = slice(512*tc, 512*(tc+1))
                    v.tensor_tensor(self.rsc1[:], buf[:, c, sl], self.ropeA[:, sl], ALU.mult)
                    v.tensor_tensor(self.rsc2[:], pb[:], self.ropeB[:, sl], ALU.mult)
                    v.tensor_tensor(buf[:, c, sl], self.rsc1[:], self.rsc2[:], ALU.add)
                    gi += 1
        self.bank_free["A"] = ("dv", dv_n[-2])
        self.bank_free["B"] = ("dv", dv_n[-1])
        return ("dv", s_dv.n)

    def ph_attention_wo(self):
        blk = self.blk
        s_pe, s_dv, s_ac, s_ld = self.s_pe, self.s_dv, self.s_ac, self.s_ld
        self.u_prev = dict(vstop=0, bca=0)
        self.opage_free = [None, None]
        self.att2_free = [None, None]
        @blk.vector
        def _(v):
            v = EW(v, self.s_dv)
            v.memset(self.sumsq[:], 0.0)
        for qc in range(NP):
            ob = qc % 2
            for h in range(NH):
                self.attn_unit(h, qc, ob)
            page_units_done = ("dv", s_dv.n)
            P0, A0 = s_pe.n, s_ac.n
            mm_n = [P0 + m + 1 for m in range(ND)]
            woev = [A0 + m + 1 for m in range(ND)]
            @blk.tensor
            def _(t):
                self.wait(t, page_units_done)
                if qc == 0:
                    self.wait(t, self.wo_ld)
                for m in range(ND):
                    pb = self.psS1 if m % 2 == 0 else self.psS2
                    if m >= 2:
                        t.wait_ge(s_ac.h, woev[m-2])
                    else:
                        self.bank_wait(t, "S1" if m % 2 == 0 else "S2")
                    ins = None
                    for k in range(ND):
                        ins = t.matmul(pb[:], self.woa[:, k, 128*m:128*(m+1)],
                                       self.opage[:, ob, k, :],
                                       start=(k == 0), stop=(k == ND - 1))
                    s_pe.inc(ins)
            @blk.scalar
            def _(a):
                a = EW(a, self.s_ac)
                self.wait(a, self.att2_free[ob])
                for m in range(ND):
                    pb = self.psS1 if m % 2 == 0 else self.psS2
                    a.wait_ge(s_pe.h, mm_n[m])
                    a.activation(self.att2p[:, ob, m, :], pb[:], AF.Copy,
                                          bias=0.0, scale=1.0)
            self.bank_free["S1"] = ("ac", woev[-2])
            self.bank_free["S2"] = ("ac", woev[-1])
            self.opage_free[ob] = ("pe", mm_n[-1])
            P1, V0, A1 = s_pe.n, s_dv.n, s_ac.n
            trn = [P1 + gg + 1 for gg in range(2 * ND)]
            vevt = [V0 + gg + (gg // ND) + 1 for gg in range(2 * ND)]
            @blk.tensor
            def _(t):
                t.wait_ge(s_ac.h, woev[-1])
                for gg in range(2 * ND):
                    c = gg % ND
                    if gg >= 2:
                        t.wait_ge(s_dv.h, vevt[gg-2])
                    else:
                        self.bank_wait(t, "T1" if gg % 2 == 0 else "T2")
                    pb = self.psT1 if gg % 2 == 0 else self.psT2
                    s_pe.inc(t.transpose(pb[:], self.att2p[:, ob, c, 128*(gg//ND):128*(gg//ND+1)],
                                         self.idt16[:]))
            @blk.vector
            def _(v):
                v = EW(v, self.s_dv)
                for half in range(2):
                    i = 2 * qc + half
                    for c in range(ND):
                        gg = half * ND + c
                        v.wait_ge(s_pe.h, trn[gg])
                        pb = self.psT1 if gg % 2 == 0 else self.psT2
                        v.tensor_copy(self.att_tok[:, i, 128*c:128*(c+1)], pb[:])
                    v.tensor_reduce(self.sums[:, i:i+1], self.att_tok[:, i, :], AX.X, ALU.add)
            assert s_dv.n == V0 + 2 * ND + 2
            @blk.scalar
            def _(a):
                a = EW(a, self.s_ac)
                for half in range(2):
                    i = 2 * qc + half
                    a.wait_ge(s_dv.h, V0 + (half + 1) * (ND + 1))
                    a.activation(self.sqscr[:], self.att_tok[:, i, :], AF.Square,
                                          bias=self.zero1[:], scale=1.0, accum_out=self.sumsq[:, i:i+1])
            self.bank_free["T1"] = ("dv", vevt[-2])
            self.bank_free["T2"] = ("dv", vevt[-1])
            self.att2_free[ob] = ("ac", s_ac.n)
        rdy = self.ln_combine(s_dv.n, s_ac.n)
        lds = [self.s_ld0, self.s_ld1]
        B0, B1 = lds[0].n, lds[1].n
        V1 = s_dv.n
        ldn = [[B0, B1][i % 2] + 16 * (i // 2 + 1) for i in range(NT)]
        src_t = self.x_h.rearrange("(n p) d -> n p d", p=128)
        x1t = lambda i: (self.x1a[:, i, :] if i < 8 else self.x1b[:, i - 8, :])
        @blk.sync
        def _(s):
            self.wait(s, self.xst_free)
            for i in range(NT):
                if i >= 2:
                    s.wait_ge(s_dv.h, V1 + 2 * (i - 1))
                lds[i % 2].inc(s.dma_start(out=self.xst[:, i % 2, :], in_=src_t[i]), 16)
        @blk.vector
        def _(v):
            v = EW(v, self.s_dv)
            v.wait_ge(s_dv.h, rdy)
            for i in range(NT):
                v.wait_ge(lds[i % 2].h, ldn[i])
                v.tensor_scalar(x1t(i), self.att_tok[:, i, :],
                                self.rstds[:, i:i+1], self.lnbias[:, i:i+1],
                                ALU.mult, ALU.add)
                v.tensor_tensor(x1t(i), x1t(i), self.xst[:, i % 2, :], ALU.add)
        self.x1_done = ("dv", s_dv.n)
        self.xst_free = self.x1_done

    def attn_unit(self, h, qc, ob):
        blk = self.blk
        s_pe, s_dv, s_ac = self.s_pe, self.s_dv, self.s_ac
        g, rb = h // 2, 64 * (h % 2)
        r64 = 64 if rb == 0 else 0
        ntk = 2 * (qc + 1)
        qsl = slice(PAGE * qc, PAGE * (qc + 1))
        P0, A0, V0 = s_pe.n, s_ac.n, s_dv.n
        smm = [P0 + j + 1 for j in range(ntk)]
        vstop = P0 + ntk + 1
        bca = P0 + ntk + 2
        expn = [A0 + j + 1 for j in range(ntk)]
        oev = A0 + ntk + 1
        rchain = V0 + 4
        ftt = V0 + 5
        first_edges = [self.ropeq_done, self.ropek_done, self.v_done] if (h == 0 and qc == 0) else []
        oedge = self.opage_free[ob] if h == 0 else None
        prev_vstop, prev_bca = self.u_prev["vstop"], self.u_prev["bca"]

        @blk.tensor
        def _(t):
            for e in first_edges:
                self.wait(t, e)
            for j in range(ntk):
                pb = self.psS1 if j % 2 == 0 else self.psS2
                if j >= 2:
                    t.wait_ge(s_ac.h, expn[j-2])
                else:
                    self.bank_wait(t, "S1" if j % 2 == 0 else "S2")
                s_pe.inc(t.matmul(pb[:], self.k0[rb:rb+64, g, 128*j:128*(j+1)],
                                  self.q0[rb:rb+64, g, qsl], start=True, stop=True))
            self.bank_wait(t, "O")
            self.bank_wait(t, "BC")
            ins = None
            for j in range(ntk):
                t.wait_ge(s_ac.h, expn[j])
                t.matmul(self.psO[rb:rb+64, :], self.vstore[:, j, 64*h:64*(h+1)],
                         self.ptile[:, j, :], start=(j == 0), stop=(j == ntk - 1),
                         tile_position=(0, rb))
                ins = t.matmul(self.psBC[r64:r64+1, :], self.onesbf[:, 0:1],
                               self.ptile[:, j, :], start=(j == 0), stop=(j == ntk - 1),
                               tile_position=(0, r64))
            s_pe.inc(ins)
            t.wait_ge(s_dv.h, rchain)
            self.bank_wait(t, "BC")
            t.matmul(self.psBC[rb:rb+64, :], self.onesbf[r64:r64+1, :],
                     self.rhl[r64:r64+1, 0, :], start=True, stop=False,
                     tile_position=(r64, rb))
            s_pe.inc(t.matmul(self.psBC[rb:rb+64, :], self.onesbf[r64:r64+1, :],
                              self.rhl[r64:r64+1, 1, :], start=False, stop=True,
                              tile_position=(r64, rb)))
        assert s_pe.n == bca

        @blk.scalar
        def _(a):
            a = EW(a, self.s_ac)
            for j in range(ntk):
                pb = self.psS1 if j % 2 == 0 else self.psS2
                a.wait_ge(s_pe.h, smm[j])
                if j == 0 and prev_vstop:
                    a.wait_ge(s_pe.h, prev_vstop)
                a.activation(self.ptile[:, j, :], pb[:], AF.Exp,
                                      bias=self.c30[:], scale=0.125)
            a.wait_ge(s_dv.h, rchain)
            a.activation(self.osc[rb:rb+64, :], self.psO[rb:rb+64, :], AF.Copy,
                                  bias=0.0, scale=1.0)
        assert s_ac.n == oev

        @blk.vector
        def _(v):
            v = EW(v, self.s_dv)
            v.wait_ge(s_pe.h, vstop)
            if prev_bca:
                v.wait_ge(s_pe.h, prev_bca)
            v.reciprocal(self.rrec[r64:r64+1, :], self.psBC[r64:r64+1, :])
            v.tensor_copy(self.rhl[r64:r64+1, 0, :], self.rrec[r64:r64+1, :])
            v.tensor_tensor(self.rlo32[r64:r64+1, :], self.rrec[r64:r64+1, :],
                            self.rhl[r64:r64+1, 0, :], ALU.subtract)
            v.tensor_copy(self.rhl[r64:r64+1, 1, :], self.rlo32[r64:r64+1, :])
            v.wait_ge(s_ac.h, oev)
            v.wait_ge(s_pe.h, bca)
            self.wait(v, oedge)
            v.tensor_tensor(self.opage[rb:rb+64, ob, g, :], self.osc[rb:rb+64, :],
                                     self.psBC[rb:rb+64, :], ALU.mult)
        assert s_dv.n == ftt

        self.bank_free["S1"] = ("ac", expn[-2] if ntk >= 2 else expn[-1])
        self.bank_free["S2"] = ("ac", expn[-1])
        self.bank_free["O"] = ("dv", ftt)
        self.bank_free["BC"] = ("dv", ftt)
        self.u_prev = dict(vstop=vstop, bca=bca)

    def ph_ffn_loads(self):
        blk = self.blk
        @blk.gpsimd
        def _(g):
            self.wait(g, self.vproj_pe)
            self.wait(g, self.kproj_pe)
            self.wait(g, self.x1_done)
            s_wf = self.s_wf
            s_wf.inc(g.dma_start(out=self.bd1f32, in_=self.bd1_h), 16)
            s_wf.inc(g.dma_start(out=self.bd2c32[:], in_=self.bd2c_h), 16)
            s_wf.inc(g.dma_start(out=self.bbasec[:], in_=self.bbase_h), 16)
            s_wf.inc(g.dma_start(out=self.bads[:], in_=self.badn_h), 16)
            s_wf.inc(g.dma_start(out=self.baup16, in_=self.baup_h), 16)
            s_wf.inc(g.dma_start(out=self.eexp16, in_=self.eexp_h), 16)
            s_wf.inc(g.dma_start(out=self.bbaser[:], in_=self.bbaser_h), 16)
        self.ffn_bias_ld = ("wf", self.s_wf.n)

    def ph_ln_sbuf(self):
        blk = self.blk
        s_dv, s_ac, s_pe = self.s_dv, self.s_ac, self.s_pe
        x1t = lambda i: (self.x1a[:, i, :] if i < 8 else self.x1b[:, i - 8, :])
        V0, A0 = s_dv.n, s_ac.n
        @blk.vector
        def _(v):
            v = EW(v, self.s_dv)
            self.wait(v, self.x1_done)
            v.memset(self.sumsq[:], 0.0)
            for i in range(NT):
                v.tensor_reduce(self.sums[:, i:i+1], x1t(i), AX.X, ALU.add)
        @blk.scalar
        def _(a):
            a = EW(a, self.s_ac)
            a.wait_ge(s_dv.h, V0 + 1)
            for i in range(NT):
                a.activation(self.sqscr[:], x1t(i), AF.Square,
                                      bias=self.zero1[:], scale=1.0, accum_out=self.sumsq[:, i:i+1])
        rdy = self.ln_combine(V0 + 1 + NT, A0 + NT)
        V1, P0 = s_dv.n, s_pe.n
        per = 2 + ND
        vxn = [V1 + i * per + 1 for i in range(NT)]
        vev = [[V1 + i * per + 2 + (c + 1) for c in range(ND)] for i in range(NT)]
        trn = [[P0 + i * ND + c + 1 for c in range(ND)] for i in range(NT)]
        @blk.tensor
        def _(t):
            self.bank_wait(t, "T1")
            self.bank_wait(t, "T2")
            for i in range(NT):
                t.wait_ge(s_dv.h, vxn[i])
                for c in range(ND):
                    if i * ND + c >= 2:
                        pi, pc = divmod(i * ND + c - 2, ND)
                        t.wait_ge(s_dv.h, vev[pi][pc])
                    pb = self.psT1 if c % 2 == 0 else self.psT2
                    s_pe.inc(t.transpose(pb[:], self.xnst[:, i % 2, 128*c:128*(c+1)], self.idt16[:]))
        @blk.vector
        def _(v):
            v = EW(v, self.s_dv)
            v.wait_ge(s_dv.h, rdy)
            for i in range(NT):
                if i >= 2:
                    v.wait_ge(s_pe.h, trn[i-2][-1])
                v.tensor_scalar(self.xnst[:, i % 2, :], x1t(i),
                                         self.means[:, i:i+1], self.rstds[:, i:i+1],
                                         ALU.subtract, ALU.mult)
                v.tensor_copy(self.xn2_tok[:, i, :], self.xnst[:, i % 2, :])
                for c in range(ND):
                    v.wait_ge(s_pe.h, trn[i][c])
                    pb = self.psT1 if c % 2 == 0 else self.psT2
                    v.tensor_copy(self.xn2T[:, c, 128*i:128*(i+1)], pb[:])
        assert s_dv.n == vev[-1][-1]
        self.bank_free["T1"] = ("dv", vev[-1][-2])
        self.bank_free["T2"] = ("dv", vev[-1][-1])
        self.ln3_done = ("dv", s_dv.n)

    def ph_gram(self):
        blk = self.blk
        s_pe, s_dv, s_ac = self.s_pe, self.s_dv, self.s_ac
        @blk.vector
        def _(v):
            v = EW(v, self.s_dv)
            self.wait(v, self.ln3_done)
            v.wait_ge(self.s_wf.h, self.ffn_bias_ld[1])
            v.memset(self.htok[:, :, 256:258], 1.0)
            v.memset(self.htok[:, :, 258:RA], 0.0)
            v.memset(self._wd2th[:, :, 256:RA], 0.0)
            v.memset(self._wd2tl[:, :, 256:RA], 0.0)
            v.memset(self.lacc[:], 0.0)
            v.tensor_scalar(self.iom[:], self.iota8[:], -99.0, None, ALU.add)
            v.tensor_copy(self.bd1f, self.bd1f32[:])
            v.tensor_tensor(self.bd1l[:], self.bd1f32[:], self.bd1f, ALU.subtract)
            v.tensor_copy(self.bd2ch[:], self.bd2c32[:])
            v.tensor_tensor(self.bd2cl[:], self.bd2c32[:], self.bd2ch[:], ALU.subtract)
        aug_init = s_dv.n

        PE_PER = NT + 6 + ND
        DV_PER = 71
        AC_PER = NT + 3
        PEB, ACB, DVB = s_pe.n, s_ac.n, s_dv.n
        pe_done = [PEB + PE_PER * (e + 1) for e in range(NE)]
        wld1 = [self.s_we0.n + 128 * (e + 1) for e in range(NE)]
        wld2 = [self.s_we1.n + 128 * (e + 1) for e in range(NE)]
        der1 = [DVB + DV_PER * e + 2 for e in range(NE)]
        der2 = [DVB + DV_PER * e + 6 for e in range(NE)]
        s1dotn = [[DVB + DV_PER * e + 6 + 3 * (rc + 1) for rc in range(3)] for e in range(NE)]
        xhdotn = [[DVB + DV_PER * e + 15 + 7 * (dm + 1) for dm in range(ND)] for e in range(NE)]
        relun = [[ACB + AC_PER * e + (i + 1) for i in range(NT)] for e in range(NE)]
        s1evn = [[ACB + AC_PER * e + NT + (rc + 1) for rc in range(3)] for e in range(NE)]
        htokn = [[PEB + PE_PER * e + (i + 1) for i in range(NT)] for e in range(NE)]
        s1mmn = [[PEB + PE_PER * e + NT + 2 * rc + 1 for rc in range(3)] for e in range(NE)]
        gmmn = [[PEB + PE_PER * e + NT + 2 * rc + 2 for rc in range(3)] for e in range(NE)]
        xhmmn = [[PEB + PE_PER * e + NT + 6 + (dm + 1) for dm in range(ND)] for e in range(NE)]

        @blk.gpsimd
        def _(g):
            g.wait_ge(s_dv.h, aug_init)
            for e in range(NE):
                if e > 0:
                    g.wait_ge(s_pe.h, pe_done[e-1])
                    g.wait_ge(s_dv.h, xhdotn[e-1][-1])
                for dc in range(ND):
                    self.s_we0.inc(g.dma_start(out=self.wstage[:, dc, :],
                                               in_=self.wd1_h[e, 128*dc:128*(dc+1), :]), 16)
                g.wait_ge(s_dv.h, der1[e])
                for dc in range(ND):
                    self.s_we1.inc(g.dma_start(out=self.wstage[:, dc, :],
                                               in_=self.wd2t_h[e, 128*dc:128*(dc+1), :]), 16)
        assert self.s_we1.n == wld2[-1]

        @blk.tensor
        def _(t):
            t.wait_ge(s_dv.h, aug_init)
            for e in range(NE):
                t.wait_ge(s_dv.h, der2[e])
                for i in range(NT):
                    pb = self.psS1 if i % 2 == 0 else self.psS2
                    if i >= 2:
                        t.wait_ge(s_ac.h, relun[e][i-2])
                    elif e > 0:
                        t.wait_ge(s_ac.h, relun[e-1][NT-2+i])
                    else:
                        self.bank_wait(t, "S1" if i % 2 == 0 else "S2")
                    for dc in range(ND):
                        t.matmul(pb[:], self.xn2T[:, dc, 128*i:128*(i+1)],
                                 self._wd1h[:, dc, :], start=(dc == 0), stop=False)
                    for dc in range(ND):
                        t.matmul(pb[:], self.xn2T[:, dc, 128*i:128*(i+1)],
                                 self._wd1l[:, dc, :], start=False, stop=False)
                    t.matmul(pb[:], self.ones512[0:1, 0:128],
                             self.bd1f[0:1, 256*e:256*(e+1)], start=False, stop=False)
                    s_pe.inc(t.matmul(pb[:], self.ones512[0:1, 0:128],
                                      self.bd1l[0:1, 256*e:256*(e+1)], start=False, stop=True))
                t.wait_ge(s_ac.h, relun[e][NT-1])
                for rc in range(3):
                    mo, mw = MC[rc]
                    if rc > 0:
                        t.wait_ge(s_ac.h, s1evn[e][rc-1])
                    elif e > 0:
                        t.wait_ge(s_dv.h, xhdotn[e-1][ND-2])
                    else:
                        self.bank_wait(t, "A")
                    ins = None
                    for i in range(NT):
                        ins = t.matmul(self.psF1[0:mw, :], self.htok[:, i, mo:mo+mw],
                                       self.htok[:, i, :], start=(i == 0), stop=(i == NT - 1))
                    s_pe.inc(ins)
                    if rc > 0:
                        t.wait_ge(s_dv.h, s1dotn[e][rc-1])
                    elif e > 0:
                        t.wait_ge(s_dv.h, xhdotn[e-1][ND-1])
                    else:
                        self.bank_wait(t, "B")
                    ins = None
                    for wa in (self._wd2th, self._wd2tl):
                        for wb in (self._wd2th, self._wd2tl):
                            for dc in range(ND):
                                ins = t.matmul(self.psF2[0:mw, :], wa[:, dc, mo:mo+mw],
                                               wb[:, dc, :],
                                               start=(wa is self._wd2th and wb is self._wd2th and dc == 0),
                                               stop=(wa is self._wd2tl and wb is self._wd2tl and dc == ND - 1))
                    s_pe.inc(ins)
                for dm in range(ND):
                    pf = self.psF1 if dm % 2 == 0 else self.psF2
                    if dm == 0:
                        t.wait_ge(s_ac.h, s1evn[e][2])
                    elif dm == 1:
                        t.wait_ge(s_dv.h, s1dotn[e][2])
                    else:
                        t.wait_ge(s_dv.h, xhdotn[e][dm-2])
                    ins = None
                    for i in range(NT):
                        ins = t.matmul(pf[:], self.xn2_tok[:, i, 128*dm:128*(dm+1)],
                                       self.htok[:, i, :], start=(i == 0), stop=(i == NT - 1))
                    s_pe.inc(ins)
        assert s_pe.n == pe_done[-1], (s_pe.n, pe_done[-1])

        @blk.scalar
        def _(a):
            a = EW(a, self.s_ac)
            for e in range(NE):
                for i in range(NT):
                    pb = self.psS1 if i % 2 == 0 else self.psS2
                    a.wait_ge(s_pe.h, htokn[e][i])
                    if i == 0 and e > 0:
                        a.wait_ge(s_pe.h, xhmmn[e-1][ND-1])
                    a.activation(self.htok[:, i, 0:256], pb[:], AF.Relu,
                                 bias=self.zero1[:], scale=1.0)
                for rc in range(3):
                    mo, mw = MC[rc]
                    a.wait_ge(s_pe.h, s1mmn[e][rc])
                    if rc == 0 and e > 0:
                        a.wait_ge(s_dv.h, s1dotn[e-1][2])
                    elif rc > 0:
                        a.wait_ge(s_dv.h, s1dotn[e][rc-1])
                    a.activation(self.gsc[0:mw, :], self.psF1[0:mw, :], AF.Copy,
                                 bias=0.0, scale=1.0)
        assert s_ac.n == ACB + AC_PER * NE

        @blk.vector
        def _(v):
            v = EW(v, self.s_dv)
            for e in range(NE):
                v.wait_ge(self.s_we0.h, wld1[e])
                v.tensor_copy(self._wd1h[:, :, :], self.wstage[:, :, :])
                v.tensor_tensor(self._wd1l[:, :, :], self.wstage[:, :, :],
                                self._wd1h[:, :, :], ALU.subtract)
                assert s_dv.n == der1[e]
                v.wait_ge(self.s_we1.h, wld2[e])
                v.tensor_copy(self._wd2th[:, :, 0:256], self.wstage[:, :, :])
                v.tensor_tensor(self._wd2tl[:, :, 0:256], self.wstage[:, :, :],
                                self._wd2th[:, :, 0:256], ALU.subtract)
                v.tensor_copy(self._wd2th[:, :, 256:257], self.bd2ch[:, e, :].unsqueeze(2))
                v.tensor_copy(self._wd2tl[:, :, 257:258], self.bd2cl[:, e, :].unsqueeze(2))
                assert s_dv.n == der2[e]
                for rc in range(3):
                    mo, mw = MC[rc]
                    v.wait_ge(s_ac.h, s1evn[e][rc])
                    v.wait_ge(s_pe.h, gmmn[e][rc])
                    v.tensor_tensor(self.gsc2[0:mw, :], self.gsc[0:mw, :], self.psF2[0:mw, :], ALU.mult)
                    v.tensor_reduce(self.vart[0:mw, 0:1], self.gsc2[0:mw, :], AX.X, ALU.add)
                    v.tensor_tensor(self.lacc[0:mw, e:e+1], self.lacc[0:mw, e:e+1],
                                    self.vart[0:mw, 0:1], ALU.add)
                    assert s_dv.n == s1dotn[e][rc]
                for dm in range(ND):
                    pf = self.psF1 if dm % 2 == 0 else self.psF2
                    v.wait_ge(s_pe.h, xhmmn[e][dm])
                    v.tensor_tensor(self.gsc2[:, :], pf[:], self._wd2th[:, dm, :], ALU.mult)
                    v.tensor_reduce(self.vart[:, 0:1], self.gsc2[:, :], AX.X, ALU.add)
                    v.tensor_tensor(self.gsc2[:, :], pf[:], self._wd2tl[:, dm, :], ALU.mult)
                    v.tensor_reduce(self.vart[:, 1:2], self.gsc2[:, :], AX.X, ALU.add)
                    v.tensor_tensor(self.vart[:, 2:3], self.vart[:, 0:1], self.vart[:, 1:2], ALU.add)
                    v.tensor_scalar(self.vart[:, 3:4], self.vart[:, 2:3], -2.0, None, ALU.mult)
                    v.tensor_tensor(self.lacc[:, e:e+1], self.lacc[:, e:e+1],
                                    self.vart[:, 3:4], ALU.add)
                    assert s_dv.n == xhdotn[e][dm]
        assert s_dv.n == DVB + DV_PER * NE
        self.gram_done = ("dv", s_dv.n)
        self.gram_pe_done = pe_done[-1]
        self.gram_xh_last = xhdotn[-1]
        self.bank_free["S1"] = ("ac", relun[-1][NT-2])
        self.bank_free["S2"] = ("ac", relun[-1][NT-1])
        self.bank_free["A"] = ("dv", xhdotn[-1][ND-2])
        self.bank_free["B"] = ("dv", xhdotn[-1][ND-1])

    def ph_argmin(self):
        blk = self.blk
        s_pe, s_dv = self.s_pe, self.s_dv
        @blk.tensor
        def _(t):
            self.wait(t, self.gram_done)
            self.bank_wait(t, "A")
            s_pe.inc(t.matmul(self.psF1[0:1, 0:8], self.onesf32[:], self.lacc[:],
                              start=True, stop=True))
        n_lred = s_pe.n
        @blk.vector
        def _(v):
            v = EW(v, self.s_dv)
            v.wait_ge(s_pe.h, n_lred)
            v.tensor_copy(self.lfin[:], self.psF1[0:1, 0:8])
            v.tensor_reduce(self.rmin[:], self.lfin[:], AX.X, ALU.min)
            v.wait_ge(s_dv.h, s_dv.n)
            v.tensor_scalar(self.eqrow[:], self.lfin[:], self.rmin[:], None, ALU.is_equal)
            v.tensor_tensor(self.zrow[:], self.eqrow[:], self.iom[:], ALU.mult)
            v.tensor_scalar(self.zrow[:], self.zrow[:], 99.0, None, ALU.add)
            v.tensor_reduce(self.idxs[:], self.zrow[:], AX.X, ALU.min)
            v.wait_ge(s_dv.h, s_dv.n)
            v.tensor_scalar(self.ohrow[:], self.iota8[:], self.idxs[:], None, ALU.is_equal)
        n_ohrow = s_dv.n
        @blk.tensor
        def _(t):
            t.wait_ge(s_dv.h, n_ohrow)
            self.bank_wait(t, "B")
            s_pe.inc(t.matmul(self.psF2[0:8, 0:1], self.ohrow[:], self.onesf32[0:1, :],
                              start=True, stop=True))
        n_ohcol = s_pe.n
        @blk.vector
        def _(v):
            v = EW(v, self.s_dv)
            v.wait_ge(s_pe.h, n_ohcol)
            v.tensor_copy(self.onec[:], self.psF2[0:8, 0:1])
        n_onec = s_dv.n
        @blk.tensor
        def _(t):
            t.wait_ge(s_dv.h, n_onec)
            ins = None
            for rc in range(4):
                ins = t.matmul(self.psF1[:, rc:rc+1], self.eexp16[0:8, 128*rc:128*(rc+1)],
                               self.onec[:], start=True, stop=True)
            s_pe.inc(ins)
        n_ohexp = s_pe.n
        @blk.vector
        def _(v):
            v = EW(v, self.s_dv)
            v.wait_ge(s_pe.h, n_ohexp)
            v.tensor_copy(self.ohexp[:], self.psF1[:, 0:4])
            v.tensor_tensor(self.badm[:], self.bads[:], self.ohexp[:], ALU.mult)
        self.route_dv = s_dv.n
        PB, VB = s_pe.n, s_dv.n
        basen = [PB + m + 1 for m in range(ND)]
        bn = [VB + 2 * (m + 1) for m in range(ND)]
        @blk.tensor
        def _(t):
            t.wait_ge(s_dv.h, self.route_dv)
            for m in range(ND):
                pf = self.psF1 if m % 2 == 0 else self.psF2
                if m >= 2:
                    t.wait_ge(s_dv.h, bn[m-2])
                s_pe.inc(t.matmul(pf[0:1, 0:128], self.onec[:], self.baup16[0:8, 128*m:128*(m+1)],
                                  start=True, stop=True))
        @blk.vector
        def _(v):
            v = EW(v, self.s_dv)
            for m in range(ND):
                pf = self.psF1 if m % 2 == 0 else self.psF2
                v.wait_ge(s_pe.h, basen[m])
                v.tensor_copy(self.basel[0:1, m, :], pf[0:1, 0:128])
                v.tensor_tensor(self.basel[0:1, m, :], self.basel[0:1, m, :],
                                         self.bbaser[0:1, 128*m:128*(m+1)], ALU.add)
        self.argmin_done = ("dv", s_dv.n)
        self.bank_free["A"] = ("dv", bn[ND-2])
        self.bank_free["B"] = ("dv", bn[ND-1])

    def ph_adapter_base(self):
        blk = self.blk
        s_pe, s_dv, s_ac = self.s_pe, self.s_dv, self.s_ac
        wbslab = {}
        cmm_pre = {}
        PEB0 = s_pe.n
        for tc in range(TC):
            for m in range(ND):
                cmm_pre[(tc, m)] = PEB0 + tc * 12 + 4 + m + 1
        @blk.gpsimd
        def _(g):
            g.wait_ge(s_pe.h, self.gram_pe_done)
            g.wait_ge(s_dv.h, self.gram_xh_last[-1])
            for dc in range(ND):
                self.s_wa.inc(g.dma_start(out=self._wad[:, dc, :], in_=self.wadn_h[128*dc:128*(dc+1), :]), 16)
            for kc in range(4):
                self.s_wa.inc(g.dma_start(out=self._wau[:, kc, :], in_=self.waup_h[128*kc:128*(kc+1), :]), 16)
        wad_ld = ("wa", self.s_wa.n)
        wbs = [self.s_wb0, self.s_wb1]
        @blk.gpsimd
        def _(g):
            g.wait_ge(self.s_ac.h, self.q_done[1])
            gi = 0
            for tc in range(TC):
                for m in range(ND):
                    if gi >= 2:
                        prev = (gi - 2) // ND, (gi - 2) % ND
                        g.wait_ge(s_pe.h, cmm_pre[prev])
                    for kc in range(ND):
                        wbs[gi % 2].inc(g.dma_start(out=self.wqa[:, gi % 2, kc, :],
                                                    in_=self.wbase_h[128*kc:128*(kc+1), 128*m:128*(m+1)]), 16)
                    wbslab[(tc, m)] = wbs[gi % 2].n
                    gi += 1

        PEB, ACB = s_pe.n, s_ac.n
        admm = {}; adev = {}; cmm = {}; cev = {}
        for tc in range(TC):
            for rc in range(4):
                admm[(tc, rc)] = PEB + tc * 12 + rc + 1
                adev[(tc, rc)] = ACB + tc * 12 + rc + 1
            for m in range(ND):
                cmm[(tc, m)] = PEB + tc * 12 + 4 + m + 1
                cev[(tc, m)] = ACB + tc * 12 + 4 + m + 1
        @blk.tensor
        def _(t):
            self.wait(t, wad_ld)
            self.wait(t, self.argmin_done)
            for tc in range(TC):
                tsl = slice(512 * tc, 512 * (tc + 1))
                for rc in range(4):
                    pb = self.psA if rc % 2 == 0 else self.psB
                    if tc == 0 and rc < 2:
                        self.bank_wait(t, "A" if rc % 2 == 0 else "B")
                    else:
                        prev = (tc, rc - 2) if rc >= 2 else (tc - 1, ND - 2 + rc)
                        t.wait_ge(s_ac.h, adev[prev] if rc >= 2 else cev[(tc - 1, ND - 2 + rc)])
                    ins = None
                    for dc in range(ND):
                        ins = t.matmul(pb[:], self._wad[:, dc, 128*rc:128*(rc+1)],
                                       self.xn2T[:, dc, tsl], start=(dc == 0), stop=(dc == ND - 1))
                    s_pe.inc(ins)
                for m in range(ND):
                    pb = self.psA if m % 2 == 0 else self.psB
                    prev_ev = adev[(tc, 2 + (m % 2))] if m < 2 else cev[(tc, m - 2)]
                    t.wait_ge(s_ac.h, prev_ev)
                    t.wait_ge(s_ac.h, adev[(tc, 3)])
                    t.wait_ge(wbs[(tc * ND + m) % 2].h, wbslab[(tc, m)])
                    gi2 = tc * ND + m
                    for dc in range(ND):
                        t.matmul(pb[:], self.wqa[:, gi2 % 2, kc2 := dc, :][:, :] if False else self.wqa[:, gi2 % 2, dc, :],
                                 self.xn2T[:, dc, tsl], start=(dc == 0), stop=False)
                    for kc in range(4):
                        t.matmul(pb[:], self._wau[:, kc, 128*m:128*(m+1)],
                                 self.achunk[:, tc % 2, kc, :], start=False, stop=False)
                    s_pe.inc(t.matmul(pb[:], self.basel[0:1, m, :], self.ones512[:],
                                      start=False, stop=True, tile_position=(0, 0)))
        @blk.scalar
        def _(a):
            a = EW(a, self.s_ac)
            for tc in range(TC):
                tsl = slice(512 * tc, 512 * (tc + 1))
                for rc in range(4):
                    pb = self.psA if rc % 2 == 0 else self.psB
                    a.wait_ge(s_pe.h, admm[(tc, rc)])
                    a.activation(self.achunk[:, tc % 2, rc, :], pb[:], AF.Relu,
                                          bias=self.badm[:, rc:rc+1],
                                          scale=self.ohexp[:, rc:rc+1])
                for m in range(ND):
                    pb = self.psA if m % 2 == 0 else self.psB
                    a.wait_ge(s_pe.h, cmm[(tc, m)])
                    a.activation(self.ffn_fm[:, m, tsl], pb[:], AF.Copy,
                                          bias=0.0, scale=1.0)
        assert s_ac.n == ACB + TC * 12
        assert s_pe.n == PEB + TC * 12
        self.adapter_done = ("ac", s_ac.n)
        self.adapter_pe = ("pe", s_pe.n)
        self.bank_free["A"] = ("ac", cev[(TC-1, ND-2)])
        self.bank_free["B"] = ("ac", cev[(TC-1, ND-1)])

    def ph_ln4_apply_store(self):
        blk = self.blk
        s_pe, s_dv, s_ac, s_st = self.s_pe, self.s_dv, self.s_ac, self.s_st
        P0, V0, A0 = s_pe.n, s_dv.n, s_ac.n
        trn = [[P0 + i * ND + c + 1 for c in range(ND)] for i in range(NT)]
        per = ND + 1
        vev = [[V0 + i * per + (c + 1) for c in range(ND)] for i in range(NT)]
        vred = [V0 + i * per + ND + 1 for i in range(NT)]
        asq = [A0 + i + 1 for i in range(NT)]
        @blk.tensor
        def _(t):
            self.wait(t, self.adapter_done)
            self.wait(t, self.adapter_pe)
            self.bank_wait(t, "T1")
            self.bank_wait(t, "T2")
            for i in range(NT):
                for c in range(ND):
                    if i * ND + c >= 2:
                        pi, pc = divmod(i * ND + c - 2, ND)
                        t.wait_ge(s_dv.h, vev[pi][pc])
                    pb = self.psT1 if c % 2 == 0 else self.psT2
                    s_pe.inc(t.transpose(pb[:], self.ffn_fm[:, c, 128*i:128*(i+1)], self.idt16[:]))
        @blk.vector
        def _(v):
            v = EW(v, self.s_dv)
            for i in range(NT):
                for c in range(ND):
                    v.wait_ge(s_pe.h, trn[i][c])
                    pb = self.psT1 if c % 2 == 0 else self.psT2
                    v.tensor_copy(self.ffn_tok[:, i, 128*c:128*(c+1)], pb[:])
                v.tensor_reduce(self.sums[:, i:i+1], self.ffn_tok[:, i, :], AX.X, ALU.add)
        zz = s_dv.n
        @blk.vector
        def _(v):
            v = EW(v, self.s_dv)
            v.memset(self.sumsq[:], 0.0)
        zz = s_dv.n
        @blk.scalar
        def _(a):
            a = EW(a, self.s_ac)
            a.wait_ge(s_dv.h, zz)
            for i in range(NT):
                a.wait_ge(s_dv.h, vred[i])
                a.activation(self.sqscr[:], self.ffn_tok[:, i, :], AF.Square,
                                      bias=self.zero1[:], scale=1.0, accum_out=self.sumsq[:, i:i+1])
        rdy = self.ln_combine(vred[-1], asq[-1])
        x1t = lambda i: (self.x1a[:, i, :] if i < 8 else self.x1b[:, i - 8, :])
        V1 = s_dv.n
        appn = [V1 + 2 * (i + 1) for i in range(NT)]
        @blk.vector
        def _(v):
            v = EW(v, self.s_dv)
            v.wait_ge(s_dv.h, rdy)
            for i in range(NT):
                v.tensor_scalar(self.xst[:, 0, :], self.ffn_tok[:, i, :],
                                self.rstds[:, i:i+1], self.lnbias[:, i:i+1],
                                ALU.mult, ALU.add)
                v.tensor_tensor(x1t(i), x1t(i), self.xst[:, 0, :], ALU.add)
        @blk.sync
        def _(s):
            s.wait_ge(s_dv.h, appn[7])
            s_st.inc(s.dma_start(out=self.out_h.rearrange("(i p) d -> p i d", p=128)[:, 0:8, :],
                                 in_=self.x1a[:, :, :]), 16)
            s.wait_ge(s_dv.h, appn[15])
            s_st.inc(s.dma_start(out=self.out_h.rearrange("(i p) d -> p i d", p=128)[:, 8:16, :],
                                 in_=self.x1b[:, :, :]), 16)
            s.wait_ge(s_st.h, s_st.n)
        self.store_done = ("st", s_st.n)

    def finalize_dbg(self):
        blk, s_st = self.blk, self.s_st
        shapes = {"q": ([D, T], F16), "k": ([D, T], F16), "lat": ([512, T], F16),
                  "v": ([T, D], BF16), "xnt": ([D, T], F16),
                  "x1": ([128, NT * D], F32), "losses": ([1, 8], F32),
                  "onehot": ([8, 1], F16)}
        for nm in self.dbg:
            if nm not in self.dbg_h:
                s, dt = shapes[nm]
                self.dbg_h[nm] = self.nc.dram_tensor("dbg_" + nm, s, dt, kind="ExternalOutput").ap()
        waits = [(sm.h, sm.n) for sm in
                 [self.s_pe, self.s_dv, self.s_ac, self.s_ld0, self.s_ld1,
                  self.s_wt, self.s_wq0, self.s_wq1, self.s_wo, self.s_wf,
                  self.s_we0, self.s_we1, self.s_wb0, self.s_wb1, self.s_wa]
                 if sm.n > 0]
        @blk.sync
        def _(s):
            for h, n in waits:
                s.wait_ge(h, n)
            for name, hap in self.dbg_h.items():
                if name in ("q", "k", "lat", "xnt"):
                    src = {"q": self.q0, "k": self.k0, "lat": self.latT, "xnt": self.xnT}[name]
                    s_st.inc(s.dma_start(out=hap.rearrange("(c p) t -> p c t", p=128),
                                         in_=src), 16)
                elif name == "v":
                    s_st.inc(s.dma_start(out=hap.rearrange("(i p) x -> p i x", p=128),
                                         in_=self.vstore), 16)
                elif name == "x1":
                    s_st.inc(s.dma_start(out=hap[:, 0:NT*D//2],
                                         in_=self.x1a.rearrange("p i d -> p (i d)")), 16)
                    s_st.inc(s.dma_start(out=hap[:, NT*D//2:],
                                         in_=self.x1b.rearrange("p i d -> p (i d)")), 16)
                elif name == "atok":
                    s_st.inc(s.dma_start(out=hap, in_=self.ar1[:, :]), 16)
                elif name == "losses":
                    s_st.inc(s.dma_start(out=hap, in_=self.lfin[:]), 16)
                elif name == "onehot":
                    s_st.inc(s.dma_start(out=hap, in_=self.onec[:]), 16)
            s.wait_ge(s_st.h, self.s_st.n)


def host_constants():
    c = {}
    c["ident16"] = np.eye(128, dtype=np.float16)
    sw = np.zeros((128, 128), dtype=np.float16)
    for k in range(128):
        sw[k, k ^ 1] = 1.0
    c["swap16"] = sw
    inv = 1.0 / 10000.0 ** (np.arange(0, HD, 2, dtype=np.float32) / HD)
    ang = np.outer(np.arange(T, dtype=np.float32), inv)
    cos = np.cos(ang).astype(np.float32)
    sin = np.sin(ang).astype(np.float32)
    A = np.zeros((128, T), dtype=np.float32)
    B = np.zeros((128, T), dtype=np.float32)
    for p in range(128):
        i = (p % HD) // 2
        A[p, :] = cos[:, i]
        B[p, :] = (-sin[:, i]) if (p % 2 == 0) else sin[:, i]
    c["ropeA"] = A.astype(np.float16)
    c["ropeB"] = B.astype(np.float16)
    c["iota8"] = np.arange(8, dtype=np.float32).reshape(1, 8)
    E = np.zeros((8, 512), dtype=np.float16)
    for e in range(8):
        E[e, 64*e:64*(e+1)] = 1.0
    c["Eexp"] = E
    Es = np.zeros((16, 16 * 128), dtype=np.float16)
    for e in range(16):
        Es[e, 128*e:128*(e+1)] = 1.0
    c["eselk16"] = Es
    return c


def l0_inmap(x_batch, ins, consts):
    return {
        "x": x_batch,
        "wq": ins["L0_attn_wq"], "w_down": ins["L0_attn_w_down"],
        "wk_up": ins["L0_attn_wk_up"], "wv_up": ins["L0_attn_wv_up"],
        "wo": ins["L0_attn_wo"],
        "ident16": consts["ident16"], "swap16": consts["swap16"],
        "ropeA": consts["ropeA"], "ropeB": consts["ropeB"],
        "W_base": ins["L0_ffn_W_base"],
        "b_basec": np.ascontiguousarray(ins["L0_ffn_b_base"].reshape(8, 128).T),
        "b_baser": ins["L0_ffn_b_base"].reshape(1, 1024),
        "Wd1": ins["L0_ffn_Wd1"],
        "bd1f": np.ascontiguousarray(ins["L0_ffn_bd1"].reshape(1, 2048)),
        "Wd2T": np.ascontiguousarray(ins["L0_ffn_Wd2"].transpose(0, 2, 1)),
        "bd2c": np.ascontiguousarray(ins["L0_ffn_bd2"].reshape(8, 8, 128).transpose(2, 0, 1)),
        "Wad_stack": np.ascontiguousarray(ins["L0_ffn_Wa_down"].transpose(1, 0, 2).reshape(1024, 512)),
        "bad_stack": np.ascontiguousarray(ins["L0_ffn_ba_down"].reshape(4, 128).T),
        "Wau_stack": np.ascontiguousarray(ins["L0_ffn_Wa_up"].reshape(512, 1024)),
        "ba_up": ins["L0_ffn_ba_up"],
        "iota8": consts["iota8"], "Eexp": consts["Eexp"],
    }


class L2B:

    def __init__(self, dbg=(), upto="all"):
        self.dbg = set(dbg)
        self.upto = upto
        self.nc = nc = bass.Bass("TRN2", target_bir_lowering=False)
        self.es = es = ExitStack()
        self.dbg_h = {}

        dram = lambda n, s, dt, k: nc.dram_tensor(n, s, dt, kind=k).ap()
        self.x_h = dram("xB", [T, D], F32, "ExternalInput")
        self.wqkv_h = dram("WqkvT", [D, 3 * D], F32, "ExternalInput")
        self.wout_h = dram("WoutT", [D, D], F32, "ExternalInput")
        self.selw_h = dram("selWT", [D, 16], F32, "ExternalInput")
        self.keys_h = dram("keysT", [D, 2 * D], F32, "ExternalInput")
        self.vals_h = dram("valsS", [2 * D, D], F32, "ExternalInput")
        self.idt_h = dram("ident16", [128, 128], F16, "ExternalInput")
        self.eselk_h = dram("eselk16", [16, 16 * 128], F16, "ExternalInput")
        self.out_h = dram("y", [T, D], F32, "ExternalOutput")

        sb = lambda *a: es.enter_context(nc.sbuf_tensor(*a))
        pst = lambda *a: es.enter_context(nc.psum_tensor(*a))

        self.idt16 = sb("idt16", [128, 128], F16)
        self.eps_t = sb("eps_t", [128, 1], F32)
        self.c30 = sb("c30", [128, 1], F32)
        self.zero1 = sb("zero1", [128, 1], F32)
        self.xst = sb("xst", [128, 2, D], F32)
        self.xnst = sb("xnst", [128, 2, D], F16)
        self.sums = sb("sums", [128, NT], F32)
        self.sumsq = sb("sumsq", [128, NT], F32)
        self.means = sb("means", [128, NT], F32)
        self.rstds = sb("rstds", [128, NT], F32)
        self.vart = sb("vart", [128, NT], F32)
        self.lnbias = sb("lnbias", [128, NT], F32)
        self.sqscr = sb("sqscr", [128, D], F16)

        self.scr = sb("scr", [128, 5120], F32)
        scr16 = self.scr.bitcast(F16)
        self.Sc = self.scr[:, 0:2048].rearrange("p (a x) -> p a x", a=2)
        self.sumE = self.scr[:, 2048:2304].rearrange("p (a x) -> p a x", a=2)
        self.recE = self.scr[:, 2304:2560].rearrange("p (a x) -> p a x", a=2)
        self.P = scr16[:, 5120:7168].rearrange("p (a x) -> p a x", a=2)
        self.prod = scr16[:, 7168:8192]
        self.page = scr16[:, 0:4096].rearrange("p (c w) -> p c w", w=512)
        self.tokst = scr16[:, 4096:6144].rearrange("p (a x) -> p a x", a=2)


        self.wqkv = sb("wqkv", [128, 2, ND, 512], F16)
        self.wout = sb("wout", [128, 2, ND, 128], F16)
        self.selw = sb("selw", [128, ND, 16], F16)
        self.keysw = sb("keysw", [128, 2, ND, 128], F16)
        self.valsw = sb("valsw", [128, 2, 16, 128], F16)
        self.selfm = sb("selfm", [16, T], F16)
        self.eselk = sb("eselk", [16, 16 * 128], F16)

        self.ar1 = sb("l2ar1", [128, 16384], F16)
        self.ar2 = sb("l2ar2", [128, 16384], F16)
        self.ar3 = sb("l2ar3", [128, 16384], F16)
        self.ar4 = sb("l2ar4", [128, 16384], F16)
        self.xn5T = self.ar1.rearrange("p (c t) -> p c t", t=T)
        self.attnT = self.ar1.rearrange("p (c t) -> p c t", t=T)
        self.h1 = self.ar1.rearrange("p (c w) -> p c w", w=1024)
        self.q_tok = self.ar2.rearrange("p (i d) -> p i d", d=D)
        self.acc = self.ar2.rearrange("p (i d) -> p i d", d=D)
        self.att2_tok = self.ar2.rearrange("p (i d) -> p i d", d=D)
        self.xn7T = self.ar2.rearrange("p (c t) -> p c t", t=T)
        self.k_tok = self.ar3.rearrange("p (i d) -> p i d", d=D)
        self.x3a = self.ar3.bitcast(F32).rearrange("p (i d) -> p i d", d=D)
        self.v_tok = self.ar4.rearrange("p (i d) -> p i d", d=D)
        self.att2fm = self.ar4.rearrange("p (c t) -> p c t", t=T)
        self.x3b = self.ar4.bitcast(F32).rearrange("p (i d) -> p i d", d=D)
        self.ffn2_d = nc.dram_tensor("ffn2scr", [T, D], F16, kind="Internal").ap()

        self.psA = pst("psA", [128, 512], F32)
        self.psB = pst("psB", [128, 512], F32)
        self.psT1 = pst("psT1", [128, 128], F16)
        self.psT2 = pst("psT2", [128, 128], F16)
        self.psS = pst("psS", [16, 512], F32)
        self.psBC1 = pst("psBC1", [128, 512], F32)
        self.psBC2 = pst("psBC2", [128, 512], F32)

        self.s_ld = Sem(es, nc, "s_ld")
        self.s_ld0 = Sem(es, nc, "s_ld0")
        self.s_ld1 = Sem(es, nc, "s_ld1")
        self.s_w0 = Sem(es, nc, "s_w0")
        self.s_w1 = Sem(es, nc, "s_w1")
        self.s_w2 = Sem(es, nc, "s_w2")
        self.s_pe = Sem(es, nc, "s_pe")
        self.s_dv = Sem(es, nc, "s_dv")
        self.s_ac = Sem(es, nc, "s_ac")
        self.s_st = Sem(es, nc, "s_st")
        self.s_st0 = Sem(es, nc, "s_st0")
        self.s_st1 = Sem(es, nc, "s_st1")
        self.sem = {"pe": self.s_pe, "dv": self.s_dv, "ac": self.s_ac,
                    "ld": self.s_ld, "st": self.s_st, "ld0": self.s_ld0,
                    "ld1": self.s_ld1, "w0": self.s_w0, "w1": self.s_w1, "w2": self.s_w2,
                    "st0": self.s_st0, "st1": self.s_st1}
        self.bank_free = {}
        self.blk = es.enter_context(nc.Block())
        self.build()

    def wait(self, eng, edge):
        if edge is None:
            return
        sname, cnt = edge
        if cnt > 0:
            eng.wait_ge(self.sem[sname].h, cnt)

    def bank_wait(self, eng, bank):
        self.wait(eng, self.bank_free.get(bank))

    def build(self):
        blk = self.blk
        s_ld = self.s_ld
        @blk.sync
        def _(s):
            s_ld.inc(s.dma_start(out=self.idt16[:], in_=self.idt_h), 16)
            s_ld.inc(s.dma_start(out=self.eselk[:], in_=self.eselk_h), 16)
        self.ld_const = ("ld", s_ld.n)
        @blk.vector
        def _(v):
            v = EW(v, self.s_dv)
            v.memset(self.eps_t[:], EPS)
            v.memset(self.zero1[:], 0.0)
            v.memset(self.c30[:], -EXPC)
        self.init_dv = ("dv", self.s_dv.n)

        self.ph_ln5()
        self.ph_qkv()
        if "q2" in self.dbg:
            self.dbg_h["q2"] = self.nc.dram_tensor("dbg_q2", [128, NT * D], F16, kind="ExternalOutput").ap()
        if "v2" in self.dbg:
            self.dbg_h["v2"] = self.nc.dram_tensor("dbg_v2", [128, NT * D], F16, kind="ExternalOutput").ap()
        if self.upto == "qkv":
            self.finalize_dbg()
            return
        self.ph_scores()
        if "P2" in self.dbg:
            self.dbg_h["P2"] = self.nc.dram_tensor("dbg_P2", [128, 2048], F16, kind="ExternalOutput").ap()
        if self.upto == "scores":
            self.finalize_dbg()
            return
        self.ph_attnv()
        if "acc2" in self.dbg:
            self.dbg_h["acc2"] = self.nc.dram_tensor("dbg_acc2", [128, NT * D], F16, kind="ExternalOutput").ap()
        if self.upto == "attnv":
            self.finalize_dbg()
            return
        self.ph_outproj_ln6()
        if "x3" in self.dbg:
            self.dbg_h["x3"] = self.nc.dram_tensor("dbg_x3", [128, NT * D], F32, kind="ExternalOutput").ap()
        if self.upto == "x3":
            self.finalize_dbg()
            return
        self.ph_ffn2()
        if self.upto != "all":
            self.finalize_dbg()

    def ph_ln5(self):
        blk = self.blk
        s_dv, s_ac, s_pe = self.s_dv, self.s_ac, self.s_pe
        src_t = self.x_h.rearrange("(n p) d -> n p d", p=128)
        lds = [self.s_ld0, self.s_ld1]
        B0, B1 = lds[0].n, lds[1].n
        V0, A0 = s_dv.n, s_ac.n
        ldn = [[B0, B1][i % 2] + 16 * (i // 2 + 1) for i in range(NT)]
        vn = [V0 + 1 + (i + 1) for i in range(NT)]
        an = [A0 + (i + 1) for i in range(NT)]
        @blk.sync
        def _(s):
            for i in range(NT):
                if i >= 2:
                    s.wait_ge(s_dv.h, vn[i-2])
                    s.wait_ge(s_ac.h, an[i-2])
                lds[i % 2].inc(s.dma_start(out=self.xst[:, i % 2, :], in_=src_t[i]), 16)
        @blk.vector
        def _(v):
            v = EW(v, self.s_dv)
            self.wait(v, self.init_dv)
            v.memset(self.sumsq[:], 0.0)
            for i in range(NT):
                v.wait_ge(lds[i % 2].h, ldn[i])
                v.tensor_reduce(self.sums[:, i:i+1], self.xst[:, i % 2, :], AX.X, ALU.add)
        @blk.scalar
        def _(a):
            a = EW(a, self.s_ac)
            a.wait_ge(s_dv.h, V0 + 1)
            for i in range(NT):
                a.wait_ge(lds[i % 2].h, ldn[i])
                a.activation(self.sqscr[:], self.xst[:, i % 2, :], AF.Square,
                             bias=self.zero1[:], scale=1.0, accum_out=self.sumsq[:, i:i+1])
        rdy = self.ln_combine(vn[-1], an[-1])
        C0, C1 = lds[0].n, lds[1].n
        V1, P0 = s_dv.n, s_pe.n
        ldn2 = [[C0, C1][i % 2] + 16 * (i // 2 + 1) for i in range(NT)]
        per = 1 + ND
        vxn = [V1 + i * per + 1 for i in range(NT)]
        vev = [[V1 + i * per + 1 + (c + 1) for c in range(ND)] for i in range(NT)]
        trn = [[P0 + i * ND + c + 1 for c in range(ND)] for i in range(NT)]
        @blk.sync
        def _(s):
            s.wait_ge(s_dv.h, vn[NT-1])
            s.wait_ge(s_ac.h, an[NT-1])
            for i in range(NT):
                if i >= 2:
                    s.wait_ge(s_dv.h, vxn[i-2])
                lds[i % 2].inc(s.dma_start(out=self.xst[:, i % 2, :], in_=src_t[i]), 16)
        @blk.tensor
        def _(t):
            self.wait(t, self.ld_const)
            for i in range(NT):
                t.wait_ge(s_dv.h, vxn[i])
                for c in range(ND):
                    if i * ND + c >= 2:
                        pi, pc = divmod(i * ND + c - 2, ND)
                        t.wait_ge(s_dv.h, vev[pi][pc])
                    pb = self.psT1 if c % 2 == 0 else self.psT2
                    self.s_pe.inc(t.transpose(pb[:], self.xnst[:, i % 2, 128*c:128*(c+1)], self.idt16[:]))
        @blk.vector
        def _(v):
            v = EW(v, self.s_dv)
            v.wait_ge(s_dv.h, rdy)
            for i in range(NT):
                v.wait_ge(lds[i % 2].h, ldn2[i])
                if i >= 2:
                    v.wait_ge(s_pe.h, trn[i-2][-1])
                v.tensor_scalar(self.xnst[:, i % 2, :], self.xst[:, i % 2, :],
                                self.means[:, i:i+1], self.rstds[:, i:i+1],
                                ALU.subtract, ALU.mult)
                for c in range(ND):
                    v.wait_ge(s_pe.h, trn[i][c])
                    pb = self.psT1 if c % 2 == 0 else self.psT2
                    v.tensor_copy(self.xn5T[:, c, 128*i:128*(i+1)], pb[:])
        assert s_dv.n == vev[-1][-1]
        self.bank_free["T1"] = ("dv", vev[-1][-2])
        self.bank_free["T2"] = ("dv", vev[-1][-1])
        self.xst_free = ("dv", vxn[-1])
        self.ln5_done = ("dv", s_dv.n)

    def ln_combine(self, v_edge, a_edge):
        blk = self.blk
        s_dv, s_ac = self.s_dv, self.s_ac
        @blk.vector
        def _(v):
            v = EW(v, self.s_dv)
            if v_edge: v.wait_ge(s_dv.h, v_edge)
            if a_edge: v.wait_ge(s_ac.h, a_edge)
            v.tensor_scalar(self.means[:], self.sums[:], 1.0 / D, None, ALU.mult)
            v.tensor_scalar(self.vart[:], self.sumsq[:], 1.0 / D, None, ALU.mult)
            v.tensor_tensor(self.rstds[:], self.means[:], self.means[:], ALU.mult)
            v.tensor_tensor(self.vart[:], self.vart[:], self.rstds[:], ALU.subtract)
        nvar = s_dv.n
        @blk.scalar
        def _(a):
            a = EW(a, self.s_ac)
            a.wait_ge(s_dv.h, nvar)
            a.activation(self.vart[:], self.vart[:], AF.Sqrt, bias=self.eps_t[:], scale=1.0)
        nstd = s_ac.n
        @blk.vector
        def _(v):
            v = EW(v, self.s_dv)
            v.wait_ge(s_ac.h, nstd)
            v.reciprocal(self.rstds[:], self.vart[:])
            v.tensor_tensor(self.lnbias[:], self.means[:], self.rstds[:], ALU.mult)
            v.tensor_scalar(self.lnbias[:], self.lnbias[:], -1.0, None, ALU.mult)
        return s_dv.n

    def ph_qkv(self):
        blk = self.blk
        s_pe, s_ac = self.s_pe, self.s_ac
        ws = [self.s_w0, self.s_w1]
        W0 = [ws[0].n, ws[1].n]
        wld = [W0[nc_ % 2] + 16 * ND * (nc_ // 2 + 1) for nc_ in range(6)]
        P0, A0 = s_pe.n, s_ac.n
        mm_n = {}
        ev_n = {}
        g = 0
        for nc_ in range(6):
            for i in range(NT):
                mm_n[(nc_, i)] = P0 + g + 1
                ev_n[(nc_, i)] = A0 + g + 1
                g += 1
        @blk.gpsimd
        def _(g):
            for nc_ in range(6):
                if nc_ >= 2:
                    g.wait_ge(s_pe.h, mm_n[(nc_ - 2, NT - 1)])
                for kc in range(ND):
                    ws[nc_ % 2].inc(g.dma_start(out=self.wqkv[:, nc_ % 2, kc, :],
                                                in_=self.wqkv_h[128*kc:128*(kc+1), 512*nc_:512*(nc_+1)]), 16)
        @blk.tensor
        def _(t):
            self.wait(t, self.ln5_done)
            for nc_ in range(6):
                t.wait_ge(ws[nc_ % 2].h, wld[nc_])
                for i in range(NT):
                    pb = self.psA if (nc_ * NT + i) % 2 == 0 else self.psB
                    gi = nc_ * NT + i
                    if gi >= 2:
                        pnc, pi = divmod(gi - 2, NT)
                        t.wait_ge(s_ac.h, ev_n[(pnc, pi)])
                    else:
                        self.bank_wait(t, "A" if gi % 2 == 0 else "B")
                    ins = None
                    for kc in range(ND):
                        ins = t.matmul(pb[:], self.xn5T[:, kc, 128*i:128*(i+1)],
                                       self.wqkv[:, nc_ % 2, kc, :],
                                       start=(kc == 0), stop=(kc == ND - 1))
                    s_pe.inc(ins)
        @blk.scalar
        def _(a):
            a = EW(a, self.s_ac)
            for nc_ in range(6):
                for i in range(NT):
                    gi = nc_ * NT + i
                    pb = self.psA if gi % 2 == 0 else self.psB
                    a.wait_ge(s_pe.h, mm_n[(nc_, i)])
                    dst = [self.q_tok, self.k_tok, self.v_tok][nc_ // 2]
                    half = nc_ % 2
                    a.activation(dst[:, i, 512*half:512*(half+1)], pb[:], AF.Copy,
                                 bias=0.0, scale=(0.125 if nc_ < 2 else 1.0))
        self.bank_free["A"] = ("ac", ev_n[(5, NT-2)])
        self.bank_free["B"] = ("ac", ev_n[(5, NT-1)])
        self.qkv_done = ("ac", s_ac.n)

    def ph_scores(self):
        blk = self.blk
        s_dv, s_ac = self.s_dv, self.s_ac
        V0, A0 = s_dv.n, s_ac.n
        @blk.vector
        def _(v):
            v = EW(v, self.s_dv)
            self.wait(v, self.qkv_done)
            for s_ in range(8):
                for t_ in range(8):
                    for half in range(2):
                        v.tensor_tensor(self.prod[:, :], self.q_tok[:, 2*s_+half, :],
                                        self.k_tok[:, 2*t_+half, :], ALU.mult)
                        v.tensor_reduce(
                            self.Sc[:, half, :].rearrange("p (h s t) -> p h s t", s=8, t=8)[:, :, s_, t_].unsqueeze(2),
                            self.prod.rearrange("p (h d) -> p h d", d=HD), AX.X, ALU.add)
        nsc = s_dv.n
        @blk.scalar
        def _(a):
            a = EW(a, self.s_ac)
            a.wait_ge(s_dv.h, nsc)
            for half in range(2):
                a.activation(self.Sc[:, half, :], self.Sc[:, half, :], AF.Exp,
                             bias=self.c30[:], scale=1.0)
        nexp = s_ac.n
        @blk.vector
        def _(v):
            v = EW(v, self.s_dv)
            v.wait_ge(s_ac.h, nexp)
            for half in range(2):
                v.tensor_reduce(self.sumE[:, half, :].unsqueeze(2),
                                self.Sc[:, half, :].rearrange("p (hs t) -> p hs t", t=8),
                                AX.X, ALU.add)
                v.reciprocal(self.recE[:, half, :], self.sumE[:, half, :])
                v.tensor_tensor(self.P[:, half, :].rearrange("p (a b) -> p a b", b=8),
                                self.Sc[:, half, :].rearrange("p (a b) -> p a b", b=8),
                                self.recE[:, half, :].unsqueeze(2).broadcast_to((128, 128, 8)),
                                ALU.mult)
        self.scores_done = ("dv", s_dv.n)

    def ph_attnv(self):
        blk = self.blk
        s_dv = self.s_dv
        @blk.vector
        def _(v):
            v = EW(v, self.s_dv)
            self.wait(v, self.scores_done)
            for s_ in range(8):
                for half in range(2):
                    i = 2 * s_ + half
                    Pv = self.P[:, half, :].rearrange("p (h s t) -> p h s t", s=8, t=8)
                    for t_ in range(8):
                        psl = Pv[:, :, s_, t_].unsqueeze(2).broadcast_to((128, NH, HD))
                        r3 = lambda ap: ap.rearrange("p (a b) -> p a b", b=HD)
                        if t_ == 0:
                            v.tensor_tensor(r3(self.acc[:, i, :]), r3(self.v_tok[:, 2*t_+half, :]), psl, ALU.mult)
                        else:
                            v.tensor_tensor(r3(self.prod[:, :]), r3(self.v_tok[:, 2*t_+half, :]), psl, ALU.mult)
                            v.tensor_tensor(self.acc[:, i, :], self.acc[:, i, :], self.prod[:, :], ALU.add)
        self.attnv_done = ("dv", s_dv.n)

    def ph_outproj_ln6(self):
        blk = self.blk
        s_pe, s_dv, s_ac = self.s_pe, self.s_dv, self.s_ac
        P0, V0 = s_pe.n, s_dv.n
        trn = [[P0 + i * ND + c + 1 for c in range(ND)] for i in range(NT)]
        vev = [[V0 + i * ND + c + 1 for c in range(ND)] for i in range(NT)]
        @blk.tensor
        def _(t):
            self.wait(t, self.attnv_done)
            self.bank_wait(t, "T1")
            self.bank_wait(t, "T2")
            for i in range(NT):
                for c in range(ND):
                    if i * ND + c >= 2:
                        pi, pc = divmod(i * ND + c - 2, ND)
                        t.wait_ge(s_dv.h, vev[pi][pc])
                    pb = self.psT1 if c % 2 == 0 else self.psT2
                    s_pe.inc(t.transpose(pb[:], self.acc[:, i, 128*c:128*(c+1)], self.idt16[:]))
        @blk.vector
        def _(v):
            v = EW(v, self.s_dv)
            for i in range(NT):
                for c in range(ND):
                    v.wait_ge(s_pe.h, trn[i][c])
                    pb = self.psT1 if c % 2 == 0 else self.psT2
                    v.tensor_copy(self.attnT[:, c, 128*i:128*(i+1)], pb[:])
        assert s_dv.n == vev[-1][-1]
        self.bank_free["T1"] = ("dv", vev[-1][-2])
        self.bank_free["T2"] = ("dv", vev[-1][-1])
        attnT_done = ("dv", s_dv.n)
        ws = [self.s_w0, self.s_w1]
        WB = [ws[0].n, ws[1].n]
        wldo = [WB[m % 2] + 16 * ND * (m // 2 + 1) for m in range(ND)]
        P1, A1, V1 = s_pe.n, s_ac.n, s_dv.n
        @blk.gpsimd
        def _(g):
            self.wait(g, self.qkv_done)
            for m in range(ND):
                if m >= 2:
                    g.wait_ge(s_pe.h, P1 + (m - 1) * TC)
                for kc in range(ND):
                    ws[m % 2].inc(g.dma_start(out=self.wout[:, m % 2, kc, :],
                                              in_=self.wout_h[128*kc:128*(kc+1), 128*m:128*(m+1)]), 16)
        att2fm = self.att2fm
        mm_n = {}
        ev_n = {}
        gidx = 0
        for m in range(ND):
            for tc in range(TC):
                mm_n[(m, tc)] = P1 + gidx + 1
                ev_n[(m, tc)] = A1 + gidx + 1
                gidx += 1
        @blk.tensor
        def _(t):
            self.wait(t, attnT_done)
            for m in range(ND):
                t.wait_ge(ws[m % 2].h, wldo[m])
                for tc in range(TC):
                    gi = m * TC + tc
                    pb = self.psA if gi % 2 == 0 else self.psB
                    if gi >= 2:
                        pm, ptc = divmod(gi - 2, TC)
                        t.wait_ge(s_ac.h, ev_n[(pm, ptc)])
                    else:
                        self.bank_wait(t, "A" if gi % 2 == 0 else "B")
                    ins = None
                    for kc in range(ND):
                        ins = t.matmul(pb[:], self.wout[:, m % 2, kc, :],
                                       self.attnT[:, kc, 512*tc:512*(tc+1)],
                                       start=(kc == 0), stop=(kc == ND - 1))
                    s_pe.inc(ins)
        @blk.scalar
        def _(a):
            a = EW(a, self.s_ac)
            for m in range(ND):
                for tc in range(TC):
                    gi = m * TC + tc
                    pb = self.psA if gi % 2 == 0 else self.psB
                    a.wait_ge(s_pe.h, mm_n[(m, tc)])
                    a.activation(att2fm[:, m, 512*tc:512*(tc+1)], pb[:], AF.Copy,
                                 bias=0.0, scale=1.0)
        self.bank_free["A"] = ("ac", ev_n[(ND-1, TC-2)])
        self.bank_free["B"] = ("ac", ev_n[(ND-1, TC-1)])
        self.wout_consumed = mm_n[(ND-1, TC-1)]
        att2_done = ("ac", s_ac.n)
        att2tok = self.att2_tok
        P2, V2, A2 = s_pe.n, s_dv.n, s_ac.n
        trn2 = [[P2 + i * ND + c + 1 for c in range(ND)] for i in range(NT)]
        per = ND + 1
        vev2 = [[V2 + i * per + c + 1 for c in range(ND)] for i in range(NT)]
        vred2 = [V2 + i * per + ND + 1 for i in range(NT)]
        @blk.tensor
        def _(t):
            self.wait(t, att2_done)
            self.bank_wait(t, "T1")
            self.bank_wait(t, "T2")
            for i in range(NT):
                for c in range(ND):
                    if i * ND + c >= 2:
                        pi, pc = divmod(i * ND + c - 2, ND)
                        t.wait_ge(s_dv.h, vev2[pi][pc])
                    pb = self.psT1 if c % 2 == 0 else self.psT2
                    s_pe.inc(t.transpose(pb[:], att2fm[:, c, 128*i:128*(i+1)], self.idt16[:]))
        @blk.vector
        def _(v):
            v = EW(v, self.s_dv)
            for i in range(NT):
                for c in range(ND):
                    v.wait_ge(s_pe.h, trn2[i][c])
                    pb = self.psT1 if c % 2 == 0 else self.psT2
                    v.tensor_copy(att2tok[:, i, 128*c:128*(c+1)], pb[:])
                v.tensor_reduce(self.sums[:, i:i+1], att2tok[:, i, :], AX.X, ALU.add)
        assert s_dv.n == vred2[-1]
        zz0 = s_dv.n + 1
        @blk.vector
        def _(v):
            v = EW(v, self.s_dv)
            v.memset(self.sumsq[:], 0.0)
        @blk.scalar
        def _(a):
            a = EW(a, self.s_ac)
            a.wait_ge(s_dv.h, zz0)
            for i in range(NT):
                a.wait_ge(s_dv.h, vred2[i])
                a.activation(self.sqscr[:], att2tok[:, i, :], AF.Square,
                             bias=self.zero1[:], scale=1.0, accum_out=self.sumsq[:, i:i+1])
        self.bank_free["T1"] = ("dv", vev2[-1][-2])
        self.bank_free["T2"] = ("dv", vev2[-1][-1])
        rdy = self.ln_combine(s_dv.n, s_ac.n)
        lds = [self.s_ld0, self.s_ld1]
        B0, B1 = lds[0].n, lds[1].n
        V3 = s_dv.n
        ldn = [[B0, B1][i % 2] + 16 * (i // 2 + 1) for i in range(NT)]
        src_t = self.x_h.rearrange("(n p) d -> n p d", p=128)
        x3t = lambda i: (self.x3a[:, i, :] if i < 8 else self.x3b[:, i - 8, :])
        @blk.sync
        def _(s):
            self.wait(s, self.xst_free)
            for i in range(NT):
                if i >= 2:
                    s.wait_ge(s_dv.h, V3 + 2 * (i - 1))
                lds[i % 2].inc(s.dma_start(out=self.xst[:, i % 2, :], in_=src_t[i]), 16)
        @blk.vector
        def _(v):
            v = EW(v, self.s_dv)
            v.wait_ge(s_dv.h, rdy)
            for i in range(NT):
                v.wait_ge(lds[i % 2].h, ldn[i])
                v.tensor_scalar(x3t(i), att2tok[:, i, :],
                                self.rstds[:, i:i+1], self.lnbias[:, i:i+1],
                                ALU.mult, ALU.add)
                v.tensor_tensor(x3t(i), x3t(i), self.xst[:, i % 2, :], ALU.add)
        self.x3_done = ("dv", s_dv.n)
        self.xst_free = self.x3_done

    def ph_ffn2(self):
        blk = self.blk
        s_pe, s_dv, s_ac = self.s_pe, self.s_dv, self.s_ac
        x3t = lambda i: (self.x3a[:, i, :] if i < 8 else self.x3b[:, i - 8, :])
        V0, A0 = s_dv.n, s_ac.n
        @blk.vector
        def _(v):
            v = EW(v, self.s_dv)
            self.wait(v, self.x3_done)
            v.memset(self.sumsq[:], 0.0)
            for i in range(NT):
                v.tensor_reduce(self.sums[:, i:i+1], x3t(i), AX.X, ALU.add)
        @blk.scalar
        def _(a):
            a = EW(a, self.s_ac)
            a.wait_ge(s_dv.h, V0 + 1)
            for i in range(NT):
                a.wait_ge(s_dv.h, V0 + 1 + (i + 1))
                a.activation(self.sqscr[:], x3t(i), AF.Square,
                             bias=self.zero1[:], scale=1.0, accum_out=self.sumsq[:, i:i+1])
        rdy = self.ln_combine(V0 + 1 + NT, A0 + NT)
        V1, P0 = s_dv.n, s_pe.n
        per = 1 + ND
        vxn = [V1 + i * per + 1 for i in range(NT)]
        vev = [[V1 + i * per + 1 + (c + 1) for c in range(ND)] for i in range(NT)]
        trn = [[P0 + i * ND + c + 1 for c in range(ND)] for i in range(NT)]
        @blk.tensor
        def _(t):
            self.bank_wait(t, "T1")
            self.bank_wait(t, "T2")
            for i in range(NT):
                t.wait_ge(s_dv.h, vxn[i])
                for c in range(ND):
                    if i * ND + c >= 2:
                        pi, pc = divmod(i * ND + c - 2, ND)
                        t.wait_ge(s_dv.h, vev[pi][pc])
                    pb = self.psT1 if c % 2 == 0 else self.psT2
                    s_pe.inc(t.transpose(pb[:], self.xnst[:, i % 2, 128*c:128*(c+1)], self.idt16[:]))
        @blk.vector
        def _(v):
            v = EW(v, self.s_dv)
            v.wait_ge(s_dv.h, rdy)
            for i in range(NT):
                if i >= 2:
                    v.wait_ge(s_pe.h, trn[i-2][-1])
                v.tensor_scalar(self.xnst[:, i % 2, :], x3t(i),
                                self.means[:, i:i+1], self.rstds[:, i:i+1],
                                ALU.subtract, ALU.mult)
                for c in range(ND):
                    v.wait_ge(s_pe.h, trn[i][c])
                    pb = self.psT1 if c % 2 == 0 else self.psT2
                    v.tensor_copy(self.xn7T[:, c, 128*i:128*(i+1)], pb[:])
        assert s_dv.n == vev[-1][-1]
        self.bank_free["T1"] = ("dv", vev[-1][-2])
        self.bank_free["T2"] = ("dv", vev[-1][-1])
        ln7_done = ("dv", s_dv.n)

        @blk.gpsimd
        def _(g):
            self.wait(g, self.ln5_done)
            for kc in range(ND):
                self.s_w2.inc(g.dma_start(out=self.selw[:, kc, :],
                                          in_=self.selw_h[128*kc:128*(kc+1), :]), 16)
        selw_ld = ("w2", self.s_w2.n)
        sel_ev = []
        for tc in range(TC):
            P1 = s_pe.n
            @blk.tensor
            def _(t):
                self.wait(t, ln7_done)
                if tc == 0:
                    self.wait(t, selw_ld)
                    self.bank_wait(t, "S")
                else:
                    t.wait_ge(s_ac.h, sel_ev[-1])
                ins = None
                for kc in range(ND):
                    ins = t.matmul(self.psS[:, :], self.selw[:, kc, :],
                                   self.xn7T[:, kc, 512*tc:512*(tc+1)],
                                   start=(kc == 0), stop=(kc == ND - 1))
                s_pe.inc(ins)
            nmm = s_pe.n
            @blk.scalar
            def _(a):
                a = EW(a, self.s_ac)
                a.wait_ge(s_pe.h, nmm)
                a.activation(self.selfm[0:16, 512*tc:512*(tc+1)], self.psS[:, :], AF.Sigmoid,
                             bias=self.zero1[0:16], scale=1.0)
            sel_ev.append(s_ac.n)
        self.bank_free["S"] = ("ac", s_ac.n)
        sel_done = ("ac", s_ac.n)

        ws = [self.s_w0, self.s_w1]
        ffn2_t = self.ffn2_d.rearrange("(n p) d -> n p d", p=128)
        stat_edges = []
        h1_free = None
        pg_free = None
        @blk.vector
        def _(v):
            v = EW(v, self.s_dv)
            v.memset(self.sumsq[:], 0.0)
        zst = s_dv.n
        for th in range(2):
            kld = []
            kmm = []
            kev = []
            for m in range(NH):
                @blk.gpsimd
                def _(g):
                    if th == 0 and m == 0:
                        self.wait(g, ("pe", self.wout_consumed))
                    if len(kmm) >= 2:
                        g.wait_ge(s_pe.h, kmm[-2])
                    if th > 0 and m < 2:
                        g.wait_ge(s_pe.h, self.vmm_prev[m - 2])
                    for kc in range(ND):
                        ws[m % 2].inc(g.dma_start(out=self.keysw[:, m % 2, kc, :],
                                                  in_=self.keys_h[128*kc:128*(kc+1), 128*m:128*(m+1)]), 16)
                kld.append(ws[m % 2].n)
                @blk.tensor
                def _(t):
                    if th == 0 and m == 0:
                        self.wait(t, h1_free)
                    t.wait_ge(ws[m % 2].h, kld[m])
                    for tcl in range(2):
                        gi = m * 2 + tcl
                        pb = self.psA if gi % 2 == 0 else self.psB
                        if len(kev) >= 2:
                            t.wait_ge(s_ac.h, kev[gi - 2] if gi - 2 < len(kev) else kev[-1])
                        elif th == 0:
                            self.bank_wait(t, "A" if gi % 2 == 0 else "B")
                        else:
                            t.wait_ge(s_ac.h, self.vev_prev[gi - 2])
                        tc = 2 * th + tcl
                        ins = None
                        for kc in range(ND):
                            ins = t.matmul(pb[:], self.keysw[:, m % 2, kc, :],
                                           self.xn7T[:, kc, 512*tc:512*(tc+1)],
                                           start=(kc == 0), stop=(kc == ND - 1))
                        s_pe.inc(ins)
                kmm.append(s_pe.n)
                @blk.scalar
                def _(a):
                    a = EW(a, self.s_ac)
                    for tcl in range(2):
                        a.wait_ge(s_pe.h, kmm[m] - 1 + tcl)
                        pb = self.psA if (m * 2 + tcl) % 2 == 0 else self.psB
                        a.activation(self.h1[:, m, 512*tcl:512*(tcl+1)], pb[:], AF.Relu,
                                     bias=self.zero1[:], scale=1.0)
                    kev.append(s_ac.n - 1)
                    kev.append(s_ac.n)
            self.kmm_prev = kmm[-2:]
            self.kev_prev = kev[-2:]
            self.bank_free["A"] = ("ac", kev[-2])
            self.bank_free["B"] = ("ac", kev[-1])
            gmm = []
            gdv = []
            for e in range(NH):
                @blk.tensor
                def _(t):
                    if e == 0:
                        t.wait_ge(s_ac.h, kev[-1])
                        self.wait(t, sel_done)
                    for tcl in range(2):
                        gi = e * 2 + tcl
                        pb = self.psBC1 if gi % 2 == 0 else self.psBC2
                        if len(gdv) >= 2 and gi - 2 < len(gdv):
                            t.wait_ge(s_dv.h, gdv[gi - 2])
                        elif th > 0 and gi < 2:
                            t.wait_ge(s_dv.h, self.gdv_prev[gi - 2])
                        else:
                            self.bank_wait(t, "BC1" if gi % 2 == 0 else "BC2")
                        tc = 2 * th + tcl
                        s_pe.inc(t.matmul(pb[:], self.eselk[0:16, 128*e:128*(e+1)],
                                          self.selfm[0:16, 512*tc:512*(tc+1)],
                                          start=True, stop=True))
                gmm.append(s_pe.n)
                @blk.vector
                def _(v):
                    v = EW(v, self.s_dv)
                    for tcl in range(2):
                        v.wait_ge(s_pe.h, gmm[e] - 1 + tcl)
                        pb = self.psBC1 if (e * 2 + tcl) % 2 == 0 else self.psBC2
                        sl = slice(512 * tcl, 512 * (tcl + 1))
                        v.tensor_tensor(self.h1[:, e, sl], self.h1[:, e, sl], pb[:], ALU.mult)
                    gdv.append(s_dv.n - 1)
                    gdv.append(s_dv.n)
            self.gdv_prev = gdv[-2:]
            self.bank_free["BC1"] = ("dv", gdv[-2])
            self.bank_free["BC2"] = ("dv", gdv[-1])
            gate_done = ("dv", s_dv.n)
            for tcl in range(2):
                tc = 2 * th + tcl
                vmm = []
                vev4 = []
                vld = []
                for m in range(ND):
                    @blk.gpsimd
                    def _(g):
                        if tcl == 0 and m < 2:
                            g.wait_ge(s_pe.h, kmm[-1])
                        elif tcl == 1 and m < 2:
                            g.wait_ge(s_pe.h, self.vmm_prev[m - 2])
                        if len(vmm) >= 2:
                            g.wait_ge(s_pe.h, vmm[-2])
                        for kc in range(NH):
                            ws[m % 2].inc(g.dma_start(out=self.valsw[:, m % 2, kc, :],
                                                      in_=self.vals_h[128*kc:128*(kc+1), 128*m:128*(m+1)]), 16)
                    vld.append(ws[m % 2].n)
                    @blk.tensor
                    def _(t):
                        if m == 0:
                            self.wait(t, gate_done if tcl == 0 else None)
                            self.wait(t, pg_free)
                        t.wait_ge(ws[m % 2].h, vld[m])
                        pb = self.psA if m % 2 == 0 else self.psB
                        if len(vev4) >= 2:
                            t.wait_ge(s_ac.h, vev4[m - 2])
                        elif (th > 0 or tcl > 0) and m < 2:
                            t.wait_ge(s_ac.h, self.vev_prev[m - 2])
                        ins = None
                        for kc in range(NH):
                            ins = t.matmul(pb[:], self.valsw[:, m % 2, kc, :],
                                           self.h1[:, kc, 512*tcl:512*(tcl+1)],
                                           start=(kc == 0), stop=(kc == NH - 1))
                        s_pe.inc(ins)
                    vmm.append(s_pe.n)
                    @blk.scalar
                    def _(a):
                        a = EW(a, self.s_ac)
                        a.wait_ge(s_pe.h, vmm[m])
                        a.activation(self.page[:, m, :], (self.psA if m % 2 == 0 else self.psB)[:],
                                     AF.Copy, bias=0.0, scale=1.0)
                    vev4.append(s_ac.n)
                self.vmm_prev = vmm[-2:]
                self.vev_prev = vev4[-2:]
                self.bank_free["A"] = ("ac", vev4[-2])
                self.bank_free["B"] = ("ac", vev4[-1])
                pg_end = []
                for ti in range(4):
                    i = 4 * tc + ti
                    trn2 = []
                    Vb = s_dv.n
                    @blk.tensor
                    def _(t):
                        if ti == 0:
                            t.wait_ge(s_ac.h, vev4[-1])
                        self.bank_wait(t, "T1")
                        self.bank_wait(t, "T2")
                        for c in range(ND):
                            if c >= 2:
                                t.wait_ge(s_dv.h, Vb + c - 1)
                            pb = self.psT1 if c % 2 == 0 else self.psT2
                            trn2.append(s_pe.inc(t.transpose(
                                pb[:], self.page[:, c, 128*ti:128*(ti+1)], self.idt16[:])))
                    @blk.vector
                    def _(v):
                        v = EW(v, self.s_dv)
                        if ti == 0:
                            v.wait_ge(s_dv.h, zst)
                        for c in range(ND):
                            v.wait_ge(s_pe.h, trn2[c])
                            pb = self.psT1 if c % 2 == 0 else self.psT2
                            v.tensor_copy(self.tokst[:, ti % 2, 128*c:128*(c+1)], pb[:])
                        v.tensor_reduce(self.sums[:, i:i+1], self.tokst[:, ti % 2, :], AX.X, ALU.add)
                    nred = s_dv.n
                    self.bank_free["T1"] = ("dv", nred - 1 - (1 if ND % 2 == 0 else 0))
                    self.bank_free["T2"] = ("dv", nred - 1)
                    @blk.scalar
                    def _(a):
                        a = EW(a, self.s_ac)
                        a.wait_ge(s_dv.h, nred)
                        a.activation(self.sqscr[:], self.tokst[:, ti % 2, :], AF.Square,
                                     bias=self.zero1[:], scale=1.0, accum_out=self.sumsq[:, i:i+1])
                    nsq = s_ac.n
                    sts = [self.s_st0, self.s_st1]
                    @blk.sync
                    def _(s):
                        s.wait_ge(s_dv.h, nred)
                        sts[i % 2].inc(s.dma_start(out=ffn2_t[i], in_=self.tokst[:, ti % 2, :]), 16)
                    pg_end.append((i % 2, sts[i % 2].n))
                    stat_edges.append((nred, nsq, (i % 2, sts[i % 2].n)))
                    if i >= 1:
                        pnred, pnsq, (pp, pst) = stat_edges[i - 1]
                        @blk.vector
                        def _(v):
                            v = EW(v, self.s_dv)
                            v.wait_ge(sts[pp].h, pst)
                            v.wait_ge(s_ac.h, pnsq)
                pg_free = ("pe", s_pe.n)
            h1_free = ("pe", s_pe.n)
        rdy2 = self.ln_combine(stat_edges[-1][0], stat_edges[-1][1])
        lds = [self.s_ld0, self.s_ld1]
        B0, B1 = lds[0].n, lds[1].n
        ldn = [[B0, B1][i % 2] + 16 * (i // 2 + 1) for i in range(NT)]
        V6 = s_dv.n
        appn = [V6 + 2 * (i + 1) for i in range(NT)]
        xfst = self.xnst
        @blk.sync
        def _(s):
            s.wait_ge(self.s_st0.h, self.s_st0.n)
            s.wait_ge(self.s_st1.h, self.s_st1.n)
            for i in range(NT):
                if i >= 2:
                    s.wait_ge(s_dv.h, appn[i-2])
                lds[i % 2].inc(s.dma_start(out=xfst[:, i % 2, :], in_=ffn2_t[i]), 16)
        @blk.vector
        def _(v):
            v = EW(v, self.s_dv)
            v.wait_ge(s_dv.h, rdy2)
            for i in range(NT):
                v.wait_ge(lds[i % 2].h, ldn[i])
                v.tensor_scalar(self.xst[:, 0, :], xfst[:, i % 2, :],
                                self.rstds[:, i:i+1], self.lnbias[:, i:i+1],
                                ALU.mult, ALU.add)
                v.tensor_tensor(x3t(i), x3t(i), self.xst[:, 0, :], ALU.add)
        @blk.sync
        def _(s):
            s.wait_ge(s_dv.h, appn[7])
            self.s_st.inc(s.dma_start(out=self.out_h.rearrange("(i p) d -> p i d", p=128)[:, 0:8, :],
                                      in_=self.x3a[:, :, :]), 16)
            s.wait_ge(s_dv.h, appn[15])
            self.s_st.inc(s.dma_start(out=self.out_h.rearrange("(i p) d -> p i d", p=128)[:, 8:16, :],
                                      in_=self.x3b[:, :, :]), 16)
            s.wait_ge(self.s_st.h, self.s_st.n)

    def finalize_dbg(self):
        blk, s_st = self.blk, self.s_st
        waits = [(sm.h, sm.n) for sm in
                 [self.s_pe, self.s_dv, self.s_ac, self.s_ld0, self.s_ld1,
                  self.s_ld, self.s_w0, self.s_w1, self.s_w2] if sm.n > 0]
        srcs = {"q2": self.q_tok, "v2": self.v_tok, "acc2": self.acc}
        @blk.sync
        def _(s):
            for h, n in waits:
                s.wait_ge(h, n)
            for name, hap in self.dbg_h.items():
                if name in srcs:
                    s_st.inc(s.dma_start(out=hap, in_=srcs[name].rearrange("p i d -> p (i d)")), 16)
                elif name == "P2":
                    s_st.inc(s.dma_start(out=hap, in_=self.P.rearrange("p a x -> p (a x)")), 16)
                elif name == "x3":
                    s_st.inc(s.dma_start(out=hap[:, 0:8192], in_=self.x3a.rearrange("p i d -> p (i d)")), 16)
                    s_st.inc(s.dma_start(out=hap[:, 8192:], in_=self.x3b.rearrange("p i d -> p (i d)")), 16)
            s.wait_ge(s_st.h, s_st.n)


def l1_inmap(xB, ins, consts):
    W = ins["L1_attn_in_proj_weight"]
    Wr = W.reshape(16, 3, 64, 1024)
    Wq = Wr[:, 0].reshape(1024, 1024)
    Wk = Wr[:, 1].reshape(1024, 1024)
    Wv = Wr[:, 2].reshape(1024, 1024)
    WqkvT = np.ascontiguousarray(np.concatenate([Wq, Wk, Wv], 0).T)
    return {
        "xB": xB,
        "WqkvT": WqkvT,
        "WoutT": np.ascontiguousarray(ins["L1_attn_out_weight"].T),
        "selWT": np.ascontiguousarray(ins["L1_ffn_expert_sel"].T),
        "keysT": np.ascontiguousarray(ins["L1_ffn_keys"].transpose(1, 0, 2).reshape(1024, 2048)),
        "valsS": np.ascontiguousarray(ins["L1_ffn_values"].reshape(2048, 1024)),
        "ident16": consts["ident16"],
        "eselk16": consts["eselk16"],
    }


def kernel(**inputs):
    ins = {k: np.asarray(v) for k, v in inputs.items()}
    x = np.ascontiguousarray(ins["L0_attn_x"], dtype=np.float32)
    consts = host_constants()

    b1 = L1B(dbg=(), upto="all")
    maps1 = [l0_inmap(np.ascontiguousarray(x[i]), ins, consts) for i in range(8)]
    res1 = run_bass_kernel_spmd(b1.nc, maps1, core_ids=list(range(8)))
    x2 = np.stack([np.asarray(res1.results[i]["x2"]) for i in range(8)], 0)

    b2 = L2B(dbg=(), upto="all")
    maps2 = [l1_inmap(np.ascontiguousarray(x2[:, 256 * i:256 * (i + 1), :].reshape(T, D)),
                      ins, consts) for i in range(8)]
    res2 = run_bass_kernel_spmd(b2.nc, maps2, core_ids=list(range(8)))

    out = np.empty((8, T, D), dtype=np.float32)
    for i in range(8):
        out[:, 256 * i:256 * (i + 1), :] = np.asarray(res2.results[i]["y"]).reshape(8, 256, D)
    return out


# version 02
# speedup vs baseline: 1.0035x; 1.0035x over previous
import math
import numpy as np
from contextlib import ExitStack

import concourse.bass as bass
import concourse.mybir as mybir
from concourse.bass_utils import run_bass_kernel_spmd

F32, F16, BF16 = mybir.dt.float32, mybir.dt.float16, mybir.dt.bfloat16
AF = mybir.ActivationFunctionType
ALU = mybir.AluOpType
AX = mybir.AxisListType

T = 2048
D = 1024
NT = 16
ND = 8
TC = 4
NH = 16
HD = 64
PAGE = 256
NP = 8
EPS = 1e-5
EXPC = 30.0
NE = 8
RA = 272
MC = [(0, 128), (128, 128), (256, 16)]


class Sem:
    def __init__(self, es, nc, name):
        self.h = es.enter_context(nc.semaphore(name))
        self.n = 0
    def inc(self, instr, k=1):
        instr.then_inc(self.h, k)
        self.n += k
        return self.n


EW_OPS = {"tensor_tensor", "tensor_scalar", "tensor_reduce", "tensor_copy",
          "reciprocal", "memset", "activation"}


class EW:
    def __init__(self, eng, sem):
        self._e, self._s = eng, sem
    def __getattr__(self, nm):
        f = getattr(self._e, nm)
        if nm in EW_OPS:
            def w(*a, **k):
                if self._s.n > 0:
                    self._e.wait_ge(self._s.h, self._s.n)
                ins = f(*a, **k)
                self._s.inc(ins)
                return ins
            return w
        return f


class L1B:
    def __init__(self, dbg=(), upto="all"):
        self.dbg = set(dbg)
        self.upto = upto
        self.nc = nc = bass.Bass("TRN2", target_bir_lowering=False)
        self.es = es = ExitStack()
        self.dbg_h = {}

        dram = lambda n, s, dt, k: nc.dram_tensor(n, s, dt, kind=k).ap()
        self.x_h = dram("x", [T, D], F32, "ExternalInput")
        self.wq_h = dram("wq", [D, D], F32, "ExternalInput")
        self.wd_h = dram("w_down", [D, 512], F32, "ExternalInput")
        self.wku_h = dram("wk_up", [512, D], F32, "ExternalInput")
        self.wvu_h = dram("wv_up", [512, D], F32, "ExternalInput")
        self.wo_h = dram("wo", [D, D], F32, "ExternalInput")
        self.idt_h = dram("ident16", [128, 128], F16, "ExternalInput")
        self.swp_h = dram("swap16", [128, 128], F16, "ExternalInput")
        self.ra_h = dram("ropeA", [128, T], F16, "ExternalInput")
        self.rb_h = dram("ropeB", [128, T], F16, "ExternalInput")
        self.wbase_h = dram("W_base", [D, D], F32, "ExternalInput")
        self.bbase_h = dram("b_basec", [128, ND], F32, "ExternalInput")
        self.bbaser_h = dram("b_baser", [1, D], F32, "ExternalInput")
        self.wd1_h = dram("Wd1", [NE, D, 256], F32, "ExternalInput")
        self.bd1_h = dram("bd1f", [1, NE * 256], F32, "ExternalInput")
        self.wd2t_h = dram("Wd2T", [NE, D, 256], F32, "ExternalInput")
        self.bd2c_h = dram("bd2c", [128, NE, ND], F32, "ExternalInput")
        self.wadn_h = dram("Wad_stack", [D, 512], F32, "ExternalInput")
        self.badn_h = dram("bad_stack", [128, 4], F32, "ExternalInput")
        self.waup_h = dram("Wau_stack", [512, D], F32, "ExternalInput")
        self.baup_h = dram("ba_up", [NE, D], F32, "ExternalInput")
        self.iota8_h = dram("iota8", [1, 8], F32, "ExternalInput")
        self.eexp_h = dram("Eexp", [8, 512], F16, "ExternalInput")
        self.out_h = dram("x2", [T, D], F32, "ExternalOutput")

        sb = lambda *a: es.enter_context(nc.sbuf_tensor(*a))
        pst = lambda *a: es.enter_context(nc.psum_tensor(*a))

        self.idt16 = sb("idt16", [128, 128], F16)
        self.swp16 = sb("swp16", [128, 128], F16)
        self.ones16 = sb("ones16", [128, 64], F16)
        self.onesbf = sb("onesbf", [128, 64], BF16)
        self.onesf32 = sb("onesf32", [128, 1], F32)
        self.ones512 = sb("ones512", [1, 512], F16)
        self.eps_t = sb("eps_t", [128, 1], F32)
        self.c30 = sb("c30", [128, 1], F32)
        self.zero1 = sb("zero1", [128, 1], F32)
        self.xst = sb("xst", [128, 2, D], F32)
        self.xnst = sb("xnst", [128, 2, D], F16)
        self.sums = sb("sums", [128, NT], F32)
        self.sumsq = sb("sumsq", [128, NT], F32)
        self.means = sb("means", [128, NT], F32)
        self.rstds = sb("rstds", [128, NT], F32)
        self.vart = sb("vart", [128, NT], F32)
        self.lnbias = sb("lnbias", [128, NT], F32)
        self.scr = sb("scr", [128, 1536], F32)
        scrf16 = self.scr.bitcast(F16)
        scrbf = self.scr.bitcast(BF16)
        self.sqscr = scrf16[:, 0:1024]
        self.rsc1 = self.scr[:, 512:1024]
        self.rsc2 = self.scr[:, 1024:1536]
        self.osc = self.scr[:, 512:768]
        self.rrec = self.scr[:, 768:1024]
        self.rhl = scrbf[:, 2048:2560].rearrange("p (a w) -> p a w", w=PAGE)
        self.rlo32 = self.scr[:, 1280:1536]

        self.wqa = sb("wqa", [128, 2, ND, 128], F16)
        self.wd16 = sb("wd16", [128, ND, 512], F16)
        self.wkv = sb("wkv", [128, 8192], F16)
        self.wku16 = self.wkv[:, 0:4096].rearrange("p (c m) -> p c m", m=1024)
        self.wvu16 = self.wkv[:, 4096:8192].rearrange("p (c m) -> p c m", m=1024)

        self.ar1 = sb("ar1", [128, 16384], F16)
        self.arq = sb("arq", [128, 16384], F16)
        self.ark = sb("ark", [128, 16384], F16)
        self.arv = sb("arv", [128, 16384], F16)
        self.arl = sb("arl", [128, 8704], F16)
        self.arr = sb("arr", [128, 2048], F32)

        self.xnT = self.ar1.rearrange("p (c t) -> p c t", t=T)
        self.att_tok = self.ar1.rearrange("p (i d) -> p i d", d=D)
        self.xn2_tok = self.att_tok
        self.ffn_fm = self.xnT
        self.q0 = self.arq.rearrange("p (c t) -> p c t", t=T)
        self.xn2T = self.q0
        self.ffn_tok = self.arq.rearrange("p (i d) -> p i d", d=D)
        self.k0 = self.ark.rearrange("p (c t) -> p c t", t=T)
        self.vstore = self.arv.bitcast(BF16).rearrange("p (i d) -> p i d", d=D)
        self.x1a = self.ark.bitcast(F32).rearrange("p (i d) -> p i d", d=D)
        self.x1b = self.arv.bitcast(F32).rearrange("p (i d) -> p i d", d=D)
        self.latT = self.arl[:, 0:8192].rearrange("p (c t) -> p c t", t=T)
        self.woa = self.arl[:, 0:8192].rearrange("p (c m) -> p c m", m=1024)
        self._wd1h = self.arl[:, 0:2048].rearrange("p (c r) -> p c r", r=256)
        self._wd1l = self.arl[:, 2048:4096].rearrange("p (c r) -> p c r", r=256)
        self._wd2th = self.arl[:, 4096:6272].rearrange("p (c r) -> p c r", r=RA)
        self._wd2tl = self.arl[:, 6272:8448].rearrange("p (c r) -> p c r", r=RA)
        self._wad = self.arl[:, 0:4096].rearrange("p (c r) -> p c r", r=512)
        self._wau = self.arl[:, 4096:8192].rearrange("p (c r) -> p c r", r=1024)
        self.htok = self.wkv[:, 0:4352].rearrange("p (i r) -> p i r", r=RA)
        self.ptile = self.wkv.bitcast(BF16)[:, 0:4096].rearrange("p (j w) -> p j w", w=PAGE)
        self.bd1f = self.wkv[0:1, 4352:6400]
        self.eexp16 = self.wkv[0:8, 6400:6912]
        self.baup16 = self.wkv[0:8, 6912:7936]
        self.opage = self.wd16.rearrange("p c m -> p (c m)").rearrange("p (b c w) -> p b c w", b=2, w=PAGE)
        self.wstage = self.arr.rearrange("p (c r) -> p c r", r=256)
        self.bd1f32 = self.arr[0:1, 0:2048]
        self.bd1l = self.wd16.rearrange("p c m -> p (c m)").bitcast(F16)[0:1, 0:2048]
        arrf16 = self.arr.bitcast(F16)
        self.ropeA = arrf16[:, 0:2048]
        self.ropeB = arrf16[:, 2048:4096]
        self.att2p = arrf16.rearrange("p (b c w) -> p b c w", b=2, w=PAGE)
        self.achunk = arrf16.rearrange("p (b c w) -> p b c w", b=2, w=512)
        self.basel = sb("basel", [1, ND, 128], F16)
        self.bbaser = sb("bbaser", [1, D], F16)

        self.bd2c32 = sb("bd2c32", [128, NE, ND], F32)
        self.bd2ch = sb("bd2ch", [128, NE, ND], F16)
        self.bd2cl = sb("bd2cl", [128, NE, ND], F16)

        self.bbasec = sb("bbasec", [128, ND], F32)
        self.badm = sb("badm", [128, 4], F32)
        self.bads = sb("bads", [128, 4], F32)
        self.ohexp = sb("ohexp", [128, 4], F32)
        self.iota8 = sb("iota8s", [1, 8], F32)
        self.iom = sb("iom", [1, 8], F32)
        self.lacc = sb("lacc", [128, 8], F32)
        self.lfin = sb("lfin", [1, 8], F32)
        self.rmin = sb("rmin", [1, 1], F32)
        self.idxs = sb("idxs", [1, 1], F32)
        self.eqrow = sb("eqrow", [1, 8], F32)
        self.zrow = sb("zrow", [1, 8], F32)
        self.ohrow = sb("ohrow", [1, 8], F32)
        self.gsc = self.scr[:, 512:784]
        self.gsc2 = self.scr[:, 784:1056]
        self.xhpart = sb("xhpart", [128, ND], F32)
        self.onec = sb("onec", [8, 1], F16)

        self.psA = pst("psA", [128, 512], F32)
        self.psB = pst("psB", [128, 512], F32)
        self.psT1 = pst("psT1", [128, 128], F16)
        self.psT2 = pst("psT2", [128, 128], F16)
        self.psS1 = pst("psS1", [128, PAGE], F32)
        self.psS2 = pst("psS2", [128, PAGE], F32)
        self.psO = pst("psO", [128, PAGE], F32)
        self.psBC = pst("psBC", [128, PAGE], F32)
        self.psF1 = None
        self.psF2 = None

        self.psF1 = self.psA[:, 0:RA]
        self.psF2 = self.psB[:, 0:RA]
        self.s_ld = Sem(es, nc, "s_ld")
        self.s_ld0 = Sem(es, nc, "s_ld0")
        self.s_ld1 = Sem(es, nc, "s_ld1")
        self.s_wt = Sem(es, nc, "s_wt")
        self.s_wq0 = Sem(es, nc, "s_wq0")
        self.s_wq1 = Sem(es, nc, "s_wq1")
        self.s_wo = Sem(es, nc, "s_wo")
        self.s_wf = Sem(es, nc, "s_wf")
        self.s_we0 = Sem(es, nc, "s_we0")
        self.s_we1 = Sem(es, nc, "s_we1")
        self.s_wb0 = Sem(es, nc, "s_wb0")
        self.s_wb1 = Sem(es, nc, "s_wb1")
        self.s_wa = Sem(es, nc, "s_wa")
        self.s_pe = Sem(es, nc, "s_pe")
        self.s_dv = Sem(es, nc, "s_dv")
        self.s_ac = Sem(es, nc, "s_ac")
        self.s_st = Sem(es, nc, "s_st")
        self.sem = {"pe": self.s_pe, "dv": self.s_dv, "ac": self.s_ac,
                    "ld": self.s_ld, "wt": self.s_wt, "st": self.s_st,
                    "ld0": self.s_ld0, "ld1": self.s_ld1,
                    "wq0": self.s_wq0, "wq1": self.s_wq1, "wo": self.s_wo,
                    "wf": self.s_wf, "we0": self.s_we0, "we1": self.s_we1,
                    "wb0": self.s_wb0, "wb1": self.s_wb1, "wa": self.s_wa}
        self.bank_free = {}

        nc.const_aps.register(F32, 0.0, self.zero1[:])
        self.blk = es.enter_context(nc.Block())
        self.build()

    def wait(self, eng, edge):
        if edge is None:
            return
        sname, cnt = edge
        if cnt > 0:
            eng.wait_ge(self.sem[sname].h, cnt)

    def bank_wait(self, eng, bank):
        self.wait(eng, self.bank_free.get(bank))

    def build(self):
        self.ph_loads()
        if self.upto == "loads":
            self.finalize_dbg()
            return
        self.lnT_done = self.ph_ln_hbm(self.x_h, dst_T=self.xnT)
        if "xnt" in self.dbg:
            self.dbg_h["xnt"] = self.nc.dram_tensor("dbg_xnt", [D, T], F16, kind="ExternalOutput").ap()
        if self.upto == "ln":
            self.finalize_dbg()
            return
        self.q_done = self.ph_proj_q()
        if self.upto == "qonly":
            self.finalize_dbg()
            return
        self.lat_done = self.ph_proj(self.latT, self.wd16, ND, 4, src=self.xnT,
                                     src_edge=self.lnT_done, evac="dv")
        if self.upto == "lat":
            self.finalize_dbg()
            return
        self.k_done = self.ph_proj(self.k0, self.wku16, 4, ND, src=self.latT,
                                   src_edge=self.lat_done, evac="ac")
        self.kproj_pe = ("pe", self.s_pe.n)
        if self.upto == "kp":
            self.finalize_dbg()
            return
        self.ph_vproj()
        if self.upto == "vp":
            self.finalize_dbg()
            return
        self.ph_wo_load()
        self.ropeq_done = self.ph_rope(self.q0, self.q_done)
        if self.upto == "ropeq":
            self.finalize_dbg()
            return
        self.ropek_done = self.ph_rope(self.k0, self.k_done)
        for nm, shape, dt in [("q", [D, T], F16), ("k", [D, T], F16),
                              ("lat", [512, T], F16), ("v", [T, D], BF16)]:
            if nm in self.dbg:
                self.dbg_h[nm] = self.nc.dram_tensor("dbg_" + nm, shape, dt, kind="ExternalOutput").ap()
        if self.upto == "qkv":
            self.finalize_dbg()
            return
        self.ph_attention_wo()
        if "x1" in self.dbg:
            self.dbg_h["x1"] = self.nc.dram_tensor("dbg_x1", [128, NT * D], F32, kind="ExternalOutput").ap()
        if "atok" in self.dbg:
            self.dbg_h["atok"] = self.nc.dram_tensor("dbg_atok", [128, NT * D], F16, kind="ExternalOutput").ap()
        if self.upto == "x1":
            self.finalize_dbg()
            return
        self.ph_ffn_loads()
        self.ph_ln_sbuf()
        self.ph_gram()
        self.ph_argmin()
        for nm, shape, dt in [("losses", [1, 8], F32), ("onehot", [8, 1], F16)]:
            if nm in self.dbg:
                self.dbg_h[nm] = self.nc.dram_tensor("dbg_" + nm, shape, dt, kind="ExternalOutput").ap()
        if self.upto == "route":
            self.finalize_dbg()
            return
        self.ph_adapter_base()
        self.ph_ln4_apply_store()
        if self.upto != "all":
            self.finalize_dbg()

    def ph_loads(self):
        blk, s_wt, s_ld = self.blk, self.s_wt, self.s_ld
        @blk.gpsimd
        def _(g):
            for kc in range(ND):
                s_wt.inc(g.dma_start(out=self.wd16[:, kc, :], in_=self.wd_h[128*kc:128*(kc+1), :]), 16)
            for kc in range(4):
                s_wt.inc(g.dma_start(out=self.wku16[:, kc, :], in_=self.wku_h[128*kc:128*(kc+1), :]), 16)
            for kc in range(4):
                s_wt.inc(g.dma_start(out=self.wvu16[:, kc, :], in_=self.wvu_h[128*kc:128*(kc+1), :]), 16)
        self.wt_attn = ("wt", s_wt.n)
        @blk.sync
        def _(s):
            s_ld.inc(s.dma_start(out=self.idt16[:], in_=self.idt_h), 16)
            s_ld.inc(s.dma_start(out=self.swp16[:], in_=self.swp_h), 16)
            s_ld.inc(s.dma_start(out=self.ropeA, in_=self.ra_h), 16)
            s_ld.inc(s.dma_start(out=self.ropeB, in_=self.rb_h), 16)
            s_ld.inc(s.dma_start(out=self.iota8[:], in_=self.iota8_h), 16)
        self.ld_const = ("ld", self.s_ld.n)
        @blk.vector
        def _(v):
            v = EW(v, self.s_dv)
            v.memset(self.ones16[:], 1.0)
            v.memset(self.onesbf[:], 1.0)
            v.memset(self.onesf32[:], 1.0)
            v.memset(self.ones512[:], 1.0)
            v.memset(self.eps_t[:], EPS)
            v.memset(self.zero1[:], 0.0)
            v.memset(self.c30[:], -EXPC)
        self.init_dv = ("dv", self.s_dv.n)

    def ln_combine(self, v_edge, a_edge):
        blk = self.blk
        s_dv, s_ac = self.s_dv, self.s_ac
        @blk.vector
        def _(v):
            v = EW(v, self.s_dv)
            if v_edge: v.wait_ge(s_dv.h, v_edge)
            if a_edge: v.wait_ge(s_ac.h, a_edge)
            v.tensor_scalar(self.means[:], self.sums[:], 1.0 / D, None, ALU.mult)
            v.tensor_scalar(self.vart[:], self.sumsq[:], 1.0 / D, None, ALU.mult)
            v.tensor_tensor(self.rstds[:], self.means[:], self.means[:], ALU.mult)
            v.tensor_tensor(self.vart[:], self.vart[:], self.rstds[:], ALU.subtract)
        nvar = s_dv.n
        @blk.scalar
        def _(a):
            a = EW(a, self.s_ac)
            a.wait_ge(s_dv.h, nvar)
            a.activation(self.vart[:], self.vart[:], AF.Sqrt, bias=self.eps_t[:], scale=1.0)
        nstd = s_ac.n
        @blk.vector
        def _(v):
            v = EW(v, self.s_dv)
            v.wait_ge(s_ac.h, nstd)
            v.reciprocal(self.rstds[:], self.vart[:])
            v.tensor_tensor(self.lnbias[:], self.means[:], self.rstds[:], ALU.mult)
            v.tensor_scalar(self.lnbias[:], self.lnbias[:], -1.0, None, ALU.mult)
        return s_dv.n

    def ph_ln_hbm(self, src_h, dst_T):
        blk = self.blk
        s_ld, s_dv, s_ac, s_pe = self.s_ld, self.s_dv, self.s_ac, self.s_pe
        src_t = src_h.rearrange("(n p) d -> n p d", p=128)
        lds = [self.s_ld0, self.s_ld1]
        B0, B1 = lds[0].n, lds[1].n
        V0, A0 = s_dv.n, s_ac.n
        ldn = [[B0, B1][i % 2] + 16 * (i // 2 + 1) for i in range(NT)]
        vn = [V0 + 1 + (i + 1) for i in range(NT)]
        an = [A0 + (i + 1) for i in range(NT)]
        @blk.sync
        def _(s):
            for i in range(NT):
                if i >= 2:
                    s.wait_ge(s_dv.h, vn[i-2])
                    s.wait_ge(s_ac.h, an[i-2])
                lds[i % 2].inc(s.dma_start(out=self.xst[:, i % 2, :], in_=src_t[i]), 16)
        z0 = s_dv.n + 1
        @blk.vector
        def _(v):
            v = EW(v, self.s_dv)
            self.wait(v, self.init_dv)
            v.memset(self.sumsq[:], 0.0)
            for i in range(NT):
                v.wait_ge(lds[i % 2].h, ldn[i])
                v.tensor_reduce(self.sums[:, i:i+1], self.xst[:, i % 2, :], AX.X, ALU.add)
        @blk.scalar
        def _(a):
            a = EW(a, self.s_ac)
            self.wait(a, self.init_dv)
            a.wait_ge(s_dv.h, z0)
            for i in range(NT):
                a.wait_ge(lds[i % 2].h, ldn[i])
                if i > 0:
                    a.wait_ge(s_ac.h, an[i-1])
                a.activation(self.sqscr[:], self.xst[:, i % 2, :], AF.Square,
                                      bias=self.zero1[:], scale=1.0, accum_out=self.sumsq[:, i:i+1])
        rdy = self.ln_combine(vn[-1], an[-1])

        C0, C1 = lds[0].n, lds[1].n
        V1, P0 = s_dv.n, s_pe.n
        ldn2 = [[C0, C1][i % 2] + 16 * (i // 2 + 1) for i in range(NT)]
        per = 1 + ND
        vxn = [V1 + i * per + 1 for i in range(NT)]
        vev = [[V1 + i * per + 1 + (c + 1) for c in range(ND)] for i in range(NT)]
        trn = [[P0 + i * ND + c + 1 for c in range(ND)] for i in range(NT)]
        @blk.sync
        def _(s):
            s.wait_ge(s_dv.h, vn[NT-1])
            s.wait_ge(s_ac.h, an[NT-1])
            for i in range(NT):
                if i >= 2:
                    s.wait_ge(s_dv.h, vxn[i-2])
                lds[i % 2].inc(s.dma_start(out=self.xst[:, i % 2, :], in_=src_t[i]), 16)
        @blk.tensor
        def _(t):
            self.wait(t, self.ld_const)
            self.bank_wait(t, "T1")
            self.bank_wait(t, "T2")
            for i in range(NT):
                t.wait_ge(s_dv.h, vxn[i])
                for c in range(ND):
                    if i * ND + c >= 2:
                        pi, pc = divmod(i * ND + c - 2, ND)
                        t.wait_ge(s_dv.h, vev[pi][pc])
                    pb = self.psT1 if c % 2 == 0 else self.psT2
                    s_pe.inc(t.transpose(pb[:], self.xnst[:, i % 2, 128*c:128*(c+1)], self.idt16[:]))
        @blk.vector
        def _(v):
            v = EW(v, self.s_dv)
            v.wait_ge(s_dv.h, rdy)
            for i in range(NT):
                v.wait_ge(lds[i % 2].h, ldn2[i])
                if i >= 2:
                    v.wait_ge(s_pe.h, trn[i-2][-1])
                v.tensor_scalar(self.xnst[:, i % 2, :], self.xst[:, i % 2, :],
                                         self.means[:, i:i+1], self.rstds[:, i:i+1],
                                         ALU.subtract, ALU.mult)
                for c in range(ND):
                    v.wait_ge(s_pe.h, trn[i][c])
                    pb = self.psT1 if c % 2 == 0 else self.psT2
                    v.tensor_copy(dst_T[:, c, 128*i:128*(i+1)], pb[:])
        assert s_dv.n == vev[-1][-1]
        self.bank_free["T1"] = ("dv", vev[-1][-2])
        self.bank_free["T2"] = ("dv", vev[-1][-1])
        self.xst_free = ("dv", vxn[-1])
        return ("dv", s_dv.n)

    def ph_proj_q(self):
        blk = self.blk
        s_pe, s_ac = self.s_pe, self.s_ac
        P0, A0 = s_pe.n, s_ac.n
        wqs = [self.s_wq0, self.s_wq1]
        Q0, Q1 = wqs[0].n, wqs[1].n
        wld = [[Q0, Q1][m % 2] + 16 * ND * (m // 2 + 1) for m in range(ND)]
        mm_n = [[P0 + m * TC + tc + 1 for tc in range(TC)] for m in range(ND)]
        ev_n = [[A0 + m * TC + tc + 1 for tc in range(TC)] for m in range(ND)]
        @blk.gpsimd
        def _(g):
            for m in range(ND):
                if m >= 2:
                    g.wait_ge(s_pe.h, mm_n[m-2][-1])
                for kc in range(ND):
                    wqs[m % 2].inc(g.dma_start(out=self.wqa[:, m % 2, kc, :],
                                               in_=self.wq_h[128*kc:128*(kc+1), 128*m:128*(m+1)]), 16)
        @blk.tensor
        def _(t):
            self.wait(t, self.lnT_done)
            gi = 0
            for m in range(ND):
                t.wait_ge(wqs[m % 2].h, wld[m])
                for tc in range(TC):
                    pb = self.psA if gi % 2 == 0 else self.psB
                    if gi >= 2:
                        pm, ptc = divmod(gi - 2, TC)
                        t.wait_ge(s_ac.h, ev_n[pm][ptc])
                    else:
                        self.bank_wait(t, "A" if gi % 2 == 0 else "B")
                    ins = None
                    for k in range(ND):
                        ins = t.matmul(pb[:], self.wqa[:, m % 2, k, :],
                                       self.xnT[:, k, 512*tc:512*(tc+1)],
                                       start=(k == 0), stop=(k == ND - 1))
                    s_pe.inc(ins)
                    gi += 1
        @blk.scalar
        def _(a):
            a = EW(a, self.s_ac)
            gi = 0
            for m in range(ND):
                for tc in range(TC):
                    pb = self.psA if gi % 2 == 0 else self.psB
                    a.wait_ge(s_pe.h, mm_n[m][tc])
                    a.activation(self.q0[:, m, 512*tc:512*(tc+1)], pb[:], AF.Copy,
                                               bias=0.0, scale=1.0)
                    gi += 1
        self.bank_free["A"] = ("ac", ev_n[-1][-2])
        self.bank_free["B"] = ("ac", ev_n[-1][-1])
        self.qproj_pe = ("pe", s_pe.n)
        return ("ac", self.s_ac.n)

    def ph_wo_load(self):
        blk = self.blk
        @blk.gpsimd
        def _(g):
            self.wait(g, self.kproj_pe)
            self.wait(g, self.vproj_pe)
            for kc in range(ND):
                self.s_wo.inc(g.dma_start(out=self.woa[:, kc, :], in_=self.wo_h[128*kc:128*(kc+1), :]), 16)
        self.wo_ld = ("wo", self.s_wo.n)

    def ph_proj(self, dst, wsb, nK, nM, src, src_edge, evac="ac", wt_edge=None):
        blk = self.blk
        s_pe = self.s_pe
        es_ = self.sem[evac]
        P0, E0 = s_pe.n, es_.n
        ngrp = nM * TC
        mm_n = [P0 + g + 1 for g in range(ngrp)]
        ev_n = [E0 + g + 1 for g in range(ngrp)]
        @blk.tensor
        def _(t):
            self.wait(t, src_edge)
            self.wait(t, wt_edge if wt_edge is not None else self.wt_attn)
            gi = 0
            for m in range(nM):
                for tc in range(TC):
                    pb = self.psA if gi % 2 == 0 else self.psB
                    if gi >= 2:
                        t.wait_ge(es_.h, ev_n[gi-2])
                    else:
                        self.bank_wait(t, "A" if gi % 2 == 0 else "B")
                    ins = None
                    for k in range(nK):
                        ins = t.matmul(pb[:], wsb[:, k, 128*m:128*(m+1)],
                                       src[:, k, 512*tc:512*(tc+1)],
                                       start=(k == 0), stop=(k == nK - 1))
                    s_pe.inc(ins)
                    gi += 1
        eng = blk.scalar if evac == "ac" else blk.vector
        @eng
        def _(a):
            gi = 0
            for m in range(nM):
                for tc in range(TC):
                    pb = self.psA if gi % 2 == 0 else self.psB
                    a.wait_ge(s_pe.h, mm_n[gi])
                    if evac == "ac":
                        es_.inc(a.activation(dst[:, m, 512*tc:512*(tc+1)], pb[:], AF.Copy,
                                             bias=0.0, scale=1.0))
                    else:
                        es_.inc(a.tensor_copy(dst[:, m, 512*tc:512*(tc+1)], pb[:]))
                    gi += 1
        self.bank_free["A"] = (evac, ev_n[-2] if ngrp >= 2 else ev_n[-1])
        self.bank_free["B"] = (evac, ev_n[-1])
        return (evac, es_.n)

    def ph_vproj(self):
        blk, s_pe, s_dv = self.blk, self.s_pe, self.s_dv
        P0, V0 = s_pe.n, s_dv.n
        ngrp = NT * 2
        mm_n = [P0 + g + 1 for g in range(ngrp)]
        ev_n = [V0 + g + 1 for g in range(ngrp)]
        @blk.tensor
        def _(t):
            self.wait(t, self.lat_done)
            gi = 0
            for i in range(NT):
                for vc in range(2):
                    pb = self.psA if gi % 2 == 0 else self.psB
                    if gi >= 2:
                        t.wait_ge(s_dv.h, ev_n[gi-2])
                    else:
                        self.bank_wait(t, "A" if gi % 2 == 0 else "B")
                    ins = None
                    for kc in range(4):
                        ins = t.matmul(pb[:], self.latT[:, kc, 128*i:128*(i+1)],
                                       self.wvu16[:, kc, 512*vc:512*(vc+1)],
                                       start=(kc == 0), stop=(kc == 3))
                    s_pe.inc(ins)
                    gi += 1
        @blk.vector
        def _(v):
            v = EW(v, self.s_dv)
            gi = 0
            for i in range(NT):
                for vc in range(2):
                    pb = self.psA if gi % 2 == 0 else self.psB
                    v.wait_ge(s_pe.h, mm_n[gi])
                    v.tensor_copy(self.vstore[:, i, 512*vc:512*(vc+1)], pb[:])
                    gi += 1
        self.bank_free["A"] = ("dv", ev_n[-2])
        self.bank_free["B"] = ("dv", ev_n[-1])
        self.vproj_pe = ("pe", s_pe.n)
        self.v_done = ("dv", s_dv.n)

    def ph_rope(self, buf, edge):
        blk, s_pe, s_dv = self.blk, self.s_pe, self.s_dv
        P0, V0 = s_pe.n, s_dv.n
        ngrp = ND * TC
        mm_n = [P0 + g + 1 for g in range(ngrp)]
        dv_n = [V0 + 3 * (g + 1) for g in range(ngrp)]
        @blk.tensor
        def _(t):
            self.wait(t, edge)
            self.wait(t, self.ld_const)
            gi = 0
            for c in range(ND):
                for tc in range(TC):
                    pb = self.psA if gi % 2 == 0 else self.psB
                    if gi >= 2:
                        t.wait_ge(s_dv.h, dv_n[gi-2])
                    else:
                        self.bank_wait(t, "A" if gi % 2 == 0 else "B")
                    s_pe.inc(t.matmul(pb[:], self.swp16[:], buf[:, c, 512*tc:512*(tc+1)],
                                      start=True, stop=True))
                    gi += 1
        @blk.vector
        def _(v):
            v = EW(v, self.s_dv)
            gi = 0
            for c in range(ND):
                for tc in range(TC):
                    pb = self.psA if gi % 2 == 0 else self.psB
                    v.wait_ge(s_pe.h, mm_n[gi])
                    sl = slice(512*tc, 512*(tc+1))
                    v.tensor_tensor(self.rsc1[:], buf[:, c, sl], self.ropeA[:, sl], ALU.mult)
                    v.tensor_tensor(self.rsc2[:], pb[:], self.ropeB[:, sl], ALU.mult)
                    v.tensor_tensor(buf[:, c, sl], self.rsc1[:], self.rsc2[:], ALU.add)
                    gi += 1
        self.bank_free["A"] = ("dv", dv_n[-2])
        self.bank_free["B"] = ("dv", dv_n[-1])
        return ("dv", s_dv.n)

    def ph_attention_wo(self):
        blk = self.blk
        s_pe, s_dv, s_ac, s_ld = self.s_pe, self.s_dv, self.s_ac, self.s_ld
        self.u_prev = dict(vstop=0, bca=0)
        self.opage_free = [None, None]
        self.att2_free = [None, None]
        @blk.vector
        def _(v):
            v = EW(v, self.s_dv)
            v.memset(self.sumsq[:], 0.0)
        for qc in range(NP):
            ob = qc % 2
            for h in range(NH):
                self.attn_unit(h, qc, ob)
            page_units_done = ("dv", s_dv.n)
            P0, A0 = s_pe.n, s_ac.n
            mm_n = [P0 + m + 1 for m in range(ND)]
            woev = [A0 + m + 1 for m in range(ND)]
            @blk.tensor
            def _(t):
                self.wait(t, page_units_done)
                if qc == 0:
                    self.wait(t, self.wo_ld)
                for m in range(ND):
                    pb = self.psS1 if m % 2 == 0 else self.psS2
                    if m >= 2:
                        t.wait_ge(s_ac.h, woev[m-2])
                    else:
                        self.bank_wait(t, "S1" if m % 2 == 0 else "S2")
                    ins = None
                    for k in range(ND):
                        ins = t.matmul(pb[:], self.woa[:, k, 128*m:128*(m+1)],
                                       self.opage[:, ob, k, :],
                                       start=(k == 0), stop=(k == ND - 1))
                    s_pe.inc(ins)
            @blk.scalar
            def _(a):
                a = EW(a, self.s_ac)
                self.wait(a, self.att2_free[ob])
                for m in range(ND):
                    pb = self.psS1 if m % 2 == 0 else self.psS2
                    a.wait_ge(s_pe.h, mm_n[m])
                    a.activation(self.att2p[:, ob, m, :], pb[:], AF.Copy,
                                          bias=0.0, scale=1.0)
            self.bank_free["S1"] = ("ac", woev[-2])
            self.bank_free["S2"] = ("ac", woev[-1])
            self.opage_free[ob] = ("pe", mm_n[-1])
            P1, V0, A1 = s_pe.n, s_dv.n, s_ac.n
            trn = [P1 + gg + 1 for gg in range(2 * ND)]
            vevt = [V0 + gg + (gg // ND) + 1 for gg in range(2 * ND)]
            @blk.tensor
            def _(t):
                t.wait_ge(s_ac.h, woev[-1])
                for gg in range(2 * ND):
                    c = gg % ND
                    if gg >= 2:
                        t.wait_ge(s_dv.h, vevt[gg-2])
                    else:
                        self.bank_wait(t, "T1" if gg % 2 == 0 else "T2")
                    pb = self.psT1 if gg % 2 == 0 else self.psT2
                    s_pe.inc(t.transpose(pb[:], self.att2p[:, ob, c, 128*(gg//ND):128*(gg//ND+1)],
                                         self.idt16[:]))
            @blk.vector
            def _(v):
                v = EW(v, self.s_dv)
                for half in range(2):
                    i = 2 * qc + half
                    for c in range(ND):
                        gg = half * ND + c
                        v.wait_ge(s_pe.h, trn[gg])
                        pb = self.psT1 if gg % 2 == 0 else self.psT2
                        v.tensor_copy(self.att_tok[:, i, 128*c:128*(c+1)], pb[:])
                    v.tensor_reduce(self.sums[:, i:i+1], self.att_tok[:, i, :], AX.X, ALU.add)
            assert s_dv.n == V0 + 2 * ND + 2
            @blk.scalar
            def _(a):
                a = EW(a, self.s_ac)
                for half in range(2):
                    i = 2 * qc + half
                    a.wait_ge(s_dv.h, V0 + (half + 1) * (ND + 1))
                    a.activation(self.sqscr[:], self.att_tok[:, i, :], AF.Square,
                                          bias=self.zero1[:], scale=1.0, accum_out=self.sumsq[:, i:i+1])
            self.bank_free["T1"] = ("dv", vevt[-2])
            self.bank_free["T2"] = ("dv", vevt[-1])
            self.att2_free[ob] = ("ac", s_ac.n)
        rdy = self.ln_combine(s_dv.n, s_ac.n)
        lds = [self.s_ld0, self.s_ld1]
        B0, B1 = lds[0].n, lds[1].n
        V1 = s_dv.n
        ldn = [[B0, B1][i % 2] + 16 * (i // 2 + 1) for i in range(NT)]
        src_t = self.x_h.rearrange("(n p) d -> n p d", p=128)
        x1t = lambda i: (self.x1a[:, i, :] if i < 8 else self.x1b[:, i - 8, :])
        @blk.sync
        def _(s):
            self.wait(s, self.xst_free)
            for i in range(NT):
                if i >= 2:
                    s.wait_ge(s_dv.h, V1 + 2 * (i - 1))
                lds[i % 2].inc(s.dma_start(out=self.xst[:, i % 2, :], in_=src_t[i]), 16)
        @blk.vector
        def _(v):
            v = EW(v, self.s_dv)
            v.wait_ge(s_dv.h, rdy)
            for i in range(NT):
                v.wait_ge(lds[i % 2].h, ldn[i])
                v.tensor_scalar(x1t(i), self.att_tok[:, i, :],
                                self.rstds[:, i:i+1], self.lnbias[:, i:i+1],
                                ALU.mult, ALU.add)
                v.tensor_tensor(x1t(i), x1t(i), self.xst[:, i % 2, :], ALU.add)
        self.x1_done = ("dv", s_dv.n)
        self.xst_free = self.x1_done

    def attn_unit(self, h, qc, ob):
        blk = self.blk
        s_pe, s_dv, s_ac = self.s_pe, self.s_dv, self.s_ac
        g, rb = h // 2, 64 * (h % 2)
        r64 = 64 if rb == 0 else 0
        ntk = 2 * (qc + 1)
        qsl = slice(PAGE * qc, PAGE * (qc + 1))
        P0, A0, V0 = s_pe.n, s_ac.n, s_dv.n
        smm = [P0 + j + 1 for j in range(ntk)]
        vstop = P0 + ntk + 1
        bca = P0 + ntk + 2
        expn = [A0 + j + 1 for j in range(ntk)]
        oev = A0 + ntk + 1
        rchain = V0 + 4
        ftt = V0 + 5
        first_edges = [self.ropeq_done, self.ropek_done, self.v_done] if (h == 0 and qc == 0) else []
        oedge = self.opage_free[ob] if h == 0 else None
        prev_vstop, prev_bca = self.u_prev["vstop"], self.u_prev["bca"]

        @blk.tensor
        def _(t):
            for e in first_edges:
                self.wait(t, e)
            for j in range(ntk):
                pb = self.psS1 if j % 2 == 0 else self.psS2
                if j >= 2:
                    t.wait_ge(s_ac.h, expn[j-2])
                else:
                    self.bank_wait(t, "S1" if j % 2 == 0 else "S2")
                s_pe.inc(t.matmul(pb[:], self.k0[rb:rb+64, g, 128*j:128*(j+1)],
                                  self.q0[rb:rb+64, g, qsl], start=True, stop=True))
            self.bank_wait(t, "O")
            self.bank_wait(t, "BC")
            ins = None
            for j in range(ntk):
                t.wait_ge(s_ac.h, expn[j])
                t.matmul(self.psO[rb:rb+64, :], self.vstore[:, j, 64*h:64*(h+1)],
                         self.ptile[:, j, :], start=(j == 0), stop=(j == ntk - 1),
                         tile_position=(0, rb))
                ins = t.matmul(self.psBC[r64:r64+1, :], self.onesbf[:, 0:1],
                               self.ptile[:, j, :], start=(j == 0), stop=(j == ntk - 1),
                               tile_position=(0, r64))
            s_pe.inc(ins)
            t.wait_ge(s_dv.h, rchain)
            self.bank_wait(t, "BC")
            t.matmul(self.psBC[rb:rb+64, :], self.onesbf[r64:r64+1, :],
                     self.rhl[r64:r64+1, 0, :], start=True, stop=False,
                     tile_position=(r64, rb))
            s_pe.inc(t.matmul(self.psBC[rb:rb+64, :], self.onesbf[r64:r64+1, :],
                              self.rhl[r64:r64+1, 1, :], start=False, stop=True,
                              tile_position=(r64, rb)))
        assert s_pe.n == bca

        @blk.scalar
        def _(a):
            a = EW(a, self.s_ac)
            for j in range(ntk):
                pb = self.psS1 if j % 2 == 0 else self.psS2
                a.wait_ge(s_pe.h, smm[j])
                if j == 0 and prev_vstop:
                    a.wait_ge(s_pe.h, prev_vstop)
                a.activation(self.ptile[:, j, :], pb[:], AF.Exp,
                                      bias=self.c30[:], scale=0.125)
            a.wait_ge(s_dv.h, rchain)
            a.activation(self.osc[rb:rb+64, :], self.psO[rb:rb+64, :], AF.Copy,
                                  bias=0.0, scale=1.0)
        assert s_ac.n == oev

        @blk.vector
        def _(v):
            v = EW(v, self.s_dv)
            v.wait_ge(s_pe.h, vstop)
            if prev_bca:
                v.wait_ge(s_pe.h, prev_bca)
            v.reciprocal(self.rrec[r64:r64+1, :], self.psBC[r64:r64+1, :])
            v.tensor_copy(self.rhl[r64:r64+1, 0, :], self.rrec[r64:r64+1, :])
            v.tensor_tensor(self.rlo32[r64:r64+1, :], self.rrec[r64:r64+1, :],
                            self.rhl[r64:r64+1, 0, :], ALU.subtract)
            v.tensor_copy(self.rhl[r64:r64+1, 1, :], self.rlo32[r64:r64+1, :])
            v.wait_ge(s_ac.h, oev)
            v.wait_ge(s_pe.h, bca)
            self.wait(v, oedge)
            v.tensor_tensor(self.opage[rb:rb+64, ob, g, :], self.osc[rb:rb+64, :],
                                     self.psBC[rb:rb+64, :], ALU.mult)
        assert s_dv.n == ftt

        self.bank_free["S1"] = ("ac", expn[-2] if ntk >= 2 else expn[-1])
        self.bank_free["S2"] = ("ac", expn[-1])
        self.bank_free["O"] = ("dv", ftt)
        self.bank_free["BC"] = ("dv", ftt)
        self.u_prev = dict(vstop=vstop, bca=bca)

    def ph_ffn_loads(self):
        blk = self.blk
        @blk.gpsimd
        def _(g):
            self.wait(g, self.vproj_pe)
            self.wait(g, self.kproj_pe)
            self.wait(g, self.x1_done)
            s_wf = self.s_wf
            s_wf.inc(g.dma_start(out=self.bd1f32, in_=self.bd1_h), 16)
            s_wf.inc(g.dma_start(out=self.bd2c32[:], in_=self.bd2c_h), 16)
            s_wf.inc(g.dma_start(out=self.bbasec[:], in_=self.bbase_h), 16)
            s_wf.inc(g.dma_start(out=self.bads[:], in_=self.badn_h), 16)
            s_wf.inc(g.dma_start(out=self.baup16, in_=self.baup_h), 16)
            s_wf.inc(g.dma_start(out=self.eexp16, in_=self.eexp_h), 16)
            s_wf.inc(g.dma_start(out=self.bbaser[:], in_=self.bbaser_h), 16)
        self.ffn_bias_ld = ("wf", self.s_wf.n)

    def ph_ln_sbuf(self):
        blk = self.blk
        s_dv, s_ac, s_pe = self.s_dv, self.s_ac, self.s_pe
        x1t = lambda i: (self.x1a[:, i, :] if i < 8 else self.x1b[:, i - 8, :])
        V0, A0 = s_dv.n, s_ac.n
        @blk.vector
        def _(v):
            v = EW(v, self.s_dv)
            self.wait(v, self.x1_done)
            v.memset(self.sumsq[:], 0.0)
            for i in range(NT):
                v.tensor_reduce(self.sums[:, i:i+1], x1t(i), AX.X, ALU.add)
        @blk.scalar
        def _(a):
            a = EW(a, self.s_ac)
            a.wait_ge(s_dv.h, V0 + 1)
            for i in range(NT):
                a.activation(self.sqscr[:], x1t(i), AF.Square,
                                      bias=self.zero1[:], scale=1.0, accum_out=self.sumsq[:, i:i+1])
        rdy = self.ln_combine(V0 + 1 + NT, A0 + NT)
        V1, P0 = s_dv.n, s_pe.n
        per = 2 + ND
        vxn = [V1 + i * per + 1 for i in range(NT)]
        vev = [[V1 + i * per + 2 + (c + 1) for c in range(ND)] for i in range(NT)]
        trn = [[P0 + i * ND + c + 1 for c in range(ND)] for i in range(NT)]
        @blk.tensor
        def _(t):
            self.bank_wait(t, "T1")
            self.bank_wait(t, "T2")
            for i in range(NT):
                t.wait_ge(s_dv.h, vxn[i])
                for c in range(ND):
                    if i * ND + c >= 2:
                        pi, pc = divmod(i * ND + c - 2, ND)
                        t.wait_ge(s_dv.h, vev[pi][pc])
                    pb = self.psT1 if c % 2 == 0 else self.psT2
                    s_pe.inc(t.transpose(pb[:], self.xnst[:, i % 2, 128*c:128*(c+1)], self.idt16[:]))
        @blk.vector
        def _(v):
            v = EW(v, self.s_dv)
            v.wait_ge(s_dv.h, rdy)
            for i in range(NT):
                if i >= 2:
                    v.wait_ge(s_pe.h, trn[i-2][-1])
                v.tensor_scalar(self.xnst[:, i % 2, :], x1t(i),
                                         self.means[:, i:i+1], self.rstds[:, i:i+1],
                                         ALU.subtract, ALU.mult)
                v.tensor_copy(self.xn2_tok[:, i, :], self.xnst[:, i % 2, :])
                for c in range(ND):
                    v.wait_ge(s_pe.h, trn[i][c])
                    pb = self.psT1 if c % 2 == 0 else self.psT2
                    v.tensor_copy(self.xn2T[:, c, 128*i:128*(i+1)], pb[:])
        assert s_dv.n == vev[-1][-1]
        self.bank_free["T1"] = ("dv", vev[-1][-2])
        self.bank_free["T2"] = ("dv", vev[-1][-1])
        self.ln3_done = ("dv", s_dv.n)

    def ph_gram(self):
        blk = self.blk
        s_pe, s_dv, s_ac = self.s_pe, self.s_dv, self.s_ac
        @blk.vector
        def _(v):
            v = EW(v, self.s_dv)
            self.wait(v, self.ln3_done)
            v.wait_ge(self.s_wf.h, self.ffn_bias_ld[1])
            v.memset(self.htok[:, :, 256:258], 1.0)
            v.memset(self.htok[:, :, 258:RA], 0.0)
            v.memset(self._wd2th[:, :, 256:RA], 0.0)
            v.memset(self._wd2tl[:, :, 256:RA], 0.0)
            v.memset(self.lacc[:], 0.0)
            v.tensor_scalar(self.iom[:], self.iota8[:], -99.0, None, ALU.add)
            v.tensor_copy(self.bd1f, self.bd1f32[:])
            v.tensor_tensor(self.bd1l[:], self.bd1f32[:], self.bd1f, ALU.subtract)
            v.tensor_copy(self.bd2ch[:], self.bd2c32[:])
            v.tensor_tensor(self.bd2cl[:], self.bd2c32[:], self.bd2ch[:], ALU.subtract)
        aug_init = s_dv.n

        PE_PER = NT + 6 + ND
        DV_PER = 71
        AC_PER = NT + 3
        PEB, ACB, DVB = s_pe.n, s_ac.n, s_dv.n
        pe_done = [PEB + PE_PER * (e + 1) for e in range(NE)]
        wld1 = [self.s_we0.n + 128 * (e + 1) for e in range(NE)]
        wld2 = [self.s_we1.n + 128 * (e + 1) for e in range(NE)]
        der1 = [DVB + DV_PER * e + 2 for e in range(NE)]
        der2 = [DVB + DV_PER * e + 6 for e in range(NE)]
        s1dotn = [[DVB + DV_PER * e + 6 + 3 * (rc + 1) for rc in range(3)] for e in range(NE)]
        xhdotn = [[DVB + DV_PER * e + 15 + 7 * (dm + 1) for dm in range(ND)] for e in range(NE)]
        relun = [[ACB + AC_PER * e + (i + 1) for i in range(NT)] for e in range(NE)]
        s1evn = [[ACB + AC_PER * e + NT + (rc + 1) for rc in range(3)] for e in range(NE)]
        htokn = [[PEB + PE_PER * e + (i + 1) for i in range(NT)] for e in range(NE)]
        s1mmn = [[PEB + PE_PER * e + NT + 2 * rc + 1 for rc in range(3)] for e in range(NE)]
        gmmn = [[PEB + PE_PER * e + NT + 2 * rc + 2 for rc in range(3)] for e in range(NE)]
        xhmmn = [[PEB + PE_PER * e + NT + 6 + (dm + 1) for dm in range(ND)] for e in range(NE)]

        @blk.gpsimd
        def _(g):
            g.wait_ge(s_dv.h, aug_init)
            for e in range(NE):
                if e > 0:
                    g.wait_ge(s_pe.h, pe_done[e-1])
                    g.wait_ge(s_dv.h, xhdotn[e-1][-1])
                for dc in range(ND):
                    self.s_we0.inc(g.dma_start(out=self.wstage[:, dc, :],
                                               in_=self.wd1_h[e, 128*dc:128*(dc+1), :]), 16)
                g.wait_ge(s_dv.h, der1[e])
                for dc in range(ND):
                    self.s_we1.inc(g.dma_start(out=self.wstage[:, dc, :],
                                               in_=self.wd2t_h[e, 128*dc:128*(dc+1), :]), 16)
        assert self.s_we1.n == wld2[-1]

        @blk.tensor
        def _(t):
            t.wait_ge(s_dv.h, aug_init)
            for e in range(NE):
                t.wait_ge(s_dv.h, der2[e])
                for i in range(NT):
                    pb = self.psS1 if i % 2 == 0 else self.psS2
                    if i >= 2:
                        t.wait_ge(s_ac.h, relun[e][i-2])
                    elif e > 0:
                        t.wait_ge(s_ac.h, relun[e-1][NT-2+i])
                    else:
                        self.bank_wait(t, "S1" if i % 2 == 0 else "S2")
                    for dc in range(ND):
                        t.matmul(pb[:], self.xn2T[:, dc, 128*i:128*(i+1)],
                                 self._wd1h[:, dc, :], start=(dc == 0), stop=False)
                    for dc in range(ND):
                        t.matmul(pb[:], self.xn2T[:, dc, 128*i:128*(i+1)],
                                 self._wd1l[:, dc, :], start=False, stop=False)
                    t.matmul(pb[:], self.ones512[0:1, 0:128],
                             self.bd1f[0:1, 256*e:256*(e+1)], start=False, stop=False)
                    s_pe.inc(t.matmul(pb[:], self.ones512[0:1, 0:128],
                                      self.bd1l[0:1, 256*e:256*(e+1)], start=False, stop=True))
                t.wait_ge(s_ac.h, relun[e][NT-1])
                for rc in range(3):
                    mo, mw = MC[rc]
                    if rc > 0:
                        t.wait_ge(s_ac.h, s1evn[e][rc-1])
                    elif e > 0:
                        t.wait_ge(s_dv.h, xhdotn[e-1][ND-2])
                    else:
                        self.bank_wait(t, "A")
                    ins = None
                    for i in range(NT):
                        ins = t.matmul(self.psF1[0:mw, :], self.htok[:, i, mo:mo+mw],
                                       self.htok[:, i, :], start=(i == 0), stop=(i == NT - 1))
                    s_pe.inc(ins)
                    if rc > 0:
                        t.wait_ge(s_dv.h, s1dotn[e][rc-1])
                    elif e > 0:
                        t.wait_ge(s_dv.h, xhdotn[e-1][ND-1])
                    else:
                        self.bank_wait(t, "B")
                    ins = None
                    for wa in (self._wd2th, self._wd2tl):
                        for wb in (self._wd2th, self._wd2tl):
                            for dc in range(ND):
                                ins = t.matmul(self.psF2[0:mw, :], wa[:, dc, mo:mo+mw],
                                               wb[:, dc, :],
                                               start=(wa is self._wd2th and wb is self._wd2th and dc == 0),
                                               stop=(wa is self._wd2tl and wb is self._wd2tl and dc == ND - 1))
                    s_pe.inc(ins)
                for dm in range(ND):
                    pf = self.psF1 if dm % 2 == 0 else self.psF2
                    if dm == 0:
                        t.wait_ge(s_ac.h, s1evn[e][2])
                    elif dm == 1:
                        t.wait_ge(s_dv.h, s1dotn[e][2])
                    else:
                        t.wait_ge(s_dv.h, xhdotn[e][dm-2])
                    ins = None
                    for i in range(NT):
                        ins = t.matmul(pf[:], self.xn2_tok[:, i, 128*dm:128*(dm+1)],
                                       self.htok[:, i, :], start=(i == 0), stop=(i == NT - 1))
                    s_pe.inc(ins)
        assert s_pe.n == pe_done[-1], (s_pe.n, pe_done[-1])

        @blk.scalar
        def _(a):
            a = EW(a, self.s_ac)
            for e in range(NE):
                for i in range(NT):
                    pb = self.psS1 if i % 2 == 0 else self.psS2
                    a.wait_ge(s_pe.h, htokn[e][i])
                    if i == 0 and e > 0:
                        a.wait_ge(s_pe.h, xhmmn[e-1][ND-1])
                    a.activation(self.htok[:, i, 0:256], pb[:], AF.Relu,
                                 bias=self.zero1[:], scale=1.0)
                for rc in range(3):
                    mo, mw = MC[rc]
                    a.wait_ge(s_pe.h, s1mmn[e][rc])
                    if rc == 0 and e > 0:
                        a.wait_ge(s_dv.h, s1dotn[e-1][2])
                    elif rc > 0:
                        a.wait_ge(s_dv.h, s1dotn[e][rc-1])
                    a.activation(self.gsc[0:mw, :], self.psF1[0:mw, :], AF.Copy,
                                 bias=0.0, scale=1.0)
        assert s_ac.n == ACB + AC_PER * NE

        @blk.vector
        def _(v):
            v = EW(v, self.s_dv)
            for e in range(NE):
                v.wait_ge(self.s_we0.h, wld1[e])
                v.tensor_copy(self._wd1h[:, :, :], self.wstage[:, :, :])
                v.tensor_tensor(self._wd1l[:, :, :], self.wstage[:, :, :],
                                self._wd1h[:, :, :], ALU.subtract)
                assert s_dv.n == der1[e]
                v.wait_ge(self.s_we1.h, wld2[e])
                v.tensor_copy(self._wd2th[:, :, 0:256], self.wstage[:, :, :])
                v.tensor_tensor(self._wd2tl[:, :, 0:256], self.wstage[:, :, :],
                                self._wd2th[:, :, 0:256], ALU.subtract)
                v.tensor_copy(self._wd2th[:, :, 256:257], self.bd2ch[:, e, :].unsqueeze(2))
                v.tensor_copy(self._wd2tl[:, :, 257:258], self.bd2cl[:, e, :].unsqueeze(2))
                assert s_dv.n == der2[e]
                for rc in range(3):
                    mo, mw = MC[rc]
                    v.wait_ge(s_ac.h, s1evn[e][rc])
                    v.wait_ge(s_pe.h, gmmn[e][rc])
                    v.tensor_tensor(self.gsc2[0:mw, :], self.gsc[0:mw, :], self.psF2[0:mw, :], ALU.mult)
                    v.tensor_reduce(self.vart[0:mw, 0:1], self.gsc2[0:mw, :], AX.X, ALU.add)
                    v.tensor_tensor(self.lacc[0:mw, e:e+1], self.lacc[0:mw, e:e+1],
                                    self.vart[0:mw, 0:1], ALU.add)
                    assert s_dv.n == s1dotn[e][rc]
                for dm in range(ND):
                    pf = self.psF1 if dm % 2 == 0 else self.psF2
                    v.wait_ge(s_pe.h, xhmmn[e][dm])
                    v.tensor_tensor(self.gsc2[:, :], pf[:], self._wd2th[:, dm, :], ALU.mult)
                    v.tensor_reduce(self.vart[:, 0:1], self.gsc2[:, :], AX.X, ALU.add)
                    v.tensor_tensor(self.gsc2[:, :], pf[:], self._wd2tl[:, dm, :], ALU.mult)
                    v.tensor_reduce(self.vart[:, 1:2], self.gsc2[:, :], AX.X, ALU.add)
                    v.tensor_tensor(self.vart[:, 2:3], self.vart[:, 0:1], self.vart[:, 1:2], ALU.add)
                    v.tensor_scalar(self.vart[:, 3:4], self.vart[:, 2:3], -2.0, None, ALU.mult)
                    v.tensor_tensor(self.lacc[:, e:e+1], self.lacc[:, e:e+1],
                                    self.vart[:, 3:4], ALU.add)
                    assert s_dv.n == xhdotn[e][dm]
        assert s_dv.n == DVB + DV_PER * NE
        self.gram_done = ("dv", s_dv.n)
        self.gram_pe_done = pe_done[-1]
        self.gram_xh_last = xhdotn[-1]
        self.bank_free["S1"] = ("ac", relun[-1][NT-2])
        self.bank_free["S2"] = ("ac", relun[-1][NT-1])
        self.bank_free["A"] = ("dv", xhdotn[-1][ND-2])
        self.bank_free["B"] = ("dv", xhdotn[-1][ND-1])

    def ph_argmin(self):
        blk = self.blk
        s_pe, s_dv = self.s_pe, self.s_dv
        @blk.tensor
        def _(t):
            self.wait(t, self.gram_done)
            self.bank_wait(t, "A")
            s_pe.inc(t.matmul(self.psF1[0:1, 0:8], self.onesf32[:], self.lacc[:],
                              start=True, stop=True))
        n_lred = s_pe.n
        @blk.vector
        def _(v):
            v = EW(v, self.s_dv)
            v.wait_ge(s_pe.h, n_lred)
            v.tensor_copy(self.lfin[:], self.psF1[0:1, 0:8])
            v.tensor_reduce(self.rmin[:], self.lfin[:], AX.X, ALU.min)
            v.wait_ge(s_dv.h, s_dv.n)
            v.tensor_scalar(self.eqrow[:], self.lfin[:], self.rmin[:], None, ALU.is_equal)
            v.tensor_tensor(self.zrow[:], self.eqrow[:], self.iom[:], ALU.mult)
            v.tensor_scalar(self.zrow[:], self.zrow[:], 99.0, None, ALU.add)
            v.tensor_reduce(self.idxs[:], self.zrow[:], AX.X, ALU.min)
            v.wait_ge(s_dv.h, s_dv.n)
            v.tensor_scalar(self.ohrow[:], self.iota8[:], self.idxs[:], None, ALU.is_equal)
        n_ohrow = s_dv.n
        @blk.tensor
        def _(t):
            t.wait_ge(s_dv.h, n_ohrow)
            self.bank_wait(t, "B")
            s_pe.inc(t.matmul(self.psF2[0:8, 0:1], self.ohrow[:], self.onesf32[0:1, :],
                              start=True, stop=True))
        n_ohcol = s_pe.n
        @blk.vector
        def _(v):
            v = EW(v, self.s_dv)
            v.wait_ge(s_pe.h, n_ohcol)
            v.tensor_copy(self.onec[:], self.psF2[0:8, 0:1])
        n_onec = s_dv.n
        @blk.tensor
        def _(t):
            t.wait_ge(s_dv.h, n_onec)
            ins = None
            for rc in range(4):
                ins = t.matmul(self.psF1[:, rc:rc+1], self.eexp16[0:8, 128*rc:128*(rc+1)],
                               self.onec[:], start=True, stop=True)
            s_pe.inc(ins)
        n_ohexp = s_pe.n
        @blk.vector
        def _(v):
            v = EW(v, self.s_dv)
            v.wait_ge(s_pe.h, n_ohexp)
            v.tensor_copy(self.ohexp[:], self.psF1[:, 0:4])
            v.tensor_tensor(self.badm[:], self.bads[:], self.ohexp[:], ALU.mult)
        self.route_dv = s_dv.n
        PB, VB = s_pe.n, s_dv.n
        basen = [PB + m + 1 for m in range(ND)]
        bn = [VB + 2 * (m + 1) for m in range(ND)]
        @blk.tensor
        def _(t):
            t.wait_ge(s_dv.h, self.route_dv)
            for m in range(ND):
                pf = self.psF1 if m % 2 == 0 else self.psF2
                if m >= 2:
                    t.wait_ge(s_dv.h, bn[m-2])
                s_pe.inc(t.matmul(pf[0:1, 0:128], self.onec[:], self.baup16[0:8, 128*m:128*(m+1)],
                                  start=True, stop=True))
        @blk.vector
        def _(v):
            v = EW(v, self.s_dv)
            for m in range(ND):
                pf = self.psF1 if m % 2 == 0 else self.psF2
                v.wait_ge(s_pe.h, basen[m])
                v.tensor_copy(self.basel[0:1, m, :], pf[0:1, 0:128])
                v.tensor_tensor(self.basel[0:1, m, :], self.basel[0:1, m, :],
                                         self.bbaser[0:1, 128*m:128*(m+1)], ALU.add)
        self.argmin_done = ("dv", s_dv.n)
        self.bank_free["A"] = ("dv", bn[ND-2])
        self.bank_free["B"] = ("dv", bn[ND-1])

    def ph_adapter_base(self):
        blk = self.blk
        s_pe, s_dv, s_ac = self.s_pe, self.s_dv, self.s_ac
        wbslab = {}
        cmm_pre = {}
        PEB0 = s_pe.n
        for tc in range(TC):
            for m in range(ND):
                cmm_pre[(tc, m)] = PEB0 + tc * 12 + 4 + m + 1
        @blk.gpsimd
        def _(g):
            g.wait_ge(s_pe.h, self.gram_pe_done)
            g.wait_ge(s_dv.h, self.gram_xh_last[-1])
            for dc in range(ND):
                self.s_wa.inc(g.dma_start(out=self._wad[:, dc, :], in_=self.wadn_h[128*dc:128*(dc+1), :]), 16)
            for kc in range(4):
                self.s_wa.inc(g.dma_start(out=self._wau[:, kc, :], in_=self.waup_h[128*kc:128*(kc+1), :]), 16)
        wad_ld = ("wa", self.s_wa.n)
        wbs = [self.s_wb0, self.s_wb1]
        @blk.gpsimd
        def _(g):
            g.wait_ge(self.s_ac.h, self.q_done[1])
            gi = 0
            for tc in range(TC):
                for m in range(ND):
                    if gi >= 2:
                        prev = (gi - 2) // ND, (gi - 2) % ND
                        g.wait_ge(s_pe.h, cmm_pre[prev])
                    for kc in range(ND):
                        wbs[gi % 2].inc(g.dma_start(out=self.wqa[:, gi % 2, kc, :],
                                                    in_=self.wbase_h[128*kc:128*(kc+1), 128*m:128*(m+1)]), 16)
                    wbslab[(tc, m)] = wbs[gi % 2].n
                    gi += 1

        PEB, ACB = s_pe.n, s_ac.n
        admm = {}; adev = {}; cmm = {}; cev = {}
        for tc in range(TC):
            for rc in range(4):
                admm[(tc, rc)] = PEB + tc * 12 + rc + 1
                adev[(tc, rc)] = ACB + tc * 12 + rc + 1
            for m in range(ND):
                cmm[(tc, m)] = PEB + tc * 12 + 4 + m + 1
                cev[(tc, m)] = ACB + tc * 12 + 4 + m + 1
        @blk.tensor
        def _(t):
            self.wait(t, wad_ld)
            self.wait(t, self.argmin_done)
            for tc in range(TC):
                tsl = slice(512 * tc, 512 * (tc + 1))
                for rc in range(4):
                    pb = self.psA if rc % 2 == 0 else self.psB
                    if tc == 0 and rc < 2:
                        self.bank_wait(t, "A" if rc % 2 == 0 else "B")
                    else:
                        prev = (tc, rc - 2) if rc >= 2 else (tc - 1, ND - 2 + rc)
                        t.wait_ge(s_ac.h, adev[prev] if rc >= 2 else cev[(tc - 1, ND - 2 + rc)])
                    ins = None
                    for dc in range(ND):
                        ins = t.matmul(pb[:], self._wad[:, dc, 128*rc:128*(rc+1)],
                                       self.xn2T[:, dc, tsl], start=(dc == 0), stop=(dc == ND - 1))
                    s_pe.inc(ins)
                for m in range(ND):
                    pb = self.psA if m % 2 == 0 else self.psB
                    prev_ev = adev[(tc, 2 + (m % 2))] if m < 2 else cev[(tc, m - 2)]
                    t.wait_ge(s_ac.h, prev_ev)
                    t.wait_ge(s_ac.h, adev[(tc, 3)])
                    t.wait_ge(wbs[(tc * ND + m) % 2].h, wbslab[(tc, m)])
                    gi2 = tc * ND + m
                    for dc in range(ND):
                        t.matmul(pb[:], self.wqa[:, gi2 % 2, kc2 := dc, :][:, :] if False else self.wqa[:, gi2 % 2, dc, :],
                                 self.xn2T[:, dc, tsl], start=(dc == 0), stop=False)
                    for kc in range(4):
                        t.matmul(pb[:], self._wau[:, kc, 128*m:128*(m+1)],
                                 self.achunk[:, tc % 2, kc, :], start=False, stop=False)
                    s_pe.inc(t.matmul(pb[:], self.basel[0:1, m, :], self.ones512[:],
                                      start=False, stop=True, tile_position=(0, 0)))
        @blk.scalar
        def _(a):
            a = EW(a, self.s_ac)
            for tc in range(TC):
                tsl = slice(512 * tc, 512 * (tc + 1))
                for rc in range(4):
                    pb = self.psA if rc % 2 == 0 else self.psB
                    a.wait_ge(s_pe.h, admm[(tc, rc)])
                    a.activation(self.achunk[:, tc % 2, rc, :], pb[:], AF.Relu,
                                          bias=self.badm[:, rc:rc+1],
                                          scale=self.ohexp[:, rc:rc+1])
                for m in range(ND):
                    pb = self.psA if m % 2 == 0 else self.psB
                    a.wait_ge(s_pe.h, cmm[(tc, m)])
                    a.activation(self.ffn_fm[:, m, tsl], pb[:], AF.Copy,
                                          bias=0.0, scale=1.0)
        assert s_ac.n == ACB + TC * 12
        assert s_pe.n == PEB + TC * 12
        self.adapter_done = ("ac", s_ac.n)
        self.adapter_pe = ("pe", s_pe.n)
        self.bank_free["A"] = ("ac", cev[(TC-1, ND-2)])
        self.bank_free["B"] = ("ac", cev[(TC-1, ND-1)])

    def ph_ln4_apply_store(self):
        blk = self.blk
        s_pe, s_dv, s_ac, s_st = self.s_pe, self.s_dv, self.s_ac, self.s_st
        P0, V0, A0 = s_pe.n, s_dv.n, s_ac.n
        trn = [[P0 + i * ND + c + 1 for c in range(ND)] for i in range(NT)]
        per = ND + 1
        vev = [[V0 + i * per + (c + 1) for c in range(ND)] for i in range(NT)]
        vred = [V0 + i * per + ND + 1 for i in range(NT)]
        asq = [A0 + i + 1 for i in range(NT)]
        @blk.tensor
        def _(t):
            self.wait(t, self.adapter_done)
            self.wait(t, self.adapter_pe)
            self.bank_wait(t, "T1")
            self.bank_wait(t, "T2")
            for i in range(NT):
                for c in range(ND):
                    if i * ND + c >= 2:
                        pi, pc = divmod(i * ND + c - 2, ND)
                        t.wait_ge(s_dv.h, vev[pi][pc])
                    pb = self.psT1 if c % 2 == 0 else self.psT2
                    s_pe.inc(t.transpose(pb[:], self.ffn_fm[:, c, 128*i:128*(i+1)], self.idt16[:]))
        @blk.vector
        def _(v):
            v = EW(v, self.s_dv)
            for i in range(NT):
                for c in range(ND):
                    v.wait_ge(s_pe.h, trn[i][c])
                    pb = self.psT1 if c % 2 == 0 else self.psT2
                    v.tensor_copy(self.ffn_tok[:, i, 128*c:128*(c+1)], pb[:])
                v.tensor_reduce(self.sums[:, i:i+1], self.ffn_tok[:, i, :], AX.X, ALU.add)
        zz = s_dv.n
        @blk.vector
        def _(v):
            v = EW(v, self.s_dv)
            v.memset(self.sumsq[:], 0.0)
        zz = s_dv.n
        @blk.scalar
        def _(a):
            a = EW(a, self.s_ac)
            a.wait_ge(s_dv.h, zz)
            for i in range(NT):
                a.wait_ge(s_dv.h, vred[i])
                a.activation(self.sqscr[:], self.ffn_tok[:, i, :], AF.Square,
                                      bias=self.zero1[:], scale=1.0, accum_out=self.sumsq[:, i:i+1])
        rdy = self.ln_combine(vred[-1], asq[-1])
        x1t = lambda i: (self.x1a[:, i, :] if i < 8 else self.x1b[:, i - 8, :])
        V1 = s_dv.n
        appn = [V1 + 2 * (i + 1) for i in range(NT)]
        @blk.vector
        def _(v):
            v = EW(v, self.s_dv)
            v.wait_ge(s_dv.h, rdy)
            for i in range(NT):
                v.tensor_scalar(self.xst[:, 0, :], self.ffn_tok[:, i, :],
                                self.rstds[:, i:i+1], self.lnbias[:, i:i+1],
                                ALU.mult, ALU.add)
                v.tensor_tensor(x1t(i), x1t(i), self.xst[:, 0, :], ALU.add)
        @blk.sync
        def _(s):
            s.wait_ge(s_dv.h, appn[7])
            s_st.inc(s.dma_start(out=self.out_h.rearrange("(i p) d -> p i d", p=128)[:, 0:8, :],
                                 in_=self.x1a[:, :, :]), 16)
            s.wait_ge(s_dv.h, appn[15])
            s_st.inc(s.dma_start(out=self.out_h.rearrange("(i p) d -> p i d", p=128)[:, 8:16, :],
                                 in_=self.x1b[:, :, :]), 16)
            s.wait_ge(s_st.h, s_st.n)
        self.store_done = ("st", s_st.n)

    def finalize_dbg(self):
        blk, s_st = self.blk, self.s_st
        shapes = {"q": ([D, T], F16), "k": ([D, T], F16), "lat": ([512, T], F16),
                  "v": ([T, D], BF16), "xnt": ([D, T], F16),
                  "x1": ([128, NT * D], F32), "losses": ([1, 8], F32),
                  "onehot": ([8, 1], F16)}
        for nm in self.dbg:
            if nm not in self.dbg_h:
                s, dt = shapes[nm]
                self.dbg_h[nm] = self.nc.dram_tensor("dbg_" + nm, s, dt, kind="ExternalOutput").ap()
        waits = [(sm.h, sm.n) for sm in
                 [self.s_pe, self.s_dv, self.s_ac, self.s_ld0, self.s_ld1,
                  self.s_wt, self.s_wq0, self.s_wq1, self.s_wo, self.s_wf,
                  self.s_we0, self.s_we1, self.s_wb0, self.s_wb1, self.s_wa]
                 if sm.n > 0]
        @blk.sync
        def _(s):
            for h, n in waits:
                s.wait_ge(h, n)
            for name, hap in self.dbg_h.items():
                if name in ("q", "k", "lat", "xnt"):
                    src = {"q": self.q0, "k": self.k0, "lat": self.latT, "xnt": self.xnT}[name]
                    s_st.inc(s.dma_start(out=hap.rearrange("(c p) t -> p c t", p=128),
                                         in_=src), 16)
                elif name == "v":
                    s_st.inc(s.dma_start(out=hap.rearrange("(i p) x -> p i x", p=128),
                                         in_=self.vstore), 16)
                elif name == "x1":
                    s_st.inc(s.dma_start(out=hap[:, 0:NT*D//2],
                                         in_=self.x1a.rearrange("p i d -> p (i d)")), 16)
                    s_st.inc(s.dma_start(out=hap[:, NT*D//2:],
                                         in_=self.x1b.rearrange("p i d -> p (i d)")), 16)
                elif name == "atok":
                    s_st.inc(s.dma_start(out=hap, in_=self.ar1[:, :]), 16)
                elif name == "losses":
                    s_st.inc(s.dma_start(out=hap, in_=self.lfin[:]), 16)
                elif name == "onehot":
                    s_st.inc(s.dma_start(out=hap, in_=self.onec[:]), 16)
            s.wait_ge(s_st.h, self.s_st.n)


def host_constants():
    c = {}
    c["ident16"] = np.eye(128, dtype=np.float16)
    sw = np.zeros((128, 128), dtype=np.float16)
    for k in range(128):
        sw[k, k ^ 1] = 1.0
    c["swap16"] = sw
    inv = 1.0 / 10000.0 ** (np.arange(0, HD, 2, dtype=np.float32) / HD)
    ang = np.outer(np.arange(T, dtype=np.float32), inv)
    cos = np.cos(ang).astype(np.float32)
    sin = np.sin(ang).astype(np.float32)
    A = np.zeros((128, T), dtype=np.float32)
    B = np.zeros((128, T), dtype=np.float32)
    for p in range(128):
        i = (p % HD) // 2
        A[p, :] = cos[:, i]
        B[p, :] = (-sin[:, i]) if (p % 2 == 0) else sin[:, i]
    c["ropeA"] = A.astype(np.float16)
    c["ropeB"] = B.astype(np.float16)
    c["iota8"] = np.arange(8, dtype=np.float32).reshape(1, 8)
    E = np.zeros((8, 512), dtype=np.float16)
    for e in range(8):
        E[e, 64*e:64*(e+1)] = 1.0
    c["Eexp"] = E
    Es = np.zeros((16, 16 * 128), dtype=np.float16)
    for e in range(16):
        Es[e, 128*e:128*(e+1)] = 1.0
    c["eselk16"] = Es
    return c


def l0_inmap(x_batch, ins, consts):
    return {
        "x": x_batch,
        "wq": ins["L0_attn_wq"], "w_down": ins["L0_attn_w_down"],
        "wk_up": ins["L0_attn_wk_up"], "wv_up": ins["L0_attn_wv_up"],
        "wo": ins["L0_attn_wo"],
        "ident16": consts["ident16"], "swap16": consts["swap16"],
        "ropeA": consts["ropeA"], "ropeB": consts["ropeB"],
        "W_base": ins["L0_ffn_W_base"],
        "b_basec": np.ascontiguousarray(ins["L0_ffn_b_base"].reshape(8, 128).T),
        "b_baser": ins["L0_ffn_b_base"].reshape(1, 1024),
        "Wd1": ins["L0_ffn_Wd1"],
        "bd1f": np.ascontiguousarray(ins["L0_ffn_bd1"].reshape(1, 2048)),
        "Wd2T": np.ascontiguousarray(ins["L0_ffn_Wd2"].transpose(0, 2, 1)),
        "bd2c": np.ascontiguousarray(ins["L0_ffn_bd2"].reshape(8, 8, 128).transpose(2, 0, 1)),
        "Wad_stack": np.ascontiguousarray(ins["L0_ffn_Wa_down"].transpose(1, 0, 2).reshape(1024, 512)),
        "bad_stack": np.ascontiguousarray(ins["L0_ffn_ba_down"].reshape(4, 128).T),
        "Wau_stack": np.ascontiguousarray(ins["L0_ffn_Wa_up"].reshape(512, 1024)),
        "ba_up": ins["L0_ffn_ba_up"],
        "iota8": consts["iota8"], "Eexp": consts["Eexp"],
    }


class L2B:

    def __init__(self, dbg=(), upto="all"):
        self.dbg = set(dbg)
        self.upto = upto
        self.nc = nc = bass.Bass("TRN2", target_bir_lowering=False)
        self.es = es = ExitStack()
        self.dbg_h = {}

        dram = lambda n, s, dt, k: nc.dram_tensor(n, s, dt, kind=k).ap()
        self.x_h = dram("xB", [T, D], F32, "ExternalInput")
        self.wqkv_h = dram("WqkvT", [D, 3 * D], F32, "ExternalInput")
        self.wout_h = dram("WoutT", [D, D], F32, "ExternalInput")
        self.selw_h = dram("selWT", [D, 16], F32, "ExternalInput")
        self.keys_h = dram("keysT", [D, 2 * D], F32, "ExternalInput")
        self.vals_h = dram("valsS", [2 * D, D], F32, "ExternalInput")
        self.idt_h = dram("ident16", [128, 128], F16, "ExternalInput")
        self.eselk_h = dram("eselk16", [16, 16 * 128], F16, "ExternalInput")
        self.out_h = dram("y", [T, D], F32, "ExternalOutput")

        sb = lambda *a: es.enter_context(nc.sbuf_tensor(*a))
        pst = lambda *a: es.enter_context(nc.psum_tensor(*a))

        self.idt16 = sb("idt16", [128, 128], F16)
        self.eps_t = sb("eps_t", [128, 1], F32)
        self.c30 = sb("c30", [128, 1], F32)
        self.zero1 = sb("zero1", [128, 1], F32)
        self.xst = sb("xst", [128, 2, D], F32)
        self.xnst = sb("xnst", [128, 2, D], F16)
        self.sums = sb("sums", [128, NT], F32)
        self.sumsq = sb("sumsq", [128, NT], F32)
        self.means = sb("means", [128, NT], F32)
        self.rstds = sb("rstds", [128, NT], F32)
        self.vart = sb("vart", [128, NT], F32)
        self.lnbias = sb("lnbias", [128, NT], F32)
        self.sqscr = sb("sqscr", [128, D], F16)

        self.scr = sb("scr", [128, 5120], F32)
        scr16 = self.scr.bitcast(F16)
        self.Sc = self.scr[:, 0:2048].rearrange("p (a x) -> p a x", a=2)
        self.sumE = self.scr[:, 2048:2304].rearrange("p (a x) -> p a x", a=2)
        self.recE = self.scr[:, 2304:2560].rearrange("p (a x) -> p a x", a=2)
        self.P = scr16[:, 5120:7168].rearrange("p (a x) -> p a x", a=2)
        self.prod = scr16[:, 7168:8192]
        self.page = scr16[:, 0:4096].rearrange("p (c w) -> p c w", w=512)
        self.tokst = scr16[:, 4096:6144].rearrange("p (a x) -> p a x", a=2)


        self.wqkv = sb("wqkv", [128, 2, ND, 512], F16)
        self.wout = sb("wout", [128, 2, ND, 128], F16)
        self.selw = sb("selw", [128, ND, 16], F16)
        self.keysw = sb("keysw", [128, 2, ND, 128], F16)
        self.valsw = sb("valsw", [128, 2, 16, 128], F16)
        self.selfm = sb("selfm", [16, T], F16)
        self.eselk = sb("eselk", [16, 16 * 128], F16)

        self.ar1 = sb("l2ar1", [128, 16384], F16)
        self.ar2 = sb("l2ar2", [128, 16384], F16)
        self.ar3 = sb("l2ar3", [128, 16384], F16)
        self.ar4 = sb("l2ar4", [128, 16384], F16)
        self.xn5T = self.ar1.rearrange("p (c t) -> p c t", t=T)
        self.attnT = self.ar1.rearrange("p (c t) -> p c t", t=T)
        self.h1 = self.ar1.rearrange("p (c w) -> p c w", w=1024)
        self.q_tok = self.ar2.rearrange("p (i d) -> p i d", d=D)
        self.acc = self.ar2.rearrange("p (i d) -> p i d", d=D)
        self.att2_tok = self.ar2.rearrange("p (i d) -> p i d", d=D)
        self.xn7T = self.ar2.rearrange("p (c t) -> p c t", t=T)
        self.k_tok = self.ar3.rearrange("p (i d) -> p i d", d=D)
        self.x3a = self.ar3.bitcast(F32).rearrange("p (i d) -> p i d", d=D)
        self.v_tok = self.ar4.rearrange("p (i d) -> p i d", d=D)
        self.att2fm = self.ar4.rearrange("p (c t) -> p c t", t=T)
        self.x3b = self.ar4.bitcast(F32).rearrange("p (i d) -> p i d", d=D)
        self.ffn2_d = nc.dram_tensor("ffn2scr", [T, D], F16, kind="Internal").ap()

        self.psA = pst("psA", [128, 512], F32)
        self.psB = pst("psB", [128, 512], F32)
        self.psT1 = pst("psT1", [128, 128], F16)
        self.psT2 = pst("psT2", [128, 128], F16)
        self.psS = pst("psS", [16, 512], F32)
        self.psBC1 = pst("psBC1", [128, 512], F32)
        self.psBC2 = pst("psBC2", [128, 512], F32)

        self.s_ld = Sem(es, nc, "s_ld")
        self.s_ld0 = Sem(es, nc, "s_ld0")
        self.s_ld1 = Sem(es, nc, "s_ld1")
        self.s_w0 = Sem(es, nc, "s_w0")
        self.s_w1 = Sem(es, nc, "s_w1")
        self.s_w2 = Sem(es, nc, "s_w2")
        self.s_pe = Sem(es, nc, "s_pe")
        self.s_dv = Sem(es, nc, "s_dv")
        self.s_ac = Sem(es, nc, "s_ac")
        self.s_st = Sem(es, nc, "s_st")
        self.s_st0 = Sem(es, nc, "s_st0")
        self.s_st1 = Sem(es, nc, "s_st1")
        self.sem = {"pe": self.s_pe, "dv": self.s_dv, "ac": self.s_ac,
                    "ld": self.s_ld, "st": self.s_st, "ld0": self.s_ld0,
                    "ld1": self.s_ld1, "w0": self.s_w0, "w1": self.s_w1, "w2": self.s_w2,
                    "st0": self.s_st0, "st1": self.s_st1}
        self.bank_free = {}
        self.blk = es.enter_context(nc.Block())
        self.build()

    def wait(self, eng, edge):
        if edge is None:
            return
        sname, cnt = edge
        if cnt > 0:
            eng.wait_ge(self.sem[sname].h, cnt)

    def bank_wait(self, eng, bank):
        self.wait(eng, self.bank_free.get(bank))

    def build(self):
        blk = self.blk
        s_ld = self.s_ld
        @blk.sync
        def _(s):
            s_ld.inc(s.dma_start(out=self.idt16[:], in_=self.idt_h), 16)
            s_ld.inc(s.dma_start(out=self.eselk[:], in_=self.eselk_h), 16)
        self.ld_const = ("ld", s_ld.n)
        @blk.vector
        def _(v):
            v = EW(v, self.s_dv)
            v.memset(self.eps_t[:], EPS)
            v.memset(self.zero1[:], 0.0)
            v.memset(self.c30[:], -EXPC)
        self.init_dv = ("dv", self.s_dv.n)

        self.ph_ln5()
        self.ph_qkv()
        if "q2" in self.dbg:
            self.dbg_h["q2"] = self.nc.dram_tensor("dbg_q2", [128, NT * D], F16, kind="ExternalOutput").ap()
        if "v2" in self.dbg:
            self.dbg_h["v2"] = self.nc.dram_tensor("dbg_v2", [128, NT * D], F16, kind="ExternalOutput").ap()
        if self.upto == "qkv":
            self.finalize_dbg()
            return
        self.ph_scores()
        if "P2" in self.dbg:
            self.dbg_h["P2"] = self.nc.dram_tensor("dbg_P2", [128, 2048], F16, kind="ExternalOutput").ap()
        if self.upto == "scores":
            self.finalize_dbg()
            return
        self.ph_attnv()
        if "acc2" in self.dbg:
            self.dbg_h["acc2"] = self.nc.dram_tensor("dbg_acc2", [128, NT * D], F16, kind="ExternalOutput").ap()
        if self.upto == "attnv":
            self.finalize_dbg()
            return
        self.ph_outproj_ln6()
        if "x3" in self.dbg:
            self.dbg_h["x3"] = self.nc.dram_tensor("dbg_x3", [128, NT * D], F32, kind="ExternalOutput").ap()
        if self.upto == "x3":
            self.finalize_dbg()
            return
        self.ph_ffn2()
        if self.upto != "all":
            self.finalize_dbg()

    def ph_ln5(self):
        blk = self.blk
        s_dv, s_ac, s_pe = self.s_dv, self.s_ac, self.s_pe
        src_t = self.x_h.rearrange("(n p) d -> n p d", p=128)
        lds = [self.s_ld0, self.s_ld1]
        B0, B1 = lds[0].n, lds[1].n
        V0, A0 = s_dv.n, s_ac.n
        ldn = [[B0, B1][i % 2] + 16 * (i // 2 + 1) for i in range(NT)]
        vn = [V0 + 1 + (i + 1) for i in range(NT)]
        an = [A0 + (i + 1) for i in range(NT)]
        @blk.sync
        def _(s):
            for i in range(NT):
                if i >= 2:
                    s.wait_ge(s_dv.h, vn[i-2])
                    s.wait_ge(s_ac.h, an[i-2])
                lds[i % 2].inc(s.dma_start(out=self.xst[:, i % 2, :], in_=src_t[i]), 16)
        @blk.vector
        def _(v):
            v = EW(v, self.s_dv)
            self.wait(v, self.init_dv)
            v.memset(self.sumsq[:], 0.0)
            for i in range(NT):
                v.wait_ge(lds[i % 2].h, ldn[i])
                v.tensor_reduce(self.sums[:, i:i+1], self.xst[:, i % 2, :], AX.X, ALU.add)
        @blk.scalar
        def _(a):
            a = EW(a, self.s_ac)
            a.wait_ge(s_dv.h, V0 + 1)
            for i in range(NT):
                a.wait_ge(lds[i % 2].h, ldn[i])
                a.activation(self.sqscr[:], self.xst[:, i % 2, :], AF.Square,
                             bias=self.zero1[:], scale=1.0, accum_out=self.sumsq[:, i:i+1])
        rdy = self.ln_combine(vn[-1], an[-1])
        C0, C1 = lds[0].n, lds[1].n
        V1, P0 = s_dv.n, s_pe.n
        ldn2 = [[C0, C1][i % 2] + 16 * (i // 2 + 1) for i in range(NT)]
        per = 1 + ND
        vxn = [V1 + i * per + 1 for i in range(NT)]
        vev = [[V1 + i * per + 1 + (c + 1) for c in range(ND)] for i in range(NT)]
        trn = [[P0 + i * ND + c + 1 for c in range(ND)] for i in range(NT)]
        @blk.sync
        def _(s):
            s.wait_ge(s_dv.h, vn[NT-1])
            s.wait_ge(s_ac.h, an[NT-1])
            for i in range(NT):
                if i >= 2:
                    s.wait_ge(s_dv.h, vxn[i-2])
                lds[i % 2].inc(s.dma_start(out=self.xst[:, i % 2, :], in_=src_t[i]), 16)
        @blk.tensor
        def _(t):
            self.wait(t, self.ld_const)
            for i in range(NT):
                t.wait_ge(s_dv.h, vxn[i])
                for c in range(ND):
                    if i * ND + c >= 2:
                        pi, pc = divmod(i * ND + c - 2, ND)
                        t.wait_ge(s_dv.h, vev[pi][pc])
                    pb = self.psT1 if c % 2 == 0 else self.psT2
                    self.s_pe.inc(t.transpose(pb[:], self.xnst[:, i % 2, 128*c:128*(c+1)], self.idt16[:]))
        @blk.vector
        def _(v):
            v = EW(v, self.s_dv)
            v.wait_ge(s_dv.h, rdy)
            for i in range(NT):
                v.wait_ge(lds[i % 2].h, ldn2[i])
                if i >= 2:
                    v.wait_ge(s_pe.h, trn[i-2][-1])
                v.tensor_scalar(self.xnst[:, i % 2, :], self.xst[:, i % 2, :],
                                self.means[:, i:i+1], self.rstds[:, i:i+1],
                                ALU.subtract, ALU.mult)
                for c in range(ND):
                    v.wait_ge(s_pe.h, trn[i][c])
                    pb = self.psT1 if c % 2 == 0 else self.psT2
                    v.tensor_copy(self.xn5T[:, c, 128*i:128*(i+1)], pb[:])
        assert s_dv.n == vev[-1][-1]
        self.bank_free["T1"] = ("dv", vev[-1][-2])
        self.bank_free["T2"] = ("dv", vev[-1][-1])
        self.xst_free = ("dv", vxn[-1])
        self.ln5_done = ("dv", s_dv.n)

    def ln_combine(self, v_edge, a_edge):
        blk = self.blk
        s_dv, s_ac = self.s_dv, self.s_ac
        @blk.vector
        def _(v):
            v = EW(v, self.s_dv)
            if v_edge: v.wait_ge(s_dv.h, v_edge)
            if a_edge: v.wait_ge(s_ac.h, a_edge)
            v.tensor_scalar(self.means[:], self.sums[:], 1.0 / D, None, ALU.mult)
            v.tensor_scalar(self.vart[:], self.sumsq[:], 1.0 / D, None, ALU.mult)
            v.tensor_tensor(self.rstds[:], self.means[:], self.means[:], ALU.mult)
            v.tensor_tensor(self.vart[:], self.vart[:], self.rstds[:], ALU.subtract)
        nvar = s_dv.n
        @blk.scalar
        def _(a):
            a = EW(a, self.s_ac)
            a.wait_ge(s_dv.h, nvar)
            a.activation(self.vart[:], self.vart[:], AF.Sqrt, bias=self.eps_t[:], scale=1.0)
        nstd = s_ac.n
        @blk.vector
        def _(v):
            v = EW(v, self.s_dv)
            v.wait_ge(s_ac.h, nstd)
            v.reciprocal(self.rstds[:], self.vart[:])
            v.tensor_tensor(self.lnbias[:], self.means[:], self.rstds[:], ALU.mult)
            v.tensor_scalar(self.lnbias[:], self.lnbias[:], -1.0, None, ALU.mult)
        return s_dv.n

    def ph_qkv(self):
        blk = self.blk
        s_pe, s_ac = self.s_pe, self.s_ac
        ws = [self.s_w0, self.s_w1]
        W0 = [ws[0].n, ws[1].n]
        wld = [W0[nc_ % 2] + 16 * ND * (nc_ // 2 + 1) for nc_ in range(6)]
        P0, A0 = s_pe.n, s_ac.n
        mm_n = {}
        ev_n = {}
        g = 0
        for nc_ in range(6):
            for i in range(NT):
                mm_n[(nc_, i)] = P0 + g + 1
                ev_n[(nc_, i)] = A0 + g + 1
                g += 1
        @blk.gpsimd
        def _(g):
            for nc_ in range(6):
                if nc_ >= 2:
                    g.wait_ge(s_pe.h, mm_n[(nc_ - 2, NT - 1)])
                for kc in range(ND):
                    ws[nc_ % 2].inc(g.dma_start(out=self.wqkv[:, nc_ % 2, kc, :],
                                                in_=self.wqkv_h[128*kc:128*(kc+1), 512*nc_:512*(nc_+1)]), 16)
        @blk.tensor
        def _(t):
            self.wait(t, self.ln5_done)
            for nc_ in range(6):
                t.wait_ge(ws[nc_ % 2].h, wld[nc_])
                for i in range(NT):
                    pb = self.psA if (nc_ * NT + i) % 2 == 0 else self.psB
                    gi = nc_ * NT + i
                    if gi >= 2:
                        pnc, pi = divmod(gi - 2, NT)
                        t.wait_ge(s_ac.h, ev_n[(pnc, pi)])
                    else:
                        self.bank_wait(t, "A" if gi % 2 == 0 else "B")
                    ins = None
                    for kc in range(ND):
                        ins = t.matmul(pb[:], self.xn5T[:, kc, 128*i:128*(i+1)],
                                       self.wqkv[:, nc_ % 2, kc, :],
                                       start=(kc == 0), stop=(kc == ND - 1))
                    s_pe.inc(ins)
        @blk.scalar
        def _(a):
            a = EW(a, self.s_ac)
            for nc_ in range(6):
                for i in range(NT):
                    gi = nc_ * NT + i
                    pb = self.psA if gi % 2 == 0 else self.psB
                    a.wait_ge(s_pe.h, mm_n[(nc_, i)])
                    dst = [self.q_tok, self.k_tok, self.v_tok][nc_ // 2]
                    half = nc_ % 2
                    a.activation(dst[:, i, 512*half:512*(half+1)], pb[:], AF.Copy,
                                 bias=0.0, scale=(0.125 if nc_ < 2 else 1.0))
        self.bank_free["A"] = ("ac", ev_n[(5, NT-2)])
        self.bank_free["B"] = ("ac", ev_n[(5, NT-1)])
        self.qkv_done = ("ac", s_ac.n)

    def ph_scores(self):
        blk = self.blk
        s_dv, s_ac = self.s_dv, self.s_ac
        V0, A0 = s_dv.n, s_ac.n
        @blk.vector
        def _(v):
            v = EW(v, self.s_dv)
            self.wait(v, self.qkv_done)
            for s_ in range(8):
                for t_ in range(8):
                    for half in range(2):
                        v.tensor_tensor(self.prod[:, :], self.q_tok[:, 2*s_+half, :],
                                        self.k_tok[:, 2*t_+half, :], ALU.mult)
                        v.tensor_reduce(
                            self.Sc[:, half, :].rearrange("p (h s t) -> p h s t", s=8, t=8)[:, :, s_, t_].unsqueeze(2),
                            self.prod.rearrange("p (h d) -> p h d", d=HD), AX.X, ALU.add)
        nsc = s_dv.n
        @blk.scalar
        def _(a):
            a = EW(a, self.s_ac)
            a.wait_ge(s_dv.h, nsc)
            for half in range(2):
                a.activation(self.Sc[:, half, :], self.Sc[:, half, :], AF.Exp,
                             bias=self.c30[:], scale=1.0)
        nexp = s_ac.n
        @blk.vector
        def _(v):
            v = EW(v, self.s_dv)
            v.wait_ge(s_ac.h, nexp)
            for half in range(2):
                v.tensor_reduce(self.sumE[:, half, :].unsqueeze(2),
                                self.Sc[:, half, :].rearrange("p (hs t) -> p hs t", t=8),
                                AX.X, ALU.add)
                v.reciprocal(self.recE[:, half, :], self.sumE[:, half, :])
                v.tensor_tensor(self.P[:, half, :].rearrange("p (a b) -> p a b", b=8),
                                self.Sc[:, half, :].rearrange("p (a b) -> p a b", b=8),
                                self.recE[:, half, :].unsqueeze(2).broadcast_to((128, 128, 8)),
                                ALU.mult)
        self.scores_done = ("dv", s_dv.n)

    def ph_attnv(self):
        blk = self.blk
        s_dv = self.s_dv
        @blk.vector
        def _(v):
            v = EW(v, self.s_dv)
            self.wait(v, self.scores_done)
            for s_ in range(8):
                for half in range(2):
                    i = 2 * s_ + half
                    Pv = self.P[:, half, :].rearrange("p (h s t) -> p h s t", s=8, t=8)
                    for t_ in range(8):
                        psl = Pv[:, :, s_, t_].unsqueeze(2).broadcast_to((128, NH, HD))
                        r3 = lambda ap: ap.rearrange("p (a b) -> p a b", b=HD)
                        if t_ == 0:
                            v.tensor_tensor(r3(self.acc[:, i, :]), r3(self.v_tok[:, 2*t_+half, :]), psl, ALU.mult)
                        else:
                            v.tensor_tensor(r3(self.prod[:, :]), r3(self.v_tok[:, 2*t_+half, :]), psl, ALU.mult)
                            v.tensor_tensor(self.acc[:, i, :], self.acc[:, i, :], self.prod[:, :], ALU.add)
        self.attnv_done = ("dv", s_dv.n)

    def ph_outproj_ln6(self):
        blk = self.blk
        s_pe, s_dv, s_ac = self.s_pe, self.s_dv, self.s_ac
        P0, V0 = s_pe.n, s_dv.n
        trn = [[P0 + i * ND + c + 1 for c in range(ND)] for i in range(NT)]
        vev = [[V0 + i * ND + c + 1 for c in range(ND)] for i in range(NT)]
        @blk.tensor
        def _(t):
            self.wait(t, self.attnv_done)
            self.bank_wait(t, "T1")
            self.bank_wait(t, "T2")
            for i in range(NT):
                for c in range(ND):
                    if i * ND + c >= 2:
                        pi, pc = divmod(i * ND + c - 2, ND)
                        t.wait_ge(s_dv.h, vev[pi][pc])
                    pb = self.psT1 if c % 2 == 0 else self.psT2
                    s_pe.inc(t.transpose(pb[:], self.acc[:, i, 128*c:128*(c+1)], self.idt16[:]))
        @blk.vector
        def _(v):
            v = EW(v, self.s_dv)
            for i in range(NT):
                for c in range(ND):
                    v.wait_ge(s_pe.h, trn[i][c])
                    pb = self.psT1 if c % 2 == 0 else self.psT2
                    v.tensor_copy(self.attnT[:, c, 128*i:128*(i+1)], pb[:])
        assert s_dv.n == vev[-1][-1]
        self.bank_free["T1"] = ("dv", vev[-1][-2])
        self.bank_free["T2"] = ("dv", vev[-1][-1])
        attnT_done = ("dv", s_dv.n)
        ws = [self.s_w0, self.s_w1]
        WB = [ws[0].n, ws[1].n]
        wldo = [WB[m % 2] + 16 * ND * (m // 2 + 1) for m in range(ND)]
        P1, A1, V1 = s_pe.n, s_ac.n, s_dv.n
        @blk.gpsimd
        def _(g):
            self.wait(g, self.qkv_done)
            for m in range(ND):
                if m >= 2:
                    g.wait_ge(s_pe.h, P1 + (m - 1) * TC)
                for kc in range(ND):
                    ws[m % 2].inc(g.dma_start(out=self.wout[:, m % 2, kc, :],
                                              in_=self.wout_h[128*kc:128*(kc+1), 128*m:128*(m+1)]), 16)
        att2fm = self.att2fm
        mm_n = {}
        ev_n = {}
        gidx = 0
        for m in range(ND):
            for tc in range(TC):
                mm_n[(m, tc)] = P1 + gidx + 1
                ev_n[(m, tc)] = A1 + gidx + 1
                gidx += 1
        @blk.tensor
        def _(t):
            self.wait(t, attnT_done)
            for m in range(ND):
                t.wait_ge(ws[m % 2].h, wldo[m])
                for tc in range(TC):
                    gi = m * TC + tc
                    pb = self.psA if gi % 2 == 0 else self.psB
                    if gi >= 2:
                        pm, ptc = divmod(gi - 2, TC)
                        t.wait_ge(s_ac.h, ev_n[(pm, ptc)])
                    else:
                        self.bank_wait(t, "A" if gi % 2 == 0 else "B")
                    ins = None
                    for kc in range(ND):
                        ins = t.matmul(pb[:], self.wout[:, m % 2, kc, :],
                                       self.attnT[:, kc, 512*tc:512*(tc+1)],
                                       start=(kc == 0), stop=(kc == ND - 1))
                    s_pe.inc(ins)
        @blk.scalar
        def _(a):
            a = EW(a, self.s_ac)
            for m in range(ND):
                for tc in range(TC):
                    gi = m * TC + tc
                    pb = self.psA if gi % 2 == 0 else self.psB
                    a.wait_ge(s_pe.h, mm_n[(m, tc)])
                    a.activation(att2fm[:, m, 512*tc:512*(tc+1)], pb[:], AF.Copy,
                                 bias=0.0, scale=1.0)
        self.bank_free["A"] = ("ac", ev_n[(ND-1, TC-2)])
        self.bank_free["B"] = ("ac", ev_n[(ND-1, TC-1)])
        self.wout_consumed = mm_n[(ND-1, TC-1)]
        att2_done = ("ac", s_ac.n)
        att2tok = self.att2_tok
        P2, V2, A2 = s_pe.n, s_dv.n, s_ac.n
        trn2 = [[P2 + i * ND + c + 1 for c in range(ND)] for i in range(NT)]
        per = ND + 1
        vev2 = [[V2 + i * per + c + 1 for c in range(ND)] for i in range(NT)]
        vred2 = [V2 + i * per + ND + 1 for i in range(NT)]
        @blk.tensor
        def _(t):
            self.wait(t, att2_done)
            self.bank_wait(t, "T1")
            self.bank_wait(t, "T2")
            for i in range(NT):
                for c in range(ND):
                    if i * ND + c >= 2:
                        pi, pc = divmod(i * ND + c - 2, ND)
                        t.wait_ge(s_dv.h, vev2[pi][pc])
                    pb = self.psT1 if c % 2 == 0 else self.psT2
                    s_pe.inc(t.transpose(pb[:], att2fm[:, c, 128*i:128*(i+1)], self.idt16[:]))
        @blk.vector
        def _(v):
            v = EW(v, self.s_dv)
            for i in range(NT):
                for c in range(ND):
                    v.wait_ge(s_pe.h, trn2[i][c])
                    pb = self.psT1 if c % 2 == 0 else self.psT2
                    v.tensor_copy(att2tok[:, i, 128*c:128*(c+1)], pb[:])
                v.tensor_reduce(self.sums[:, i:i+1], att2tok[:, i, :], AX.X, ALU.add)
        assert s_dv.n == vred2[-1]
        zz0 = s_dv.n + 1
        @blk.vector
        def _(v):
            v = EW(v, self.s_dv)
            v.memset(self.sumsq[:], 0.0)
        @blk.scalar
        def _(a):
            a = EW(a, self.s_ac)
            a.wait_ge(s_dv.h, zz0)
            for i in range(NT):
                a.wait_ge(s_dv.h, vred2[i])
                a.activation(self.sqscr[:], att2tok[:, i, :], AF.Square,
                             bias=self.zero1[:], scale=1.0, accum_out=self.sumsq[:, i:i+1])
        self.bank_free["T1"] = ("dv", vev2[-1][-2])
        self.bank_free["T2"] = ("dv", vev2[-1][-1])
        rdy = self.ln_combine(s_dv.n, s_ac.n)
        lds = [self.s_ld0, self.s_ld1]
        B0, B1 = lds[0].n, lds[1].n
        V3 = s_dv.n
        ldn = [[B0, B1][i % 2] + 16 * (i // 2 + 1) for i in range(NT)]
        src_t = self.x_h.rearrange("(n p) d -> n p d", p=128)
        x3t = lambda i: (self.x3a[:, i, :] if i < 8 else self.x3b[:, i - 8, :])
        @blk.sync
        def _(s):
            self.wait(s, self.xst_free)
            for i in range(NT):
                if i >= 2:
                    s.wait_ge(s_dv.h, V3 + 2 * (i - 1))
                lds[i % 2].inc(s.dma_start(out=self.xst[:, i % 2, :], in_=src_t[i]), 16)
        @blk.vector
        def _(v):
            v = EW(v, self.s_dv)
            v.wait_ge(s_dv.h, rdy)
            for i in range(NT):
                v.wait_ge(lds[i % 2].h, ldn[i])
                v.tensor_scalar(x3t(i), att2tok[:, i, :],
                                self.rstds[:, i:i+1], self.lnbias[:, i:i+1],
                                ALU.mult, ALU.add)
                v.tensor_tensor(x3t(i), x3t(i), self.xst[:, i % 2, :], ALU.add)
        self.x3_done = ("dv", s_dv.n)
        self.xst_free = self.x3_done

    def ph_ffn2(self):
        blk = self.blk
        s_pe, s_dv, s_ac = self.s_pe, self.s_dv, self.s_ac
        x3t = lambda i: (self.x3a[:, i, :] if i < 8 else self.x3b[:, i - 8, :])
        V0, A0 = s_dv.n, s_ac.n
        @blk.vector
        def _(v):
            v = EW(v, self.s_dv)
            self.wait(v, self.x3_done)
            v.memset(self.sumsq[:], 0.0)
            for i in range(NT):
                v.tensor_reduce(self.sums[:, i:i+1], x3t(i), AX.X, ALU.add)
        @blk.scalar
        def _(a):
            a = EW(a, self.s_ac)
            a.wait_ge(s_dv.h, V0 + 1)
            for i in range(NT):
                a.wait_ge(s_dv.h, V0 + 1 + (i + 1))
                a.activation(self.sqscr[:], x3t(i), AF.Square,
                             bias=self.zero1[:], scale=1.0, accum_out=self.sumsq[:, i:i+1])
        rdy = self.ln_combine(V0 + 1 + NT, A0 + NT)
        V1, P0 = s_dv.n, s_pe.n
        per = 1 + ND
        vxn = [V1 + i * per + 1 for i in range(NT)]
        vev = [[V1 + i * per + 1 + (c + 1) for c in range(ND)] for i in range(NT)]
        trn = [[P0 + i * ND + c + 1 for c in range(ND)] for i in range(NT)]
        @blk.tensor
        def _(t):
            self.bank_wait(t, "T1")
            self.bank_wait(t, "T2")
            for i in range(NT):
                t.wait_ge(s_dv.h, vxn[i])
                for c in range(ND):
                    if i * ND + c >= 2:
                        pi, pc = divmod(i * ND + c - 2, ND)
                        t.wait_ge(s_dv.h, vev[pi][pc])
                    pb = self.psT1 if c % 2 == 0 else self.psT2
                    s_pe.inc(t.transpose(pb[:], self.xnst[:, i % 2, 128*c:128*(c+1)], self.idt16[:]))
        @blk.vector
        def _(v):
            v = EW(v, self.s_dv)
            v.wait_ge(s_dv.h, rdy)
            for i in range(NT):
                if i >= 2:
                    v.wait_ge(s_pe.h, trn[i-2][-1])
                v.tensor_scalar(self.xnst[:, i % 2, :], x3t(i),
                                self.means[:, i:i+1], self.rstds[:, i:i+1],
                                ALU.subtract, ALU.mult)
                for c in range(ND):
                    v.wait_ge(s_pe.h, trn[i][c])
                    pb = self.psT1 if c % 2 == 0 else self.psT2
                    v.tensor_copy(self.xn7T[:, c, 128*i:128*(i+1)], pb[:])
        assert s_dv.n == vev[-1][-1]
        self.bank_free["T1"] = ("dv", vev[-1][-2])
        self.bank_free["T2"] = ("dv", vev[-1][-1])
        ln7_done = ("dv", s_dv.n)

        @blk.gpsimd
        def _(g):
            self.wait(g, self.ln5_done)
            for kc in range(ND):
                self.s_w2.inc(g.dma_start(out=self.selw[:, kc, :],
                                          in_=self.selw_h[128*kc:128*(kc+1), :]), 16)
        selw_ld = ("w2", self.s_w2.n)
        sel_ev = []
        for tc in range(TC):
            P1 = s_pe.n
            @blk.tensor
            def _(t):
                self.wait(t, ln7_done)
                if tc == 0:
                    self.wait(t, selw_ld)
                    self.bank_wait(t, "S")
                else:
                    t.wait_ge(s_ac.h, sel_ev[-1])
                ins = None
                for kc in range(ND):
                    ins = t.matmul(self.psS[:, :], self.selw[:, kc, :],
                                   self.xn7T[:, kc, 512*tc:512*(tc+1)],
                                   start=(kc == 0), stop=(kc == ND - 1))
                s_pe.inc(ins)
            nmm = s_pe.n
            @blk.scalar
            def _(a):
                a = EW(a, self.s_ac)
                a.wait_ge(s_pe.h, nmm)
                a.activation(self.selfm[0:16, 512*tc:512*(tc+1)], self.psS[:, :], AF.Sigmoid,
                             bias=self.zero1[0:16], scale=1.0)
            sel_ev.append(s_ac.n)
        self.bank_free["S"] = ("ac", s_ac.n)
        sel_done = ("ac", s_ac.n)

        ws = [self.s_w0, self.s_w1]
        ffn2_t = self.ffn2_d.rearrange("(n p) d -> n p d", p=128)
        stat_edges = []
        h1_free = None
        pg_free = None
        @blk.vector
        def _(v):
            v = EW(v, self.s_dv)
            v.memset(self.sumsq[:], 0.0)
        zst = s_dv.n
        for th in range(2):
            kld = []
            kmm = []
            kev = []
            for m in range(NH):
                @blk.gpsimd
                def _(g):
                    if th == 0 and m == 0:
                        self.wait(g, ("pe", self.wout_consumed))
                    if len(kmm) >= 2:
                        g.wait_ge(s_pe.h, kmm[-2])
                    if th > 0 and m < 2:
                        g.wait_ge(s_pe.h, self.vmm_prev[m - 2])
                    for kc in range(ND):
                        ws[m % 2].inc(g.dma_start(out=self.keysw[:, m % 2, kc, :],
                                                  in_=self.keys_h[128*kc:128*(kc+1), 128*m:128*(m+1)]), 16)
                kld.append(ws[m % 2].n)
                @blk.tensor
                def _(t):
                    if th == 0 and m == 0:
                        self.wait(t, h1_free)
                    t.wait_ge(ws[m % 2].h, kld[m])
                    for tcl in range(2):
                        gi = m * 2 + tcl
                        pb = self.psA if gi % 2 == 0 else self.psB
                        if len(kev) >= 2:
                            t.wait_ge(s_ac.h, kev[gi - 2] if gi - 2 < len(kev) else kev[-1])
                        elif th == 0:
                            self.bank_wait(t, "A" if gi % 2 == 0 else "B")
                        else:
                            t.wait_ge(s_ac.h, self.vev_prev[gi - 2])
                        tc = 2 * th + tcl
                        ins = None
                        for kc in range(ND):
                            ins = t.matmul(pb[:], self.keysw[:, m % 2, kc, :],
                                           self.xn7T[:, kc, 512*tc:512*(tc+1)],
                                           start=(kc == 0), stop=(kc == ND - 1))
                        s_pe.inc(ins)
                kmm.append(s_pe.n)
                @blk.scalar
                def _(a):
                    a = EW(a, self.s_ac)
                    for tcl in range(2):
                        a.wait_ge(s_pe.h, kmm[m] - 1 + tcl)
                        pb = self.psA if (m * 2 + tcl) % 2 == 0 else self.psB
                        a.activation(self.h1[:, m, 512*tcl:512*(tcl+1)], pb[:], AF.Relu,
                                     bias=self.zero1[:], scale=1.0)
                    kev.append(s_ac.n - 1)
                    kev.append(s_ac.n)
            self.kmm_prev = kmm[-2:]
            self.kev_prev = kev[-2:]
            self.bank_free["A"] = ("ac", kev[-2])
            self.bank_free["B"] = ("ac", kev[-1])
            gmm = []
            gdv = []
            for e in range(NH):
                @blk.tensor
                def _(t):
                    if e == 0:
                        t.wait_ge(s_ac.h, kev[-1])
                        self.wait(t, sel_done)
                    for tcl in range(2):
                        gi = e * 2 + tcl
                        pb = self.psBC1 if gi % 2 == 0 else self.psBC2
                        if len(gdv) >= 2 and gi - 2 < len(gdv):
                            t.wait_ge(s_dv.h, gdv[gi - 2])
                        elif th > 0 and gi < 2:
                            t.wait_ge(s_dv.h, self.gdv_prev[gi - 2])
                        else:
                            self.bank_wait(t, "BC1" if gi % 2 == 0 else "BC2")
                        tc = 2 * th + tcl
                        s_pe.inc(t.matmul(pb[:], self.eselk[0:16, 128*e:128*(e+1)],
                                          self.selfm[0:16, 512*tc:512*(tc+1)],
                                          start=True, stop=True))
                gmm.append(s_pe.n)
                @blk.vector
                def _(v):
                    v = EW(v, self.s_dv)
                    for tcl in range(2):
                        v.wait_ge(s_pe.h, gmm[e] - 1 + tcl)
                        pb = self.psBC1 if (e * 2 + tcl) % 2 == 0 else self.psBC2
                        sl = slice(512 * tcl, 512 * (tcl + 1))
                        v.tensor_tensor(self.h1[:, e, sl], self.h1[:, e, sl], pb[:], ALU.mult)
                    gdv.append(s_dv.n - 1)
                    gdv.append(s_dv.n)
            self.gdv_prev = gdv[-2:]
            self.bank_free["BC1"] = ("dv", gdv[-2])
            self.bank_free["BC2"] = ("dv", gdv[-1])
            gate_done = ("dv", s_dv.n)
            for tcl in range(2):
                tc = 2 * th + tcl
                vmm = []
                vev4 = []
                vld = []
                for m in range(ND):
                    @blk.gpsimd
                    def _(g):
                        if tcl == 0 and m < 2:
                            g.wait_ge(s_pe.h, kmm[-1])
                        elif tcl == 1 and m < 2:
                            g.wait_ge(s_pe.h, self.vmm_prev[m - 2])
                        if len(vmm) >= 2:
                            g.wait_ge(s_pe.h, vmm[-2])
                        for kc in range(NH):
                            ws[m % 2].inc(g.dma_start(out=self.valsw[:, m % 2, kc, :],
                                                      in_=self.vals_h[128*kc:128*(kc+1), 128*m:128*(m+1)]), 16)
                    vld.append(ws[m % 2].n)
                    @blk.tensor
                    def _(t):
                        if m == 0:
                            self.wait(t, gate_done if tcl == 0 else None)
                            self.wait(t, pg_free)
                        t.wait_ge(ws[m % 2].h, vld[m])
                        pb = self.psA if m % 2 == 0 else self.psB
                        if len(vev4) >= 2:
                            t.wait_ge(s_ac.h, vev4[m - 2])
                        elif (th > 0 or tcl > 0) and m < 2:
                            t.wait_ge(s_ac.h, self.vev_prev[m - 2])
                        ins = None
                        for kc in range(NH):
                            ins = t.matmul(pb[:], self.valsw[:, m % 2, kc, :],
                                           self.h1[:, kc, 512*tcl:512*(tcl+1)],
                                           start=(kc == 0), stop=(kc == NH - 1))
                        s_pe.inc(ins)
                    vmm.append(s_pe.n)
                    @blk.scalar
                    def _(a):
                        a = EW(a, self.s_ac)
                        a.wait_ge(s_pe.h, vmm[m])
                        a.activation(self.page[:, m, :], (self.psA if m % 2 == 0 else self.psB)[:],
                                     AF.Copy, bias=0.0, scale=1.0)
                    vev4.append(s_ac.n)
                self.vmm_prev = vmm[-2:]
                self.vev_prev = vev4[-2:]
                self.bank_free["A"] = ("ac", vev4[-2])
                self.bank_free["B"] = ("ac", vev4[-1])
                pg_end = []
                for ti in range(4):
                    i = 4 * tc + ti
                    trn2 = []
                    Vb = s_dv.n
                    @blk.tensor
                    def _(t):
                        if ti == 0:
                            t.wait_ge(s_ac.h, vev4[-1])
                        self.bank_wait(t, "T1")
                        self.bank_wait(t, "T2")
                        for c in range(ND):
                            if c >= 2:
                                t.wait_ge(s_dv.h, Vb + c - 1)
                            pb = self.psT1 if c % 2 == 0 else self.psT2
                            trn2.append(s_pe.inc(t.transpose(
                                pb[:], self.page[:, c, 128*ti:128*(ti+1)], self.idt16[:])))
                    @blk.vector
                    def _(v):
                        v = EW(v, self.s_dv)
                        if ti == 0:
                            v.wait_ge(s_dv.h, zst)
                        for c in range(ND):
                            v.wait_ge(s_pe.h, trn2[c])
                            pb = self.psT1 if c % 2 == 0 else self.psT2
                            v.tensor_copy(self.tokst[:, ti % 2, 128*c:128*(c+1)], pb[:])
                        v.tensor_reduce(self.sums[:, i:i+1], self.tokst[:, ti % 2, :], AX.X, ALU.add)
                    nred = s_dv.n
                    self.bank_free["T1"] = ("dv", nred - 1 - (1 if ND % 2 == 0 else 0))
                    self.bank_free["T2"] = ("dv", nred - 1)
                    @blk.scalar
                    def _(a):
                        a = EW(a, self.s_ac)
                        a.wait_ge(s_dv.h, nred)
                        a.activation(self.sqscr[:], self.tokst[:, ti % 2, :], AF.Square,
                                     bias=self.zero1[:], scale=1.0, accum_out=self.sumsq[:, i:i+1])
                    nsq = s_ac.n
                    sts = [self.s_st0, self.s_st1]
                    @blk.sync
                    def _(s):
                        s.wait_ge(s_dv.h, nred)
                        sts[i % 2].inc(s.dma_start(out=ffn2_t[i], in_=self.tokst[:, ti % 2, :]), 16)
                    pg_end.append((i % 2, sts[i % 2].n))
                    stat_edges.append((nred, nsq, (i % 2, sts[i % 2].n)))
                    if i >= 1:
                        pnred, pnsq, (pp, pst) = stat_edges[i - 1]
                        @blk.vector
                        def _(v):
                            v = EW(v, self.s_dv)
                            v.wait_ge(sts[pp].h, pst)
                            v.wait_ge(s_ac.h, pnsq)
                pg_free = ("pe", s_pe.n)
            h1_free = ("pe", s_pe.n)
        rdy2 = self.ln_combine(stat_edges[-1][0], stat_edges[-1][1])
        lds = [self.s_ld0, self.s_ld1]
        B0, B1 = lds[0].n, lds[1].n
        ldn = [[B0, B1][i % 2] + 16 * (i // 2 + 1) for i in range(NT)]
        V6 = s_dv.n
        appn = [V6 + 2 * (i + 1) for i in range(NT)]
        xfst = self.xnst
        @blk.sync
        def _(s):
            s.wait_ge(self.s_st0.h, self.s_st0.n)
            s.wait_ge(self.s_st1.h, self.s_st1.n)
            for i in range(NT):
                if i >= 2:
                    s.wait_ge(s_dv.h, appn[i-2])
                lds[i % 2].inc(s.dma_start(out=xfst[:, i % 2, :], in_=ffn2_t[i]), 16)
        @blk.vector
        def _(v):
            v = EW(v, self.s_dv)
            v.wait_ge(s_dv.h, rdy2)
            for i in range(NT):
                v.wait_ge(lds[i % 2].h, ldn[i])
                v.tensor_scalar(self.xst[:, 0, :], xfst[:, i % 2, :],
                                self.rstds[:, i:i+1], self.lnbias[:, i:i+1],
                                ALU.mult, ALU.add)
                v.tensor_tensor(x3t(i), x3t(i), self.xst[:, 0, :], ALU.add)
        @blk.sync
        def _(s):
            s.wait_ge(s_dv.h, appn[7])
            self.s_st.inc(s.dma_start(out=self.out_h.rearrange("(i p) d -> p i d", p=128)[:, 0:8, :],
                                      in_=self.x3a[:, :, :]), 16)
            s.wait_ge(s_dv.h, appn[15])
            self.s_st.inc(s.dma_start(out=self.out_h.rearrange("(i p) d -> p i d", p=128)[:, 8:16, :],
                                      in_=self.x3b[:, :, :]), 16)
            s.wait_ge(self.s_st.h, self.s_st.n)

    def finalize_dbg(self):
        blk, s_st = self.blk, self.s_st
        waits = [(sm.h, sm.n) for sm in
                 [self.s_pe, self.s_dv, self.s_ac, self.s_ld0, self.s_ld1,
                  self.s_ld, self.s_w0, self.s_w1, self.s_w2] if sm.n > 0]
        srcs = {"q2": self.q_tok, "v2": self.v_tok, "acc2": self.acc}
        @blk.sync
        def _(s):
            for h, n in waits:
                s.wait_ge(h, n)
            for name, hap in self.dbg_h.items():
                if name in srcs:
                    s_st.inc(s.dma_start(out=hap, in_=srcs[name].rearrange("p i d -> p (i d)")), 16)
                elif name == "P2":
                    s_st.inc(s.dma_start(out=hap, in_=self.P.rearrange("p a x -> p (a x)")), 16)
                elif name == "x3":
                    s_st.inc(s.dma_start(out=hap[:, 0:8192], in_=self.x3a.rearrange("p i d -> p (i d)")), 16)
                    s_st.inc(s.dma_start(out=hap[:, 8192:], in_=self.x3b.rearrange("p i d -> p (i d)")), 16)
            s.wait_ge(s_st.h, s_st.n)


def l1_inmap(xB, ins, consts):
    W = ins["L1_attn_in_proj_weight"]
    Wr = W.reshape(16, 3, 64, 1024)
    Wq = Wr[:, 0].reshape(1024, 1024)
    Wk = Wr[:, 1].reshape(1024, 1024)
    Wv = Wr[:, 2].reshape(1024, 1024)
    WqkvT = np.ascontiguousarray(np.concatenate([Wq, Wk, Wv], 0).T)
    return {
        "xB": xB,
        "WqkvT": WqkvT,
        "WoutT": np.ascontiguousarray(ins["L1_attn_out_weight"].T),
        "selWT": np.ascontiguousarray(ins["L1_ffn_expert_sel"].T),
        "keysT": np.ascontiguousarray(ins["L1_ffn_keys"].transpose(1, 0, 2).reshape(1024, 2048)),
        "valsS": np.ascontiguousarray(ins["L1_ffn_values"].reshape(2048, 1024)),
        "ident16": consts["ident16"],
        "eselk16": consts["eselk16"],
    }


def kernel(**inputs):
    ins = {k: np.asarray(v) for k, v in inputs.items()}
    ins.pop("L1_attn_key", None)
    ins.pop("L1_attn_value", None)
    x = np.ascontiguousarray(ins["L0_attn_x"], dtype=np.float32)
    consts = host_constants()

    b1 = L1B(dbg=(), upto="all")
    maps1 = [l0_inmap(np.ascontiguousarray(x[i]), ins, consts) for i in range(8)]
    res1 = run_bass_kernel_spmd(b1.nc, maps1, core_ids=list(range(8)))
    x2 = np.stack([np.asarray(res1.results[i]["x2"]) for i in range(8)], 0)

    b2 = L2B(dbg=(), upto="all")
    maps2 = [l1_inmap(np.ascontiguousarray(x2[:, 256 * i:256 * (i + 1), :].reshape(T, D)),
                      ins, consts) for i in range(8)]
    res2 = run_bass_kernel_spmd(b2.nc, maps2, core_ids=list(range(8)))

    out = np.empty((8, T, D), dtype=np.float32)
    for i in range(8):
        out[:, 256 * i:256 * (i + 1), :] = np.asarray(res2.results[i]["y"]).reshape(8, 256, D)
    return out
